# version 21
# baseline (speedup 1.0000x reference)
import numpy as np
import ml_dtypes
from contextlib import ExitStack

import concourse.bass as bass
import concourse.mybir as mybir
from concourse.bass_utils import run_bass_kernel_spmd

F32 = mybir.dt.float32
BF16 = mybir.dt.bfloat16
AF = mybir.ActivationFunctionType
ALU = mybir.AluOpType
AX = mybir.AxisListType
BF = ml_dtypes.bfloat16

D = 1024
SEQ = 2048
NCH = 8
NH = 16
DK = 64
DH = 4096
HCH = 32
NQS = 4
NKC = 16
LN_EPS = 1e-5


def _img_chunks(a, p=128):
    c = a.shape[0] // p
    return np.ascontiguousarray(
        a.reshape(c, p, a.shape[1]).transpose(1, 0, 2).reshape(p, c * a.shape[1]))


def _wimg(wT, oc_size):
    K, O = wT.shape
    kc, oc = K // 128, O // oc_size
    b = wT.reshape(kc, 128, oc, oc_size).transpose(2, 1, 0, 3)
    return np.ascontiguousarray(b.reshape(oc, 128, kc * oc_size))


def _prep(inputs):
    f = np.float32
    g = {}
    x = np.asarray(inputs["L0_attn_x"], dtype=f)
    g["x_img"] = [np.ascontiguousarray(
        x[b].T.reshape(NCH, 128, SEQ).transpose(1, 0, 2).reshape(128, NCH * SEQ))
        for b in range(8)]

    g["wq_img"] = _wimg(np.ascontiguousarray(np.asarray(inputs["L0_attn_W_Q"], f).T), 128)
    g["wk_img"] = _wimg(np.ascontiguousarray(np.asarray(inputs["L0_attn_W_K"], f).T), 128)
    g["wv_img"] = _img_chunks(np.ascontiguousarray(np.asarray(inputs["L0_attn_W_V"], f).T))
    g["wo_img"] = _wimg(np.ascontiguousarray(np.asarray(inputs["L0_attn_W_O"], f).T), 128)

    for nm, tag in [("L0_ffn_w1_w", "w1"), ("L0_ffn_w2_w", "w2"),
                    ("L0_ffn_h1_w", "h1"), ("L0_ffn_h2_w", "h2"),
                    ("L0_ffn_w3_w", "w3")]:
        wT = np.ascontiguousarray(np.asarray(inputs[nm], f).T).astype(BF)
        g[tag + "_img"] = _wimg(wT, 128)

    def scl(name, c):
        return np.ascontiguousarray(np.asarray(inputs[name], f).reshape(c, 128).T)
    g["s1"] = scl("L0_ffn_w1_s", 32)
    g["s2"] = scl("L0_ffn_w2_s", 32)
    g["sh1"] = scl("L0_ffn_h1_s", 32)
    g["sh2"] = scl("L0_ffn_h2_s", 32)
    g["s3"] = scl("L0_ffn_w3_s", 8)
    g["l1gamma"] = scl("L1_attn_gamma", 8)
    g["qkva_img"] = _wimg(
        np.ascontiguousarray(np.asarray(inputs["L1_attn_w_qkva"], f).T), 128)
    g["l1g"] = scl("L1_ffn_ln_gamma", 8)
    g["l1b"] = scl("L1_ffn_ln_beta", 8)
    g["mu_img"] = _img_chunks(np.asarray(inputs["L1_ffn_mu"], f).reshape(D, 256))
    g["l1scale"] = np.asarray(inputs["L1_ffn_scale"], f).reshape(1, 1)
    w1 = np.asarray(inputs["L1_ffn_w1"], f)
    g["ew1_img"] = np.ascontiguousarray(
        w1.reshape(16, 8, 128, 256).transpose(0, 2, 1, 3).reshape(16, 128, 2048))
    w2 = np.asarray(inputs["L1_ffn_w2"], f)
    g["ew2_img"] = np.ascontiguousarray(
        w2.reshape(16, 2, 128, 8, 128).transpose(3, 2, 0, 1, 4).reshape(8, 128, 4096))
    g["b1t"] = np.ascontiguousarray(
        np.asarray(inputs["L1_ffn_b1"], f).reshape(16, 2, 128)
        .transpose(2, 0, 1).reshape(128, 32))
    g["b2t"] = np.ascontiguousarray(
        np.asarray(inputs["L1_ffn_b2"], f).reshape(16, 8, 128)
        .transpose(2, 0, 1).reshape(128, 128))

    theta = 1.0 / 10000.0 ** (np.arange(0, DK, 2, dtype=np.float64) / DK)
    ang = np.outer(np.arange(SEQ, dtype=np.float64), theta)
    cosT = np.cos(ang).T.astype(f)
    sinT = np.sin(ang).T.astype(f)
    c64 = np.concatenate([cosT, cosT], 0)
    sf64 = np.concatenate([-sinT, sinT], 0)
    g["cos2"] = np.ascontiguousarray(np.concatenate([c64, c64], 0)).astype(BF)
    g["sflip2"] = np.ascontiguousarray(np.concatenate([sf64, sf64], 0)).astype(BF)
    m = np.zeros((128, 128), dtype=BF)
    for p in range(128):
        m[p, p:] = 1
    g["mask"] = m
    g["identb"] = np.eye(128, dtype=BF)
    onesq = np.zeros((128, 33), dtype=BF)
    onesq[0:64, 0] = 1
    onesq[64:128, 32] = 1
    g["onesq"] = onesq
    oa = np.zeros((128, 128), dtype=BF)
    oa[0:64, :] = 1
    g["ones64a"] = oa
    ob = np.zeros((128, 128), dtype=BF)
    ob[64:128, :] = 1
    g["ones64b"] = ob
    g["onescol"] = np.ones((128, 1), dtype=BF)
    g["onesrow"] = np.ones((1, 128), dtype=f)
    epsc = np.zeros((128, 4), dtype=f)
    epsc[:, 0] = LN_EPS
    epsc[:, 1] = 1e-12
    epsc[:, 2] = LN_EPS * LN_EPS
    g["epsc"] = epsc
    return g


def build(dbg=(), stop_after=None, check=False):
    nc = bass.Bass("TRN2", target_bir_lowering=False)
    dt = nc.dram_tensor

    def din(name, shape, dtype):
        return dt(name, shape, dtype, kind="ExternalInput").ap()

    x_img = din("x_img", [128, NCH * SEQ], F32)
    wq_img = din("wq_img", [8, 128, 1024], F32)
    wk_img = din("wk_img", [8, 128, 1024], F32)
    wv_img = din("wv_img", [128, 8192], F32)
    wo_img = din("wo_img", [8, 128, 1024], F32)
    w1_img = din("w1_img", [32, 128, 1024], BF16)
    w2_img = din("w2_img", [32, 128, 1024], BF16)
    h1_img = din("h1_img", [32, 128, 4096], BF16)
    h2_img = din("h2_img", [32, 128, 4096], BF16)
    w3_img = din("w3_img", [8, 128, 4096], BF16)
    s1_i = din("s1", [128, 32], F32)
    s2_i = din("s2", [128, 32], F32)
    sh1_i = din("sh1", [128, 32], F32)
    sh2_i = din("sh2", [128, 32], F32)
    s3_i = din("s3", [128, 8], F32)
    l1gamma_i = din("l1gamma", [128, 8], F32)
    qkva_img = din("qkva_img", [24, 128, 1024], F32)
    l1g_i = din("l1g", [128, 8], F32)
    l1b_i = din("l1b", [128, 8], F32)
    mu_img = din("mu_img", [128, 2048], F32)
    l1scale_i = din("l1scale", [1, 1], F32)
    ew1_img = din("ew1_img", [16, 128, 2048], F32)
    ew2_img = din("ew2_img", [8, 128, 4096], F32)
    b1t_i = din("b1t", [128, 32], F32)
    b2t_i = din("b2t", [128, 128], F32)
    cos2_i = din("cos2", [128, 2048], BF16)
    sflip2_i = din("sflip2", [128, 2048], BF16)
    mask_i = din("mask", [128, 128], BF16)
    identb_i = din("identb", [128, 128], BF16)
    onesq_i = din("onesq", [128, 33], BF16)
    ones64a_i = din("ones64a", [128, 128], BF16)
    ones64b_i = din("ones64b", [128, 128], BF16)
    onescol_i = din("onescol", [128, 1], BF16)
    onesrow_i = din("onesrow", [1, 128], F32)
    epsc_i = din("epsc", [128, 4], F32)

    outT = dt("outT", [128, NCH * SEQ], F32, kind="ExternalOutput").ap()
    settle = dt("settle", [128, NCH * SEQ], F32, kind="Internal").ap()
    snaps = {}
    if "snaps" in dbg:
        for nm in ["ss1", "ss2", "ss3"]:
            snaps[nm] = dt(nm, [128, NCH * SEQ], F32, kind="ExternalOutput").ap()
    okind = "ExternalOutput" if "oT" in dbg else "Internal"
    odram = dt("odram", [8, 128, SEQ], BF16, kind=okind).ap()

    dbg_t = {}
    for name in dbg:
        if name in ("oT", "snaps"):
            continue
        shp, dtp = {
            "xn": ([128, NCH * SEQ], BF16), "x2": ([128, NCH * SEQ], F32),
            "attn": ([128, NCH * SEQ], F32), "xr": ([128, NCH * SEQ], BF16),
            "xln": ([128, NCH * SEQ], BF16), "mixS": ([128, NCH * SEQ], F32),
            "disp": ([128, 16 * 256], BF16), "combt": ([128, 2 * SEQ], BF16),
            "ctm": ([128, 16 * 256], BF16), "ltm": ([128, 16 * 256], BF16),
            "cm": ([128, 64], F32),
            "sit": ([128, 8 * 256], BF16), "sot": ([128, 8 * 256], BF16),
            "lsb": ([128, 2 * SEQ], BF16), "bh": ([128, 64], F32),
            "bt": ([1, 32], F32), "qt": ([128, SEQ], BF16),
            "vt": ([128, NKC * NH * 65], BF16),
            "qsq": ([128, SEQ], BF16), "nrm": ([128, 4096], F32),
            "c2t": ([128, SEQ], BF16), "sft": ([128, SEQ], BF16),
            "qct": ([128, SEQ], BF16), "qft": ([128, SEQ], BF16),
        }[name]
        dbg_t[name] = dt("dbg_" + name, shp, dtp, kind="ExternalOutput").ap()

    ctx = ExitStack()
    with ctx:
        sb = lambda n, s, d: ctx.enter_context(nc.sbuf_tensor(n, s, d))
        S = sb("S", [128, NCH * SEQ], F32)
        XN = sb("XN", [128, NCH * SEQ], BF16)
        WS = sb("WS", [128, 2 * 4096], BF16)
        s1 = sb("s1t", [128, 32], F32)
        s2 = sb("s2t", [128, 32], F32)
        sh1 = sb("sh1t", [128, 32], F32)
        sh2 = sb("sh2t", [128, 32], F32)
        s3 = sb("s3t", [128, 8], F32)
        l1gm = sb("l1gm", [128, 8], F32)
        l1g = sb("l1gt", [128, 8], F32)
        l1b = sb("l1bt", [128, 8], F32)
        b1t = sb("b1tt", [128, 32], F32)
        b2t = sb("b2tt", [128, 128], F32)
        mask = sb("maskt", [128, 128], BF16)
        idb = sb("idbt", [128, 128], BF16)
        onesq = sb("onesqt", [128, 33], BF16)
        ones64a = sb("ones64at", [128, 128], BF16)
        ones64b = sb("ones64bt", [128, 128], BF16)
        onescol = sb("onescolt", [128, 1], BF16)
        onesrow = sb("onesrowt", [1, 128], F32)
        l1scale = sb("l1scalet", [1, 4], F32)
        BH = sb("BHt", [128, 64], F32)
        JNK = sb("JNKt", [128, 256], F32)
        EPSC = sb("EPSCt", [128, 4], F32)
        BT = sb("BTt", [1, 32], F32)
        WVT = sb("WVT", [128, 4096], BF16)

        pA = ctx.enter_context(nc.psum_tensor("pA", [128, 512], F32))
        pB = ctx.enter_context(nc.psum_tensor("pB", [128, 512], F32))
        pC = ctx.enter_context(nc.psum_tensor("pC", [128, 512], F32))
        pD = ctx.enter_context(nc.psum_tensor("pD", [128, 512], F32))
        pW = ctx.enter_context(nc.psum_tensor("pW", [128, 2048], F32))
        pb4 = [pA, pB, pC, pD]

        sem = {}
        cnt = {}
        for name in ["gq", "hq", "pe", "ve", "se",
                     "gA", "gB", "gC", "gD", "hA", "hB", "hC",
                     "oA", "oB", "xA", "xB"]:
            sem[name] = ctx.enter_context(nc.semaphore(name))
            cnt[name] = 0

        def inc(name, instr, k=1):
            instr.then_inc(sem[name], k)
            cnt[name] += k
            return cnt[name]

        _wlog = []

        def W(eng, name, val=None):
            v = cnt[name] if val is None else val
            _wlog.append((name, v))
            eng.wait_ge(sem[name], v)

        def vflush(v):
            v.tensor_copy(JNK[:, 0:128], JNK[:, 128:256])

        def _check_waits():
            bad = [(n, v, cnt[n]) for (n, v) in _wlog if v > cnt[n]]
            if bad:
                raise RuntimeError("unsatisfiable waits: %s" % bad[:10])

        def _finish():
            with nc.Block() as b:
                @b.sync
                def _(sy):
                    W(sy, "ve")
                    W(sy, "se")
                    W(sy, "pe")
                    inc("hq", sy.dma_start(outT, S[:, :]), 16)
                    W(sy, "hq")

        ws_free = [0, 0, 0]
        _gslot = ["gA", "gB", "gC"]
        _hslot = ["hA", "hB", "hC"]

        def ws_load(eng_dma, qname, src, ncols, slot):
            if ws_free[slot] > 0:
                eng_dma.wait_ge(sem["pe"], ws_free[slot])
            sname = _gslot[slot] if qname == "gq" else _hslot[slot]
            r = eng_dma.dma_start(WS[:, slot * 4096: slot * 4096 + ncols], src)
            inc(sname, r, 16)
            return (sname, cnt[sname])

        _blkid = [0]

        def blk():
            _blkid[0] += 1
            return nc.Block("b%d" % _blkid[0], no_gpsimd_drain=True)

        with blk() as b:
            @b.sync
            def _(sy):
                inc("xA", sy.dma_start(S[:, 0:4 * SEQ], x_img[:, 0:4 * SEQ]),
                    16)
                small = [(s1[:, :], s1_i), (s2[:, :], s2_i),
                         (sh1[:, :], sh1_i), (sh2[:, :], sh2_i), (s3[:, :], s3_i),
                         (l1gm[:, :], l1gamma_i), (l1g[:, :], l1g_i),
                         (l1b[:, :], l1b_i), (b1t[:, :], b1t_i),
                         (b2t[:, :], b2t_i), (mask[:, :], mask_i),
                         (idb[:, :], identb_i), (onesq[:, :], onesq_i),
                         (ones64a[:, :], ones64a_i), (ones64b[:, :], ones64b_i),
                         (onescol[:, :], onescol_i), (onesrow[:, :], onesrow_i),
                         (l1scale[0:1, 0:1], l1scale_i), (EPSC[:, :], epsc_i)]
                for d_, s_ in small:
                    inc("hq", sy.dma_start(d_, s_), 16)
                W(sy, "hq")
                W(sy, "xA")
                inc("xB", sy.dma_start(S[:, 4 * SEQ:], x_img[:, 4 * SEQ:]),
                    16)

            @b.gpsimd
            def _(gp):
                for dc in range(NCH):
                    inc("gD", gp.dma_start(
                        WVT[:, dc * 512:(dc + 1) * 512],
                        wv_img[:, dc * 1024: dc * 1024 + 512]), 16)

            @b.vector
            def _(v):
                v.memset(JNK[:, :], 0.0)

        _lnid = [0]

        def ln_stats(src, gates=None, scratch=None):
            _lnid[0] += 1
            sctx = ExitStack()
            if scratch is not None:
                XBF = scratch[:, 0:SEQ]
                XSQ = scratch[:, SEQ:2 * SEQ]
            else:
                XBF = sctx.enter_context(nc.sbuf_tensor("st_xbf%d" % _lnid[0], [128, SEQ], BF16))
                XSQ = sctx.enter_context(nc.sbuf_tensor("st_xsq%d" % _lnid[0], [128, SEQ], BF16))
            with blk() as b:
                se0 = cnt["se"]
                ve0 = cnt["ve"]
                pe0 = cnt["pe"]

                @b.scalar
                def _(sc):
                    for c in range(NCH):
                        if gates is not None:
                            sc.wait_ge(sem[gates[c][0]], gates[c][1])
                        if c >= 1:
                            W(sc, "pe", pe0 + c)
                        inc("se", sc.copy(XBF[:, :],
                                          src[:, c * SEQ:(c + 1) * SEQ]))

                @b.vector
                def _(v):
                    for c in range(NCH):
                        if gates is not None:
                            v.wait_ge(sem[gates[c][0]], gates[c][1])
                        if c >= 1:
                            W(v, "pe", pe0 + c)
                        inc("ve", v.tensor_mul(
                            XSQ[:, :], src[:, c * SEQ:(c + 1) * SEQ],
                            src[:, c * SEQ:(c + 1) * SEQ]))

                @b.tensor
                def _(te):
                    for c in range(NCH):
                        W(te, "se", se0 + c + 1)
                        W(te, "ve", ve0 + c + 1)
                        for st in range(4):
                            cl = slice(st * 512, st * 512 + 512)
                            te.matmul(pW[0:1, cl], onescol[:, :], XBF[:, cl],
                                      start=(c == 0), stop=(c == NCH - 1))
                            mm = te.matmul(pW[32:33, cl], onescol[:, :], XSQ[:, cl],
                                           start=(c == 0), stop=(c == NCH - 1))
                            if st == 3:
                                inc("pe", mm)
            sctx.close()

        def bcast_rows(rows_tiles):
            with blk() as b:
                pe0 = cnt["pe"]
                ve0 = cnt["ve"]

                @b.tensor
                def _(te):
                    W(te, "ve")
                    for i, (row, _t) in enumerate(rows_tiles):
                        if i >= 1:
                            W(te, "ve", ve0 + i * 4)
                        for st in range(4):
                            inc("pe", te.matmul(
                                pb4[st][0:128, 0:512], onesrow[0:1, :],
                                row[0:1, st * 512:(st + 1) * 512],
                                start=True, stop=True))

                @b.vector
                def _(v):
                    for i, (row, tile) in enumerate(rows_tiles):
                        for st in range(4):
                            W(v, "pe", pe0 + i * 4 + st + 1)
                            inc("ve", v.tensor_copy(
                                tile[:, st * 512:(st + 1) * 512],
                                pb4[st][0:128, 0:512]))

        s1ctx = ExitStack()
        BCA = s1ctx.enter_context(nc.sbuf_tensor("BCA", [128, SEQ], F32))
        BCB = s1ctx.enter_context(nc.sbuf_tensor("BCB", [128, SEQ], F32))
        RW = s1ctx.enter_context(nc.sbuf_tensor("RW", [1, 6144], F32))
        A_ = RW[0:1, 0:2048]
        B_ = RW[0:1, 2048:4096]
        C_ = RW[0:1, 4096:6144]

        ln_stats(S, gates=[("xA", cnt["xA"])] * 4 + [("xB", cnt["xB"])] * 4)
        with blk() as b:
            @b.vector
            def _(v):
                v.tensor_scalar_mul(A_, pW[0:1, :], 1.0 / D)
                v.tensor_scalar_mul(B_, pW[32:33, :], 1.0 / D)
                vflush(v)
                v.tensor_mul(C_, A_, A_)
                vflush(v)
                inc("ve", v.tensor_sub(B_, B_, C_))
        with blk() as b:
            @b.scalar
            def _(sc):
                inc("se", sc.activation(C_, B_, AF.Sqrt, bias=EPSC[0:1, 0:1]))
        with blk() as b:
            @b.vector
            def _(v):
                v.reciprocal(B_, C_)
                vflush(v)
                inc("ve", v.scalar_tensor_tensor(A_, A_, -1.0, B_,
                                                 ALU.mult, ALU.mult))
        bcast_rows([(B_, BCA), (A_, BCB)])
        with blk() as b:
            @b.vector
            def _(v):
                for c in range(NCH):
                    cs = slice(c * SEQ, (c + 1) * SEQ)
                    v.tensor_mul(pW[:, :], S[:, cs], BCA[:, :])
                    inc("ve", v.tensor_add(XN[:, cs], pW[:, :], BCB[:, :]))

            if "xn" in dbg_t:
                @b.sync
                def _(sy):
                    W(sy, "ve")
                    inc("hq", sy.dma_start(dbg_t["xn"], XN[:, :]), 16)
                    W(sy, "hq")
        s1ctx.close()
        if stop_after == "ln0":
            _finish()
            return nc

        actx = ExitStack()
        Vt = actx.enter_context(nc.sbuf_tensor("Vt", [128, NKC * NH * 65], BF16))
        C2 = actx.enter_context(nc.sbuf_tensor("C2", [128, SEQ], BF16))
        SFt = actx.enter_context(nc.sbuf_tensor("SFt", [128, SEQ], BF16))
        QT = actx.enter_context(nc.sbuf_tensor("QT", [128, 2 * SEQ], BF16))
        KT = actx.enter_context(nc.sbuf_tensor("KT", [128, SEQ], BF16))
        QC = actx.enter_context(nc.sbuf_tensor("QC", [128, SEQ], BF16))
        QF = actx.enter_context(nc.sbuf_tensor("QF", [128, SEQ], BF16))
        PT = actx.enter_context(nc.sbuf_tensor("PT", [128, 4 * 512], BF16))
        QSQ = actx.enter_context(nc.sbuf_tensor("QSQ", [128, 2 * SEQ], BF16))
        OSB = actx.enter_context(nc.sbuf_tensor("OSB", [128, 2 * 512], BF16))
        RBt = actx.enter_context(nc.sbuf_tensor("RBt", [128, 512], F32))
        RRt = actx.enter_context(nc.sbuf_tensor("RRt", [1, 1024], F32))
        vtr = Vt[:, :].rearrange("p (k h c) -> p k h c", k=NKC, h=NH)

        slot_q = {hp: 0 for hp in range(NCH)}
        slot_k = {hp: 1 for hp in range(NCH)}
        gq_load = {}
        osb_free = [0, 0]

        with blk() as b:
            gd_h0 = cnt["gD"]
            pe0 = cnt["pe"]
            se0 = cnt["se"]

            @b.sync
            def _(sy):
                inc("hq", sy.dma_start(C2[:, :], cos2_i), 16)
                inc("hq", sy.dma_start(SFt[:, :], sflip2_i), 16)

            @b.vector
            def _(v):
                inc("ve", v.memset(vtr[:, :, :, 64:65], 1.0))

            @b.gpsimd
            def _(gp):
                gp.wait_ge(sem["pe"], pe0 + 16)
                for dc in range(NCH):
                    inc("gD", gp.dma_start(
                        WVT[:, dc * 512:(dc + 1) * 512],
                        wv_img[:, dc * 1024 + 512: dc * 1024 + 1024]), 16)
            gd_h1 = cnt["gD"]

            @b.tensor
            def _(te):
                for st2 in range(2):
                    W(te, "gD", gd_h0 if st2 == 0 else gd_h1)
                    for nc_ in range(NKC):
                        i = st2 * 16 + nc_
                        if i >= 4:
                            W(te, "se", se0 + i - 3)
                        bank = pb4[i % 4]
                        for dc in range(NCH):
                            mm = te.matmul(
                                bank[0:128, 0:512],
                                XN[:, dc * SEQ + nc_ * 128: dc * SEQ + nc_ * 128 + 128],
                                WVT[:, dc * 512:(dc + 1) * 512],
                                start=(dc == 0), stop=(dc == NCH - 1))
                        inc("pe", mm)

            @b.scalar
            def _(sc):
                for st2 in range(2):
                    for nc_ in range(NKC):
                        i = st2 * 16 + nc_
                        W(sc, "pe", pe0 + i + 1)
                        src = pb4[i % 4][0:128, 0:512].rearrange(
                            "p (h c) -> p h c", h=8)
                        inc("se", sc.copy(
                            vtr[:, nc_, st2 * 8:(st2 + 1) * 8, 0:64], src))

        proj_done = {}


        def emit_bound_block(nhp, par):
            with blk() as b:
                pe0 = cnt["pe"]
                ve0 = cnt["ve"]

                @b.tensor
                def _(te):
                    for ti in range(2):
                        if ti == 1:
                            W(te, "ve", ve0 + 1)
                        qb = (par if ti == 0 else 1 - par) * SEQ
                        for st in range(4):
                            cl = slice(qb + st * 512, qb + (st + 1) * 512)
                            te.matmul(pW[0:128, st * 512:(st + 1) * 512],
                                      ones64a[:, :], QSQ[:, cl],
                                      start=True, stop=True)
                            mm = te.matmul(pb4[st][0:128, 0:512],
                                           ones64b[:, :], QSQ[:, cl],
                                           start=True, stop=True)
                        inc("pe", mm)

                @b.vector
                def _(v):
                    v.tensor_copy(QF[0:32, :], QC[32:64, :])
                    v.tensor_copy(QF[32:64, :], QC[0:32, :])
                    v.tensor_copy(QF[64:96, :], QC[96:128, :])
                    v.tensor_copy(QF[96:128, :], QC[64:96, :])
                    v.tensor_mul(KT[:, :], QC[:, :], C2[:, :])
                    v.tensor_mul(QF[:, :], QF[:, :], SFt[:, :])
                    v.tensor_add(KT[:, :], KT[:, :], QF[:, :])
                    for ti in range(2):
                        W(v, "pe", pe0 + ti + 1)
                        v.reduce_max(BH[:, 8 + ti * 4:8 + ti * 4 + 1],
                                     pW[:, :], axis=AX.X)
                        for st in range(4):
                            v.reduce_max(BH[:, 16 + st:16 + st + 1],
                                         pb4[st][0:128, 0:512], axis=AX.X)
                        vflush(v)
                        v.tensor_max(BH[:, 24:25], BH[:, 16:17], BH[:, 17:18])
                        v.tensor_max(BH[:, 25:26], BH[:, 18:19], BH[:, 19:20])
                        vflush(v)
                        v.tensor_max(BH[:, 8 + ti * 4 + 1:8 + ti * 4 + 2],
                                     BH[:, 24:25], BH[:, 25:26])
                        inc("ve", v.tensor_copy(JNK[:, 128:256], JNK[:, 0:128]))
                    vflush(v)
                    v.tensor_mul(BH[:, 26:27], BH[:, 8:9], BH[:, 12:13])
                    v.tensor_mul(BH[:, 27:28], BH[:, 9:10], BH[:, 13:14])
                    vflush(v)
                    inc("ve", v.tensor_tensor(BH[:, 28 + par:29 + par],
                                              BH[:, 26:27], BH[:, 27:28],
                                              ALU.max))

        for hp in range(-1, NCH):
            nhp = hp + 1
            par = nhp % 2
            cpar = hp % 2
            do_proj = nhp < NCH
            do_scores = hp >= 0
            h0, h1 = 2 * hp, 2 * hp + 1

            with blk() as b:
                if do_scores:
                    sched = [(qs, ki) for qs in range(NQS)
                             for ki in range(4 * qs + 4)]
                    nt_ = len(sched)
                else:
                    sched = []
                    nt_ = 0
                events = []
                for idx in range(nt_):
                    events.append(("S", idx))
                    if idx >= 1:
                        j = idx - 1
                        events.append(("P", j))
                        if sched[j][1] == 4 * sched[j][0] + 3:
                            events.append(("R", sched[j][0]))
                if nt_:
                    events.append(("P", nt_ - 1))
                    events.append(("R", NQS - 1))
                if do_proj:
                    jev = [("J", k) for k in range(8)]
                    if do_scores:
                        merged = []
                        nxt = 4
                        for i, e in enumerate(events):
                            merged.append(e)
                            if i + 1 == nxt and jev:
                                merged.append(jev.pop(0))
                                nxt += 9
                        merged.extend(jev)
                        events = merged
                    else:
                        events = jev
                pe0 = cnt["pe"]
                se0 = cnt["se"]
                ve0 = cnt["ve"]
                pe_scores = {}
                pe_pv = {}
                pe_rb = {}
                pe_strip = {}
                p = pe0
                for e in events:
                    p += 1
                    if e[0] == "S":
                        pe_scores[e[1]] = p
                    elif e[0] == "P":
                        pe_pv[e[1]] = p
                    elif e[0] == "R":
                        pe_rb[e[1]] = p
                    else:
                        pe_strip[e[1]] = p
                se_exp = {}
                se_ocp = {}
                se_qc = {}
                s_ = se0 + (1 if do_scores else 0)
                qcpos = {}
                qc_nexp = {}
                if do_proj and do_scores:
                    scnt, maxp = 0, -1
                    for e in events:
                        if e[0] == "S":
                            scnt += 1
                        elif e[0] == "P":
                            maxp = max(maxp, e[1])
                        elif e[0] == "J":
                            k = e[1]
                            nx = max(scnt - 1, maxp + 1, 1)
                            if k >= 4:
                                nx = max(nx, qc_nexp[3] + 1)
                            nx = min(nx, nt_)
                            qc_nexp[k] = nx
                            qcpos.setdefault(nx, []).append(k)
                se_lcp = {}
                se_rbt = {}
                sjobs = {}
                nexp = 0
                for idx, (qs, ki) in enumerate(sched):
                    nexp += 1
                    qcs = list(qcpos.get(nexp, ()))
                    if ki == 4 * qs + 3:
                        jl = [("qc", k) for k in qcs
                              if pe_strip[k] < pe_rb[qs]]
                        jl += [("lcp", qs), ("ocp", qs), ("rbt", qs)]
                        jl += [("qc", k) for k in qcs
                               if pe_strip[k] > pe_rb[qs]]
                    else:
                        jl = [("qc", k) for k in qcs]
                    sjobs[idx] = jl
                sd = {"qc": se_qc, "lcp": se_lcp, "ocp": se_ocp,
                      "rbt": se_rbt}
                for idx, (qs, ki) in enumerate(sched):
                    s_ += 1
                    se_exp[idx] = s_
                    for j in sjobs[idx]:
                        s_ += 1
                        sd[j[0]][j[1]] = s_
                if do_proj:
                    for k in range(8):
                        if k not in se_qc:
                            s_ += 1
                            se_qc[k] = s_
                bankw = {}
                jwait = {}
                rwait = {}
                for e in events:
                    if e[0] == "J":
                        lw = bankw.get(e[1] % 2)
                        if lw is not None:
                            jwait[e[1]] = lw
                        bankw[e[1] % 2] = e
                    elif e[0] == "R":
                        lw = bankw.get(e[1] % 2)
                        if lw is not None:
                            rwait[e[1]] = lw
                        bankw[e[1] % 2] = e

                ve_mask = {}
                ve_nrm = {}
                ve_sq = {}
                ve_ropeq = None
                vjobs = {}
                if do_proj and do_scores:
                    for k in range(8):
                        pos = min(qc_nexp[k], nt_ - 1)
                        vjobs.setdefault(pos, []).append(("sq", k))
                        if k == 3:
                            vjobs.setdefault(pos, []).append(("ropeq",))
                if do_scores:
                    for idx, (qs, ki) in enumerate(sched):
                        if ki == 4 * qs + 3:
                            for i in range(4):
                                vjobs.setdefault(idx + 1 + i, []).append(
                                    ("cl", qs, i))
                v_ = ve0
                vposs = sorted(set(range(nt_)) | set(vjobs.keys()))
                for idx in vposs:
                    if idx < nt_:
                        qs, ki = sched[idx]
                        if ki >= 4 * qs:
                            v_ += 1
                        ve_mask[idx] = v_
                    for job in vjobs.get(idx, []):
                        if job[0] == "sq":
                            v_ += 1
                            ve_sq[job[1]] = v_
                        elif job[0] == "ropeq":
                            v_ += 1
                            ve_ropeq = v_
                        elif job[2] == 3:
                            v_ += 1
                            ve_nrm[job[1]] = v_
                if not do_scores and do_proj:
                    for k in range(8):
                        v_ += 1
                        ve_sq[k] = v_
                        if k == 3:
                            v_ += 1
                            ve_ropeq = v_

                if do_proj:
                    @b.gpsimd
                    def _(gp):
                        if nhp >= 1:
                            ws_free[slot_q[nhp]] = proj_done[(nhp - 1, 0)]
                            ws_free[slot_k[nhp]] = proj_done[(nhp - 1, 1)]
                        gq_load[nhp] = [
                            ws_load(gp, "gq", wq_img[nhp], 1024, slot_q[nhp]),
                            ws_load(gp, "gq", wk_img[nhp], 1024, slot_k[nhp])]
                    proj_done[(nhp, 0)] = pe_strip[3]
                    proj_done[(nhp, 1)] = pe_strip[7]

                @b.tensor
                def _(te):
                    for e in events:
                        if e[0] == "S":
                            idx = e[1]
                            qs, ki = sched[idx]
                            pr = idx % 2
                            cl = slice(cpar * SEQ + qs * 512,
                                       cpar * SEQ + (qs + 1) * 512)
                            kb = slice(ki * 128, (ki + 1) * 128)
                            if idx >= 2:
                                W(te, "se", se_exp[idx - 2])
                            te.matmul(pW[:, pr * 1024:pr * 1024 + 512],
                                      KT[0:64, kb], QT[0:64, cl],
                                      start=True, stop=True)
                            inc("pe", te.matmul(
                                pW[:, pr * 1024 + 512:pr * 1024 + 1024],
                                KT[64:128, kb], QT[64:128, cl],
                                start=True, stop=True))
                        elif e[0] == "P":
                            j = e[1]
                            qs, ki = sched[j]
                            sb_ = j % 2
                            W(te, "se", se_exp[j])
                            W(te, "ve", ve_mask[j])
                            if ki == 0 and qs > 0:
                                W(te, "se", se_ocp[qs - 1])
                            pt0 = PT[:, sb_ * 1024: sb_ * 1024 + 512]
                            pt1 = PT[:, sb_ * 1024 + 512: sb_ * 1024 + 1024]
                            te.matmul(pC[0:65, 0:512],
                                      Vt[:, ki * NH * 65 + h0 * 65: ki * NH * 65 + h0 * 65 + 65],
                                      pt0, start=(ki == 0), stop=(ki == 4 * qs + 3))
                            inc("pe", te.matmul(
                                pD[0:65, 0:512],
                                Vt[:, ki * NH * 65 + h1 * 65: ki * NH * 65 + h1 * 65 + 65],
                                pt1, start=(ki == 0), stop=(ki == 4 * qs + 3)))
                        elif e[0] == "R":
                            qs = e[1]
                            W(te, "se", se_lcp[qs])
                            lw = rwait.get(qs)
                            if lw is not None:
                                W(te, "se", se_qc[lw[1]] if lw[0] == "J"
                                  else se_rbt[lw[1]])
                            pbR = pb4[qs % 2]
                            te.matmul(pbR[0:64, 0:512], onesrow[0:1, 0:64],
                                      RRt[0:1, 0:512], start=True, stop=True)
                            inc("pe", te.matmul(pbR[64:128, 0:512],
                                                onesrow[0:1, 0:64],
                                                RRt[0:1, 512:1024],
                                                start=True, stop=True))
                        else:
                            k = e[1]
                            ti, st = k // 4, k % 4
                            sn, sv = gq_load[nhp][ti]
                            te.wait_ge(sem[sn], sv)
                            lw = jwait.get(k)
                            if lw is not None:
                                W(te, "se", se_qc[lw[1]] if lw[0] == "J"
                                  else se_rbt[lw[1]])
                            slot = slot_q[nhp] if ti == 0 else slot_k[nhp]
                            bank = pb4[k % 2]
                            for dc in range(NCH):
                                mm = te.matmul(
                                    bank[0:128, 0:512],
                                    WS[:, slot * 4096 + dc * 128: slot * 4096 + dc * 128 + 128],
                                    XN[:, dc * SEQ + st * 512: dc * SEQ + (st + 1) * 512],
                                    start=(dc == 0), stop=(dc == NCH - 1))
                            inc("pe", mm)

                @b.scalar
                def _(sc):
                    if do_scores:
                        sc.copy(JNK[:, 0:64], JNK[:, 64:128])
                        sc.activation(BH[:, 30:31], BH[:, 28 + cpar:29 + cpar],
                                      AF.Sqrt, scale=1.0 / 64.0)
                        sc.copy(JNK[:, 64:128], JNK[:, 0:64])
                        inc("se", sc.activation(BH[:, 48 + cpar:49 + cpar],
                                                BH[:, 30:31], AF.Identity,
                                                scale=-1.0))
                    nexp = 0
                    for idx, (qs, ki) in enumerate(sched):
                        sb_ = idx % 2
                        W(sc, "pe", pe_scores[idx])
                        inc("se", sc.activation(
                            PT[:, sb_ * 1024:(sb_ + 1) * 1024],
                            pW[:, sb_ * 1024:(sb_ + 1) * 1024],
                            AF.Exp, bias=BH[:, 48 + cpar:49 + cpar],
                            scale=0.125))
                        for j in sjobs[idx]:
                            if j[0] == "qc":
                                k = j[1]
                                st = k % 4
                                W(sc, "pe", pe_strip[k])
                                if k >= 4:
                                    W(sc, "ve", ve_ropeq)
                                inc("se", sc.copy(
                                    QC[:, st * 512:(st + 1) * 512],
                                    pb4[k % 2][0:128, 0:512]))
                            elif j[0] == "lcp":
                                W(sc, "pe", pe_pv[idx])
                                if qs >= 1:
                                    W(sc, "pe", pe_rb[qs - 1])
                                sc.copy(RRt[0:1, 0:512], pC[64:65, 0:512])
                                inc("se", sc.copy(RRt[0:1, 512:1024],
                                                  pD[64:65, 0:512]))
                            elif j[0] == "ocp":
                                osl = OSB[:, (qs % 2) * 512:(qs % 2 + 1) * 512]
                                if osb_free[qs % 2] > 0:
                                    sc.wait_ge(
                                        sem["oA" if qs % 2 == 0 else "oB"],
                                        osb_free[qs % 2])
                                sc.copy(osl[0:64, :], pC[0:64, 0:512])
                                inc("se", sc.copy(osl[64:128, :],
                                                  pD[0:64, 0:512]))
                            else:
                                W(sc, "pe", pe_rb[qs])
                                if qs >= 1:
                                    W(sc, "ve", ve_nrm[qs - 1])
                                pbR = pb4[qs % 2]
                                sc.copy(RBt[0:64, :], pbR[0:64, 0:512])
                                inc("se", sc.copy(RBt[64:128, :],
                                                  pbR[64:128, 0:512]))
                    if do_proj and not do_scores:
                        for k in range(8):
                            ti, st = k // 4, k % 4
                            W(sc, "pe", pe_strip[k])
                            if k >= 4:
                                W(sc, "ve", ve_ropeq)
                            inc("se", sc.copy(
                                QC[:, st * 512:(st + 1) * 512],
                                pb4[k % 2][0:128, 0:512]))

                def _emit_vjob(v, job):
                    ds = slice(par * SEQ, par * SEQ + SEQ)
                    if job[0] == "cl":
                        qs_, i = job[1], job[2]
                        c0 = slice(i * 128, (i + 1) * 128)
                        ob_ = slice((qs_ % 2) * 512 + i * 128,
                                    (qs_ % 2) * 512 + (i + 1) * 128)
                        if i == 0:
                            W(v, "se", se_rbt[qs_])
                        v.reciprocal(RBt[:, c0], RBt[:, c0])
                        r = v.tensor_mul(OSB[:, ob_], OSB[:, ob_], RBt[:, c0])
                        if i == 3:
                            inc("ve", r)
                    elif job[0] == "sq":
                        k = job[1]
                        st = k % 4
                        W(v, "se", se_qc[k])
                        cl = slice(st * 512, (st + 1) * 512)
                        qb = (par if k < 4 else 1 - par) * SEQ
                        dcl = slice(qb + st * 512, qb + (st + 1) * 512)
                        inc("ve", v.tensor_mul(QSQ[:, dcl], QC[:, cl],
                                               QC[:, cl]))
                    else:
                        v.tensor_copy(QF[0:32, :], QC[32:64, :])
                        v.tensor_copy(QF[32:64, :], QC[0:32, :])
                        v.tensor_copy(QF[64:96, :], QC[96:128, :])
                        v.tensor_copy(QF[96:128, :], QC[64:96, :])
                        v.tensor_mul(QT[:, ds], QC[:, :], C2[:, :])
                        v.tensor_mul(QF[:, :], QF[:, :], SFt[:, :])
                        inc("ve", v.tensor_add(QT[:, ds], QT[:, ds], QF[:, :]))

                @b.vector
                def _(v):
                    for idx in vposs:
                        if idx < nt_:
                            qs, ki = sched[idx]
                            sb_ = idx % 2
                            m = ki - 4 * qs
                            if m >= 0:
                                W(v, "se", se_exp[idx])
                                for half in range(2):
                                    tb = sb_ * 1024 + half * 512
                                    if m >= 1:
                                        v.memset(PT[:, tb:tb + m * 128], 0.0)
                                    r = v.tensor_mul(
                                        PT[:, tb + m * 128:tb + (m + 1) * 128],
                                        PT[:, tb + m * 128:tb + (m + 1) * 128],
                                        mask[:, :])
                                inc("ve", r)
                        for job in vjobs.get(idx, []):
                            _emit_vjob(v, job)
                    if not do_scores and do_proj:
                        for k in range(8):
                            _emit_vjob(v, ("sq", k))
                            if k == 3:
                                _emit_vjob(v, ("ropeq",))

                if do_scores:
                    @b.sync
                    def _(sy):
                        for qs in range(NQS):
                            W(sy, "ve", ve_nrm[qs])
                            osl = OSB[:, (qs % 2) * 512:(qs % 2 + 1) * 512]
                            osb_free[qs % 2] = inc(
                                "oA" if qs % 2 == 0 else "oB",
                                sy.dma_start(
                                    odram[hp, :, qs * 512:(qs + 1) * 512],
                                    osl), 16)
                        W(sy, "oA")
                        W(sy, "oB")

            if do_proj:
                emit_bound_block(nhp, par)
            if stop_after == ("ahp%d" % hp):
                _finish()
                actx.close()
                _check_waits()
                return nc

        actx.close()

        woctx = ExitStack()
        WOT = woctx.enter_context(nc.sbuf_tensor("WOT", [128, 8192], BF16))
        OL = woctx.enter_context(nc.sbuf_tensor("OL", [128, 2 * 4096], BF16))
        with blk() as b:
            g0 = cnt["gq"]

            @b.gpsimd
            def _(gp):
                for oc in range(8):
                    inc("gD", gp.dma_start(WOT[:, oc * 1024:(oc + 1) * 1024],
                                           wo_img[oc]), 16)
            gd_wot = cnt["gD"]
            pe0 = cnt["pe"]
            ve0 = cnt["ve"]
            ol_cnt = {}

            @b.sync
            def _(sy):
                for ns in range(4):
                    if ns >= 2:
                        W(sy, "pe", pe0 + (ns - 1) * 8)
                    src = odram[:, :, ns * 512:(ns + 1) * 512] \
                        .rearrange("h p n -> p h n")
                    sn = _hslot[ns % 2]
                    r = sy.dma_start(
                        OL[:, (ns % 2) * 4096:(ns % 2 + 1) * 4096]
                        .rearrange("p (h n) -> p h n", h=8), src)
                    ol_cnt[ns] = (sn, inc(sn, r, 16))

            @b.tensor
            def _(te):
                W(te, "gD", gd_wot)
                for ns in range(4):
                    te.wait_ge(sem[ol_cnt[ns][0]], ol_cnt[ns][1])
                    ob = (ns % 2) * 4096
                    for oc in range(8):
                        bank = pb4[oc % 4]
                        if ns * 8 + oc >= 4:
                            W(te, "ve", ve0 + ns * 8 + oc - 3)
                        for dc in range(NCH):
                            mm = te.matmul(
                                bank[0:128, 0:512],
                                WOT[:, oc * 1024 + dc * 128: oc * 1024 + dc * 128 + 128],
                                OL[:, ob + dc * 512: ob + (dc + 1) * 512],
                                start=(dc == 0), stop=(dc == NCH - 1))
                        inc("pe", mm)

            @b.vector
            def _(v):
                for ns in range(4):
                    for oc in range(8):
                        W(v, "pe", pe0 + ns * 8 + oc + 1)
                        bank = pb4[oc % 4]
                        dstc = slice(oc * SEQ + ns * 512, oc * SEQ + (ns + 1) * 512)
                        inc("ve", v.tensor_add(S[:, dstc], S[:, dstc],
                                               bank[0:128, 0:512]))
        woctx.close()
        if snaps:
            with blk() as b:
                @b.sync
                def _(sy):
                    W(sy, "ve")
                    inc("hq", sy.dma_start(snaps["ss1"], S[:, :]), 16)
        if "attn" in dbg_t:
            with blk() as b:
                @b.sync
                def _(sy):
                    W(sy, "ve")
                    inc("hq", sy.dma_start(dbg_t["attn"], S[:, :]), 16)
                    W(sy, "hq")

        if stop_after == "attn":
            _finish()
            return nc

        fctx = ExitStack()
        HA = fctx.enter_context(nc.sbuf_tensor("HA", [128, HCH * 512], BF16))
        HB = fctx.enter_context(nc.sbuf_tensor("HB", [128, HCH * 512], BF16))
        WS3 = fctx.enter_context(nc.sbuf_tensor("WS3", [128, 4096], BF16))

        def wsap(slot, lo, hi):
            if slot < 2:
                return WS[:, slot * 4096 + lo: slot * 4096 + hi]
            return WS3[:, lo:hi]
        for nt in range(4):
            ntc = slice(nt * 512, (nt + 1) * 512)
            with blk() as b:
                pe0 = cnt["pe"]
                se0 = cnt["se"]
                ve0 = cnt["ve"]
                hq0 = cnt["hq"]
                peA1 = {o: pe0 + 2 * o + 1 for o in range(32)}
                peA2 = {o: pe0 + 2 * o + 2 for o in range(32)}
                seA = {o: se0 + o + 1 for o in range(32)}
                veA = {o: ve0 + o + 1 for o in range(32)}
                peH1 = {o: pe0 + 64 + o + 1 for o in range(32)}
                seH1 = {o: se0 + 32 + o + 1 for o in range(32)}
                peH2 = {o: pe0 + 96 + o + 1 for o in range(32)}
                seH2 = {o: se0 + 64 + o + 1 for o in range(32)}
                peW3 = {o: pe0 + 128 + o + 1 for o in range(8)}
                veW3 = {o: ve0 + 32 + o + 1 for o in range(8)}
                hqA = {}
                hqH1 = {}
                hqH2 = {}
                hqW3 = {}
                _scnt = dict(cnt)

                def _ld2(o, tab):
                    slot = o % 3
                    sn = _hslot[slot]
                    _scnt[sn] += 32
                    tab[o] = (sn, _scnt[sn])
                for o in range(32):
                    _ld2(o, hqA)
                for o in range(32):
                    _ld2(o, hqH1)
                for o in range(32):
                    _ld2(o, hqH2)

                def _ld1(o, tab):
                    slot = o % 3
                    sn = _hslot[slot]
                    _scnt[sn] += 16
                    tab[o] = (sn, _scnt[sn])
                hqA = {}
                hqH1 = {}
                hqH2 = {}
                hqW3 = {}
                _scnt = dict(cnt)
                for o in range(32):
                    sn = _hslot[o % 3]
                    _scnt[sn] += 32
                    hqA[o] = (sn, _scnt[sn])
                for o in range(32):
                    sn = _hslot[o % 3]
                    _scnt[sn] += 16
                    hqH1[o] = (sn, _scnt[sn])
                for o in range(32):
                    sn = _hslot[o % 3]
                    _scnt[sn] += 16
                    hqH2[o] = (sn, _scnt[sn])
                for o in range(8):
                    sn = _hslot[o % 3]
                    _scnt[sn] += 16
                    hqW3[o] = (sn, _scnt[sn])

                @b.sync
                def _(sy):
                    for o in range(32):
                        slot = o % 3
                        sn = _hslot[slot]
                        if ws_free[slot] > 0:
                            sy.wait_ge(sem["pe"], ws_free[slot])
                        inc(sn, sy.dma_start(
                            wsap(slot, 0, 1024), w1_img[o]), 16)
                        inc(sn, sy.dma_start(
                            wsap(slot, 1024, 2048),
                            w2_img[o]), 16)
                        ws_free[slot] = peA2[o]
                    for img, tab in [(h1_img, peH1), (h2_img, peH2)]:
                        for o in range(32):
                            slot = o % 3
                            sn = _hslot[slot]
                            sy.wait_ge(sem["pe"], ws_free[slot])
                            inc(sn, sy.dma_start(
                                wsap(slot, 0, 4096), img[o]), 16)
                            ws_free[slot] = tab[o]
                    for o in range(8):
                        slot = o % 3
                        sn = _hslot[slot]
                        sy.wait_ge(sem["pe"], ws_free[slot])
                        inc(sn, sy.dma_start(
                            wsap(slot, 0, 4096), w3_img[o]), 16)
                        ws_free[slot] = peW3[o]

                @b.tensor
                def _(te):
                    for o in range(32):
                        slot = o % 3
                        te.wait_ge(sem[hqA[o][0]], hqA[o][1])
                        if o >= 2:
                            W(te, "ve", veA[o - 2])
                        bA = pb4[2 * (o % 2)]
                        bB = pb4[2 * (o % 2) + 1]
                        for dc in range(NCH):
                            mm = te.matmul(
                                bA[0:128, 0:512],
                                wsap(slot, dc * 128, dc * 128 + 128),
                                XN[:, dc * SEQ + nt * 512: dc * SEQ + (nt + 1) * 512],
                                start=(dc == 0), stop=(dc == NCH - 1))
                        inc("pe", mm)
                        for dc in range(NCH):
                            mm = te.matmul(
                                bB[0:128, 0:512],
                                wsap(slot, 1024 + dc * 128, 1024 + dc * 128 + 128),
                                XN[:, dc * SEQ + nt * 512: dc * SEQ + (nt + 1) * 512],
                                start=(dc == 0), stop=(dc == NCH - 1))
                        inc("pe", mm)
                    for o in range(32):
                        slot = o % 3
                        te.wait_ge(sem[hqH1[o][0]], hqH1[o][1])
                        if o < 4:
                            W(te, "ve", veA[31])
                        bank = pb4[o % 4]
                        if o >= 4:
                            W(te, "se", seH1[o - 4])
                        for kc in range(HCH):
                            mm = te.matmul(
                                bank[0:128, 0:512],
                                wsap(slot, kc * 128, kc * 128 + 128),
                                HA[:, kc * 512:(kc + 1) * 512],
                                start=(kc == 0), stop=(kc == HCH - 1))
                        inc("pe", mm)
                    for o in range(32):
                        slot = o % 3
                        te.wait_ge(sem[hqH2[o][0]], hqH2[o][1])
                        if o < 4:
                            W(te, "se", seH1[31])
                        bank = pb4[o % 4]
                        if o >= 4:
                            W(te, "se", seH2[o - 4])
                        for kc in range(HCH):
                            mm = te.matmul(
                                bank[0:128, 0:512],
                                wsap(slot, kc * 128, kc * 128 + 128),
                                HB[:, kc * 512:(kc + 1) * 512],
                                start=(kc == 0), stop=(kc == HCH - 1))
                        inc("pe", mm)
                    for o in range(8):
                        slot = o % 3
                        te.wait_ge(sem[hqW3[o][0]], hqW3[o][1])
                        if o < 4:
                            W(te, "se", seH2[31])
                        bank = pb4[o % 4]
                        if o >= 4:
                            W(te, "ve", veW3[o - 4])
                        for kc in range(HCH):
                            mm = te.matmul(
                                bank[0:128, 0:512],
                                wsap(slot, kc * 128, kc * 128 + 128),
                                HA[:, kc * 512:(kc + 1) * 512],
                                start=(kc == 0), stop=(kc == HCH - 1))
                        inc("pe", mm)

                @b.scalar
                def _(sc):
                    for o in range(32):
                        W(sc, "pe", peA1[o])
                        inc("se", sc.activation(HA[:, o * 512:(o + 1) * 512],
                                                pb4[2 * (o % 2)][0:128, 0:512],
                                                AF.Silu, scale=s1[:, o:o + 1]))
                    for o in range(32):
                        W(sc, "pe", peH1[o])
                        inc("se", sc.activation(HB[:, o * 512:(o + 1) * 512],
                                                pb4[o % 4][0:128, 0:512],
                                                AF.Gelu, scale=sh1[:, o:o + 1]))
                    for o in range(32):
                        W(sc, "pe", peH2[o])
                        inc("se", sc.activation(HA[:, o * 512:(o + 1) * 512],
                                                pb4[o % 4][0:128, 0:512],
                                                AF.Gelu, scale=sh2[:, o:o + 1]))

                @b.vector
                def _(v):
                    for o in range(32):
                        W(v, "pe", peA2[o])
                        W(v, "se", seA[o])
                        hsl = HA[:, o * 512:(o + 1) * 512]
                        inc("ve", v.scalar_tensor_tensor(
                            hsl, pb4[2 * (o % 2) + 1][0:128, 0:512],
                            s2[:, o:o + 1], hsl, ALU.mult, ALU.mult))
                    for o in range(8):
                        W(v, "pe", peW3[o])
                        dstc = slice(o * SEQ + nt * 512, o * SEQ + (nt + 1) * 512)
                        inc("ve", v.scalar_tensor_tensor(
                            S[:, dstc], pb4[o % 4][0:128, 0:512],
                            s3[:, o:o + 1], S[:, dstc], ALU.mult, ALU.add))
        fctx.close()
        if snaps:
            with blk() as b:
                @b.sync
                def _(sy):
                    W(sy, "ve")
                    inc("hq", sy.dma_start(snaps["ss2"], S[:, :]), 16)
        if "x2" in dbg_t:
            with blk() as b:
                @b.sync
                def _(sy):
                    W(sy, "ve")
                    inc("hq", sy.dma_start(dbg_t["x2"], S[:, :]), 16)
                    W(sy, "hq")

        if stop_after == "ffn":
            _finish()
            return nc

        xrctx = ExitStack()
        XR = xrctx.enter_context(nc.sbuf_tensor("XR", [128, NCH * SEQ], BF16))
        s7ctx = ExitStack()
        BCA = s7ctx.enter_context(nc.sbuf_tensor("BCA7", [128, SEQ], F32))
        BCB = s7ctx.enter_context(nc.sbuf_tensor("BCB7", [128, SEQ], F32))
        s7r = ExitStack()
        RW = s7r.enter_context(nc.sbuf_tensor("RW7", [1, 8192], F32))
        A_ = RW[0:1, 0:2048]
        B_ = RW[0:1, 2048:4096]
        C_ = RW[0:1, 4096:6144]
        V_ = RW[0:1, 6144:8192]
        D_ = BCA[0:1, 0:2048]
        E_ = BCB[0:1, 0:2048]

        ln_stats(S, scratch=WVT)
        with blk() as b:
            @b.vector
            def _(v):
                v.tensor_scalar_mul(A_, pW[0:1, :], 1.0 / D)
                v.tensor_scalar_mul(B_, pW[32:33, :], 1.0 / D)
                vflush(v)
                v.tensor_mul(C_, A_, A_)
                vflush(v)
                inc("ve", v.tensor_sub(V_, B_, C_))
        with blk() as b:
            @b.scalar
            def _(sc):
                sc.activation(C_, V_, AF.Sqrt, scale=1.0 + LN_EPS,
                              bias=EPSC[0:1, 2:3])
                inc("se", sc.activation(D_, V_, AF.Sqrt, scale=float(D)))
        with blk() as b:
            @b.vector
            def _(v):
                v.reciprocal(C_, C_)
                v.tensor_scalar_max(D_, D_, 1e-18)
                vflush(v)
                v.scalar_tensor_tensor(B_, A_, -1.0, C_,
                                       ALU.mult, ALU.mult)
                v.reciprocal(D_, D_)
                vflush(v)
                v.tensor_scalar_mul(D_, D_, 32.0)
                vflush(v)
                inc("ve", v.scalar_tensor_tensor(E_, A_, -1.0, D_,
                                                 ALU.mult, ALU.mult))
        bcast_rows([(D_, BCA), (E_, BCB)])
        with blk() as b:
            @b.vector
            def _(v):
                for c in range(NCH):
                    cs = slice(c * SEQ, (c + 1) * SEQ)
                    v.tensor_mul(pW[:, :], S[:, cs], BCA[:, :])
                    inc("ve", v.tensor_add(XR[:, cs], pW[:, :], BCB[:, :]))
            if "xr" in dbg_t:
                @b.sync
                def _(sy):
                    W(sy, "ve")
                    inc("hq", sy.dma_start(dbg_t["xr"], XR[:, :]), 16)
                    W(sy, "hq")
        bcast_rows([(C_, BCA), (B_, BCB)])
        s7r.close()
        xln_in_mixer = not (stop_after == "l1n" or "xln" in dbg_t)
        if not xln_in_mixer:
            with blk() as b:
                @b.vector
                def _(v):
                    for c in range(NCH):
                        cs = slice(c * SEQ, (c + 1) * SEQ)
                        v.tensor_mul(pW[:, :], S[:, cs], BCA[:, :])
                        v.tensor_add(XN[:, cs], pW[:, :], BCB[:, :])
                        inc("ve", v.tensor_scalar(
                            XN[:, cs], XN[:, cs], l1g[:, c:c + 1],
                            l1b[:, c:c + 1], ALU.mult, ALU.add))
                if "xln" in dbg_t:
                    @b.sync
                    def _(sy):
                        W(sy, "ve")
                        inc("hq", sy.dma_start(dbg_t["xln"], XN[:, :]), 16)
                        W(sy, "hq")

        if stop_after == "l1n":
            _finish()
            s7ctx.close()
            xrctx.close()
            return nc

        mctx = ExitStack()
        KVT = mctx.enter_context(nc.sbuf_tensor("KVT", [128, SEQ], F32))
        AT = mctx.enter_context(nc.sbuf_tensor("AT", [128, SEQ], F32))
        KVS = mctx.enter_context(nc.sbuf_tensor("KVS", [128, SEQ], F32))
        MIXT = mctx.enter_context(nc.sbuf_tensor("MIXT", [128, SEQ], F32))
        with blk() as b:
            mx_prev = None
            for c in range(NCH):
                g0 = cnt["gq"]
                pe0 = cnt["pe"]
                se0 = cnt["se"]
                ve0 = cnt["ve"]
                prevc = mx_prev

                mx_ld = {}

                @b.gpsimd
                def _(gp):
                    for j, oc in enumerate([8 + c, 16 + c, c]):
                        slot = j % 2
                        if ws_free[slot] > 0:
                            gp.wait_ge(sem["pe"], ws_free[slot])
                        if j == 2:
                            gp.wait_ge(sem["pe"], pe0 + 4)
                        mx_ld[j] = ws_load(gp, "gq", qkva_img[oc], 1024, slot)

                @b.vector
                def _(v):
                    for j in range(3):
                        slot = j % 2
                        v.wait_ge(sem[mx_ld[j][0]], mx_ld[j][1])
                        for dc in range(NCH):
                            r = v.tensor_scalar_mul(
                                WS[:, slot * 4096 + dc * 128: slot * 4096 + (dc + 1) * 128],
                                WS[:, slot * 4096 + dc * 128: slot * 4096 + (dc + 1) * 128],
                                l1gm[:, dc:dc + 1])
                        inc("ve", r)

                @b.tensor
                def _(te):
                    for j, dst in enumerate(["kv", "a", "q"]):
                        slot = j % 2
                        W(te, "ve", ve0 + j + 1)
                        if j == 0 and prevc is not None:
                            W(te, "se", prevc[0])
                        if j == 2 and prevc is not None:
                            W(te, "ve", prevc[2])
                        for st in range(4):
                            bank = pb4[st]
                            if j >= 1:
                                W(te, "se", se0 + j * 4 - 4 + st + 1)
                            for dc in range(NCH):
                                out = bank[0:128, 0:512] if j < 2 else \
                                    pW[:, st * 512:(st + 1) * 512]
                                mm = te.matmul(
                                    out,
                                    WS[:, slot * 4096 + dc * 128: slot * 4096 + dc * 128 + 128],
                                    XR[:, dc * SEQ + st * 512: dc * SEQ + (st + 1) * 512],
                                    start=(dc == 0), stop=(dc == NCH - 1))
                            inc("pe", mm)
                ws_free[0] = cnt["pe"]
                ws_free[1] = cnt["pe"]

                @b.scalar
                def _(sc):
                    for st in range(4):
                        W(sc, "pe", pe0 + st + 1)
                        if st == 0 and prevc is not None:
                            W(sc, "ve", prevc[1])
                        inc("se", sc.copy(KVT[:, st * 512:(st + 1) * 512],
                                          pb4[st][0:128, 0:512]))
                    for st in range(4):
                        W(sc, "pe", pe0 + 4 + st + 1)
                        inc("se", sc.activation(AT[:, st * 512:(st + 1) * 512],
                                                pb4[st][0:128, 0:512], AF.Sigmoid))

                @b.vector
                def _(v):
                    cs = slice(c * SEQ, (c + 1) * SEQ)
                    if xln_in_mixer:
                        v.tensor_mul(MIXT[:, :], S[:, cs], BCA[:, :])
                        v.tensor_add(XN[:, cs], MIXT[:, :], BCB[:, :])
                        v.tensor_scalar(XN[:, cs], XN[:, cs],
                                        l1g[:, c:c + 1], l1b[:, c:c + 1],
                                        ALU.mult, ALU.add)
                    W(v, "se", se0 + 8)
                    inc("ve", v.tensor_tensor_scan(KVS[:, :], AT[:, :], KVT[:, :],
                                                   0.0, ALU.mult, ALU.add))
                    W(v, "pe", pe0 + 12)
                    v.tensor_mul(MIXT[:, :], pW[:, :], KVS[:, :])
                    inc("ve", v.tensor_add(S[:, cs], S[:, cs], MIXT[:, :]))
                mx_prev = (se0 + 8, ve0 + 4, ve0 + 5)
        if snaps:
            with blk() as b:
                @b.sync
                def _(sy):
                    W(sy, "ve")
                    inc("hq", sy.dma_start(snaps["ss3"], S[:, :]), 16)
        mctx.close()
        s7ctx.close()
        xrctx.close()
        if "mixS" in dbg_t:
            with blk() as b:
                @b.sync
                def _(sy):
                    W(sy, "ve")
                    inc("hq", sy.dma_start(dbg_t["mixS"], S[:, :]), 16)
                    W(sy, "hq")

        if stop_after == "mix":
            _finish()
            return nc

        m1ctx = ExitStack()
        DTM = m1ctx.enter_context(nc.sbuf_tensor("DTM", [128, 16 * 256], BF16))
        CBT = m1ctx.enter_context(nc.sbuf_tensor("CBT", [128, 2 * SEQ], BF16))
        inner9 = ExitStack()
        MUB = inner9.enter_context(nc.sbuf_tensor("MUB", [128, 2048], BF16))
        R3B = inner9.enter_context(nc.sbuf_tensor("R3B", [128, SEQ], F32))
        LSB = inner9.enter_context(nc.sbuf_tensor("LSB", [128, 2 * SEQ], BF16))
        SQS = inner9.enter_context(nc.sbuf_tensor("SQS", [128, 2048], BF16))
        SQS2 = inner9.enter_context(nc.sbuf_tensor("SQS2", [128, 2048], BF16))
        RW9 = inner9.enter_context(nc.sbuf_tensor("RW9", [1, 2304], F32))
        NMX = inner9.enter_context(nc.sbuf_tensor("NMX", [128, 32], F32))
        DSP = inner9.enter_context(nc.sbuf_tensor("DSP", [128, SEQ], BF16))
        CTM = inner9.enter_context(nc.sbuf_tensor("CTM", [128, 16 * 256], BF16))
        LTM = inner9.enter_context(nc.sbuf_tensor("LTM", [128, 16 * 256], BF16))
        CM0 = inner9.enter_context(nc.sbuf_tensor("CM0", [128, 16], F32))
        CM1 = inner9.enter_context(nc.sbuf_tensor("CM1", [128, 16], F32))
        CS0 = inner9.enter_context(nc.sbuf_tensor("CS0", [128, 16], F32))
        CS1 = inner9.enter_context(nc.sbuf_tensor("CS1", [128, 16], F32))
        r3row = RW9[0:1, 0:2048]
        murow = RW9[0:1, 2048:2304]

        with blk() as b:
            g0 = cnt["gq"]

            @b.gpsimd
            def _(gp):
                inc("gD", gp.dma_start(MUB[:, :], mu_img), 16)
            gd_mub = cnt["gD"]
            se0 = cnt["se"]
            pe0 = cnt["pe"]

            @b.scalar
            def _(sc):
                W(sc, "gD", gd_mub)
                inc("se", sc.activation(SQS[:, :], MUB[:, :], AF.Square))

            @b.tensor
            def _(te):
                W(te, "se", se0 + 1)
                for dc in range(NCH):
                    mm = te.matmul(pA[0:1, 0:256], onescol[:, :],
                                   SQS[:, dc * 256:(dc + 1) * 256],
                                   start=(dc == 0), stop=(dc == NCH - 1))
                inc("pe", mm)

            @b.scalar
            def _(sc):
                W(sc, "pe", pe0 + 1)
                inc("se", sc.activation(murow, pA[0:1, 0:256], AF.Sqrt))
        with blk() as b:
            @b.vector
            def _(v):
                v.tensor_scalar_max(murow, murow, 1e-12)
                vflush(v)
                inc("ve", v.reciprocal(murow, murow))
        with blk() as b:
            pe1 = cnt["pe"]

            @b.tensor
            def _(te):
                inc("pe", te.matmul(pB[0:128, 0:256], onesrow[0:1, :], murow,
                                    start=True, stop=True))

            @b.vector
            def _(v):
                W(v, "pe", pe1 + 1)
                for dc in range(NCH):
                    r = v.tensor_mul(MUB[:, dc * 256:(dc + 1) * 256],
                                     MUB[:, dc * 256:(dc + 1) * 256],
                                     pB[0:128, 0:256])
                inc("ve", r)

        with blk() as b:
            se0 = cnt["se"]
            ve0 = cnt["ve"]
            pe0 = cnt["pe"]
            sq_done = {}
            scn, ven = 0, 0
            for c in range(NCH):
                if c % 2 == 0:
                    scn += 1
                    sq_done[c] = ("se", se0 + scn)
                else:
                    ven += 1
                    sq_done[c] = ("ve", ve0 + ven)

            @b.scalar
            def _(sc):
                for c in range(0, NCH, 2):
                    if c >= 2:
                        W(sc, "pe", pe0 + c - 1)
                    inc("se", sc.activation(SQS[:, :],
                                            XN[:, c * SEQ:(c + 1) * SEQ],
                                            AF.Square))

            @b.vector
            def _(v):
                for c in range(1, NCH, 2):
                    if c >= 2:
                        W(v, "pe", pe0 + c - 1)
                    inc("ve", v.tensor_mul(SQS2[:, :],
                                           XN[:, c * SEQ:(c + 1) * SEQ],
                                           XN[:, c * SEQ:(c + 1) * SEQ]))

            @b.tensor
            def _(te):
                for c in range(NCH):
                    en, val = sq_done[c]
                    W(te, en, val)
                    buf = SQS if c % 2 == 0 else SQS2
                    for st in range(4):
                        mm = te.matmul(pW[0:1, st * 512:(st + 1) * 512],
                                       onescol[:, :],
                                       buf[:, st * 512:(st + 1) * 512],
                                       start=(c == 0), stop=(c == NCH - 1))
                    inc("pe", mm)

            @b.scalar
            def _(sc):
                W(sc, "pe", pe0 + 8)
                inc("se", sc.activation(r3row, pW[0:1, :], AF.Sqrt))
        with blk() as b:
            @b.vector
            def _(v):
                v.tensor_scalar_max(r3row, r3row, 1e-12)
                vflush(v)
                v.reciprocal(r3row, r3row)
                vflush(v)
                inc("ve", v.tensor_scalar_mul(r3row, r3row,
                                              l1scale[0:1, 0:1]))
        bcast_rows([(r3row, R3B)])

        for esc in range(2):
            with blk() as b:
                pe0 = cnt["pe"]

                @b.tensor
                def _(te):
                    for st in range(4):
                        for dc in range(NCH):
                            mm = te.matmul(
                                pW[:, st * 512:(st + 1) * 512],
                                MUB[:, dc * 256 + esc * 128: dc * 256 + esc * 128 + 128],
                                XN[:, dc * SEQ + st * 512: dc * SEQ + (st + 1) * 512],
                                start=(dc == 0), stop=(dc == NCH - 1))
                    inc("pe", mm)

                @b.vector
                def _(v):
                    W(v, "pe", pe0 + 1)
                    lsl = LSB[:, esc * SEQ:(esc + 1) * SEQ]
                    v.tensor_mul(lsl, pW[:, :], R3B[:, :])
                    inc("ve", v.reduce_max(NMX[:, 0:1], lsl, axis=AX.X,
                                           negate=True))
            with blk() as b:
                @b.scalar
                def _(sc):
                    lsl = LSB[:, esc * SEQ:(esc + 1) * SEQ]
                    inc("se", sc.activation(DSP[:, :], lsl, AF.Exp,
                                            bias=NMX[:, 0:1],
                                            accum_out=NMX[:, 8:9]))
            with blk() as b:
                @b.vector
                def _(v):
                    v.reciprocal(NMX[:, 16:17], NMX[:, 8:9])
                    vflush(v)
                    inc("ve", v.tensor_scalar_mul(DSP[:, :], DSP[:, :],
                                                  NMX[:, 16:17]))
            with blk() as b:
                pe1 = cnt["pe"]
                se1 = cnt["se"]

                @b.tensor
                def _(te):
                    for nc_ in range(16):
                        pt = (pA if nc_ % 2 == 0 else pB)[:, 0:64].bitcast(BF16)
                        if nc_ >= 2:
                            W(te, "se", se1 + nc_ - 1)
                        inc("pe", te.transpose(pt, DSP[:, nc_ * 128:(nc_ + 1) * 128],
                                               idb[:, :]))

                @b.scalar
                def _(sc):
                    for nc_ in range(16):
                        W(sc, "pe", pe1 + nc_ + 1)
                        pt = (pA if nc_ % 2 == 0 else pB)[:, 0:64].bitcast(BF16)
                        inc("se", sc.copy(
                            DTM[:, nc_ * 256 + esc * 128: nc_ * 256 + (esc + 1) * 128],
                            pt))
        if "disp" in dbg_t:
            with blk() as b:
                @b.sync
                def _(sy):
                    W(sy, "se")
                    inc("hq", sy.dma_start(dbg_t["disp"], DTM[:, :]), 16)
                    W(sy, "hq")
        if "lsb" in dbg_t:
            with blk() as b:
                @b.sync
                def _(sy):
                    W(sy, "se")
                    inc("hq", sy.dma_start(dbg_t["lsb"], LSB[:, :]), 16)
                    W(sy, "hq")

        with blk() as b:
            pe0 = cnt["pe"]
            se0 = cnt["se"]

            @b.tensor
            def _(te):
                for i in range(32):
                    nc_, esc = i // 2, i % 2
                    pt = (pA if i % 2 == 0 else pB)[:, 0:64].bitcast(BF16)
                    if i >= 2:
                        W(te, "se", se0 + i - 1)
                    inc("pe", te.transpose(
                        pt, LSB[:, esc * SEQ + nc_ * 128: esc * SEQ + (nc_ + 1) * 128],
                        idb[:, :]))

            @b.scalar
            def _(sc):
                for i in range(32):
                    nc_, esc = i // 2, i % 2
                    W(sc, "pe", pe0 + i + 1)
                    pt = (pA if i % 2 == 0 else pB)[:, 0:64].bitcast(BF16)
                    inc("se", sc.copy(
                        LTM[:, nc_ * 256 + esc * 128: nc_ * 256 + (esc + 1) * 128],
                        pt))
        with blk() as b:
            @b.vector
            def _(v):
                for nc_ in range(16):
                    v.reduce_max(CM0[:, nc_:nc_ + 1],
                                 LTM[:, nc_ * 256: nc_ * 256 + 128], axis=AX.X)
                    v.reduce_max(CM1[:, nc_:nc_ + 1],
                                 LTM[:, nc_ * 256 + 128: nc_ * 256 + 256], axis=AX.X)
                vflush(v)
                v.tensor_max(CM0[:, 0:16], CM0[:, 0:16], CM1[:, 0:16])
                vflush(v)
                inc("ve", v.tensor_scalar_mul(CM0[:, 0:16], CM0[:, 0:16], -1.0))
        with blk() as b:
            @b.scalar
            def _(sc):
                for nc_ in range(16):
                    sc.activation(CTM[:, nc_ * 256: nc_ * 256 + 128],
                                  LTM[:, nc_ * 256: nc_ * 256 + 128], AF.Exp,
                                  bias=CM0[:, nc_:nc_ + 1],
                                  accum_out=CS0[:, nc_:nc_ + 1])
                    inc("se", sc.activation(
                        CTM[:, nc_ * 256 + 128: nc_ * 256 + 256],
                        LTM[:, nc_ * 256 + 128: nc_ * 256 + 256], AF.Exp,
                        bias=CM0[:, nc_:nc_ + 1],
                        accum_out=CS1[:, nc_:nc_ + 1]))
        with blk() as b:
            @b.vector
            def _(v):
                v.tensor_add(CS0[:, 0:16], CS0[:, 0:16], CS1[:, 0:16])
                vflush(v)
                v.reciprocal(CS0[:, 0:16], CS0[:, 0:16])
                vflush(v)
                for nc_ in range(16):
                    r = v.tensor_scalar_mul(CTM[:, nc_ * 256:(nc_ + 1) * 256],
                                            CTM[:, nc_ * 256:(nc_ + 1) * 256],
                                            CS0[:, nc_:nc_ + 1])
                inc("ve", r)

        if dbg_t:
            with blk() as b:
                @b.sync
                def _(sy):
                    W(sy, "ve")
                    for nm, src in [("ctm", CTM[:, :]), ("ltm", LTM[:, :])]:
                        if nm in dbg_t:
                            inc("hq", sy.dma_start(dbg_t[nm], src), 16)
                    if "cm" in dbg_t:
                        inc("hq", sy.dma_start(dbg_t["cm"][:, 0:16], CM0[:, :]), 16)
                        inc("hq", sy.dma_start(dbg_t["cm"][:, 16:32], CM1[:, :]), 16)
                        inc("hq", sy.dma_start(dbg_t["cm"][:, 32:48], CS0[:, :]), 16)
                        inc("hq", sy.dma_start(dbg_t["cm"][:, 48:64], CS1[:, :]), 16)
                    W(sy, "hq")

        with blk() as b:
            pe0 = cnt["pe"]
            se0 = cnt["se"]

            @b.tensor
            def _(te):
                W(te, "ve")
                for i in range(32):
                    nc_, esc = i // 2, i % 2
                    pt = (pA if i % 2 == 0 else pB)[:, 0:64].bitcast(BF16)
                    if i >= 2:
                        W(te, "se", se0 + i - 1)
                    inc("pe", te.transpose(
                        pt, CTM[:, nc_ * 256 + esc * 128: nc_ * 256 + (esc + 1) * 128],
                        idb[:, :]))

            @b.scalar
            def _(sc):
                for i in range(32):
                    nc_, esc = i // 2, i % 2
                    W(sc, "pe", pe0 + i + 1)
                    pt = (pA if i % 2 == 0 else pB)[:, 0:64].bitcast(BF16)
                    inc("se", sc.copy(
                        CBT[:, esc * SEQ + nc_ * 128: esc * SEQ + (nc_ + 1) * 128],
                        pt))
        if "combt" in dbg_t:
            with blk() as b:
                @b.sync
                def _(sy):
                    W(sy, "se")
                    inc("hq", sy.dma_start(dbg_t["combt"], CBT[:, :]), 16)
                    W(sy, "hq")

        inner9.close()

        s9b = ExitStack()
        XTM = s9b.enter_context(nc.sbuf_tensor("XTM", [128, 16 * 1024], BF16))
        SIT = s9b.enter_context(nc.sbuf_tensor("SIT", [128, 8 * 256], BF16))
        with blk() as b:
            pe0 = cnt["pe"]
            se0 = cnt["se"]

            @b.tensor
            def _(te):
                for i in range(128):
                    dc, nc_ = i // 16, i % 16
                    pt = (pA if i % 2 == 0 else pB)[:, 0:64].bitcast(BF16)
                    if i >= 2:
                        W(te, "se", se0 + i - 1)
                    inc("pe", te.transpose(
                        pt, XN[:, dc * SEQ + nc_ * 128: dc * SEQ + (nc_ + 1) * 128],
                        idb[:, :]))

            @b.scalar
            def _(sc):
                for i in range(128):
                    dc, nc_ = i // 16, i % 16
                    W(sc, "pe", pe0 + i + 1)
                    pt = (pA if i % 2 == 0 else pB)[:, 0:64].bitcast(BF16)
                    inc("se", sc.copy(
                        XTM[:, nc_ * 1024 + dc * 128: nc_ * 1024 + (dc + 1) * 128],
                        pt))

        with blk() as b:
            pe0 = cnt["pe"]
            se0 = cnt["se"]

            @b.tensor
            def _(te):
                for dc in range(NCH):
                    bank = pb4[dc % 4]
                    if dc >= 4:
                        W(te, "se", se0 + dc - 3)
                    for nc_ in range(16):
                        mm = te.matmul(
                            bank[0:128, 0:256],
                            XTM[:, nc_ * 1024 + dc * 128: nc_ * 1024 + (dc + 1) * 128],
                            DTM[:, nc_ * 256:(nc_ + 1) * 256],
                            start=(nc_ == 0), stop=(nc_ == 15))
                    inc("pe", mm)

            @b.scalar
            def _(sc):
                for dc in range(NCH):
                    W(sc, "pe", pe0 + dc + 1)
                    inc("se", sc.copy(SIT[:, dc * 256:(dc + 1) * 256],
                                      pb4[dc % 4][0:128, 0:256]))
        if "sit" in dbg_t:
            with blk() as b:
                @b.sync
                def _(sy):
                    W(sy, "se")
                    inc("hq", sy.dma_start(dbg_t["sit"], SIT[:, :]), 16)
                    W(sy, "hq")

        if stop_after == "moe1":
            _finish()
            s9b.close()
            m1ctx.close()
            return nc

        s9c = ExitStack()
        HT = s9c.enter_context(nc.sbuf_tensor("HT", [128, 2 * 256], BF16))
        SOT = s9c.enter_context(nc.sbuf_tensor("SOT", [128, 8 * 256], BF16))
        STM = s9c.enter_context(nc.sbuf_tensor("STM", [128, 2 * 1024], BF16))
        with blk() as b:
            pe0 = cnt["pe"]
            se0 = cnt["se"]
            g0 = cnt["gq"]

            e_ld = {}

            @b.gpsimd
            def _(gp):
                for e in range(16):
                    slot = e % 2
                    e_ld[e] = ws_load(gp, "gq", ew1_img[e], 2048, slot)
                    ws_free[slot] = pe0 + (e + 1) * 2

            @b.tensor
            def _(te):
                for e in range(16):
                    slot = e % 2
                    te.wait_ge(sem[e_ld[e][0]], e_ld[e][1])
                    if e >= 1:
                        W(te, "se", se0 + (e - 1) * 2 + 2)
                    for ih in range(2):
                        for dc in range(NCH):
                            mm = te.matmul(
                                pC[0:128, ih * 16:(ih + 1) * 16],
                                WS[:, slot * 4096 + dc * 256 + ih * 128: slot * 4096 + dc * 256 + (ih + 1) * 128],
                                SIT[:, dc * 256 + e * 16: dc * 256 + (e + 1) * 16],
                                start=(dc == 0), stop=(dc == NCH - 1))
                        inc("pe", mm)

            @b.scalar
            def _(sc):
                for e in range(16):
                    W(sc, "pe", pe0 + e * 2 + 2)
                    for ih in range(2):
                        inc("se", sc.activation(
                            HT[:, ih * 256 + e * 16: ih * 256 + (e + 1) * 16],
                            pC[0:128, ih * 16:(ih + 1) * 16], AF.Gelu,
                            bias=b1t[:, e * 2 + ih: e * 2 + ih + 1]))

        if stop_after == "moe2a":
            _finish()
            s9c.close()
            s9b.close()
            m1ctx.close()
            return nc

        with blk() as b:
            pe0 = cnt["pe"]
            se0 = cnt["se"]
            g0 = cnt["gq"]

            w2_ld = {}

            @b.gpsimd
            def _(gp):
                for dc in range(8):
                    slot = dc % 2
                    w2_ld[dc] = ws_load(gp, "gq", ew2_img[dc], 4096, slot)
                    ws_free[slot] = pe0 + (dc + 1) * 16

            @b.tensor
            def _(te):
                for dc in range(8):
                    slot = dc % 2
                    te.wait_ge(sem[w2_ld[dc][0]], w2_ld[dc][1])
                    bank = pC if dc % 2 == 0 else pD
                    if dc >= 2:
                        W(te, "se", se0 + (dc - 2) * 16 + 16)
                    for e in range(16):
                        for hh in range(2):
                            mm = te.matmul(
                                bank[0:128, e * 16:(e + 1) * 16],
                                WS[:, slot * 4096 + (e * 2 + hh) * 128: slot * 4096 + (e * 2 + hh + 1) * 128],
                                HT[:, hh * 256 + e * 16: hh * 256 + (e + 1) * 16],
                                start=(hh == 0), stop=(hh == 1))
                        inc("pe", mm)

            @b.vector
            def _(v):
                for dc in range(8):
                    bank = pC if dc % 2 == 0 else pD
                    W(v, "pe", pe0 + (dc + 1) * 16)
                    for e in range(16):
                        inc("se", v.tensor_scalar_add(
                            SOT[:, dc * 256 + e * 16: dc * 256 + (e + 1) * 16],
                            bank[0:128, e * 16:(e + 1) * 16],
                            b2t[:, e * 8 + dc: e * 8 + dc + 1]))
        if "sot" in dbg_t:
            with blk() as b:
                @b.sync
                def _(sy):
                    W(sy, "se")
                    inc("hq", sy.dma_start(dbg_t["sot"], SOT[:, :]), 16)
                    W(sy, "hq")

        if stop_after == "moe2b":
            _finish()
            s9c.close()
            s9b.close()
            m1ctx.close()
            return nc

        with blk() as b:
            pe0 = cnt["pe"]
            se0 = cnt["se"]

            @b.tensor
            def _(te):
                W(te, "se")
                for i in range(16):
                    dc, esc = i // 2, i % 2
                    pt = (pA if i % 2 == 0 else pB)[:, 0:64].bitcast(BF16)
                    if i >= 2:
                        W(te, "se", se0 + i - 1)
                    inc("pe", te.transpose(
                        pt, SOT[:, dc * 256 + esc * 128: dc * 256 + (esc + 1) * 128],
                        idb[:, :]))

            @b.scalar
            def _(sc):
                for i in range(16):
                    dc, esc = i // 2, i % 2
                    W(sc, "pe", pe0 + i + 1)
                    pt = (pA if i % 2 == 0 else pB)[:, 0:64].bitcast(BF16)
                    inc("se", sc.copy(
                        STM[:, esc * 1024 + dc * 128: esc * 1024 + (dc + 1) * 128],
                        pt))

        if stop_after == "moe2c":
            _finish()
            s9c.close()
            s9b.close()
            m1ctx.close()
            return nc

        with blk() as b:
            pe0 = cnt["pe"]
            ve0 = cnt["ve"]

            @b.tensor
            def _(te):
                W(te, "se")
                i = 0
                for dc in range(8):
                    for st in range(4):
                        bank = pb4[i % 4]
                        if i >= 4:
                            W(te, "ve", ve0 + i - 3)
                        for esc in range(2):
                            mm = te.matmul(
                                bank[0:128, 0:512],
                                STM[:, esc * 1024 + dc * 128: esc * 1024 + (dc + 1) * 128],
                                CBT[:, esc * SEQ + st * 512: esc * SEQ + (st + 1) * 512],
                                start=(esc == 0), stop=(esc == 1))
                        inc("pe", mm)
                        i += 1

            @b.vector
            def _(v):
                i = 0
                for dc in range(8):
                    for st in range(4):
                        W(v, "pe", pe0 + i + 1)
                        dstc = slice(dc * SEQ + st * 512, dc * SEQ + (st + 1) * 512)
                        inc("ve", v.tensor_add(S[:, dstc], S[:, dstc],
                                               pb4[i % 4][0:128, 0:512]))
                        i += 1

            @b.sync
            def _(sy):
                W(sy, "ve")
                inc("hq", sy.dma_start(outT, S[:, :]), 16)
                W(sy, "hq")
        s9c.close()
        s9b.close()
        m1ctx.close()
        _check_waits()
    return nc


def kernel(**inputs):
    g = _prep(inputs)
    nc = build()
    common = {k: v for k, v in g.items() if k != "x_img"}
    in_maps = []
    for b in range(8):
        m = dict(common)
        m["x_img"] = g["x_img"][b]
        in_maps.append(m)
    res = run_bass_kernel_spmd(nc, in_maps, core_ids=list(range(8)))
    outs = []
    for b in range(8):
        img = np.asarray(res.results[b]["outT"], dtype=np.float32)
        outs.append(img.reshape(128, NCH, SEQ).transpose(1, 0, 2)
                    .reshape(D, SEQ).T)
    return np.stack(outs, 0).astype(np.float32)


# version 22 (final)
# speedup vs baseline: 1.0033x; 1.0033x over previous
import numpy as np
import ml_dtypes
from contextlib import ExitStack

import concourse.bass as bass
import concourse.mybir as mybir
from concourse.bass_utils import run_bass_kernel_spmd

F32 = mybir.dt.float32
BF16 = mybir.dt.bfloat16
AF = mybir.ActivationFunctionType
ALU = mybir.AluOpType
AX = mybir.AxisListType
BF = ml_dtypes.bfloat16

D = 1024
SEQ = 2048
NCH = 8
NH = 16
DK = 64
DH = 4096
HCH = 32
NQS = 4
NKC = 16
LN_EPS = 1e-5


def _img_chunks(a, p=128):
    c = a.shape[0] // p
    return np.ascontiguousarray(
        a.reshape(c, p, a.shape[1]).transpose(1, 0, 2).reshape(p, c * a.shape[1]))


def _wimg(wT, oc_size):
    K, O = wT.shape
    kc, oc = K // 128, O // oc_size
    b = wT.reshape(kc, 128, oc, oc_size).transpose(2, 1, 0, 3)
    return np.ascontiguousarray(b.reshape(oc, 128, kc * oc_size))


def _prep(inputs):
    f = np.float32
    g = {}
    x = np.asarray(inputs["L0_attn_x"], dtype=f)
    g["x_img"] = [np.ascontiguousarray(
        x[b].T.reshape(NCH, 128, SEQ).transpose(1, 0, 2).reshape(128, NCH * SEQ))
        for b in range(8)]

    g["wq_img"] = _wimg(np.ascontiguousarray(np.asarray(inputs["L0_attn_W_Q"], f).T), 128)
    g["wk_img"] = _wimg(np.ascontiguousarray(np.asarray(inputs["L0_attn_W_K"], f).T), 128)
    g["wv_img"] = _img_chunks(np.ascontiguousarray(np.asarray(inputs["L0_attn_W_V"], f).T))
    g["wo_img"] = _wimg(np.ascontiguousarray(np.asarray(inputs["L0_attn_W_O"], f).T), 128)

    for nm, tag in [("L0_ffn_w1_w", "w1"), ("L0_ffn_w2_w", "w2"),
                    ("L0_ffn_h1_w", "h1"), ("L0_ffn_h2_w", "h2"),
                    ("L0_ffn_w3_w", "w3")]:
        wT = np.ascontiguousarray(np.asarray(inputs[nm], f).T).astype(BF)
        g[tag + "_img"] = _wimg(wT, 128)

    def scl(name, c):
        return np.ascontiguousarray(np.asarray(inputs[name], f).reshape(c, 128).T)
    g["s1"] = scl("L0_ffn_w1_s", 32)
    g["s2"] = scl("L0_ffn_w2_s", 32)
    g["sh1"] = scl("L0_ffn_h1_s", 32)
    g["sh2"] = scl("L0_ffn_h2_s", 32)
    g["s3"] = scl("L0_ffn_w3_s", 8)
    g["l1gamma"] = scl("L1_attn_gamma", 8)
    g["qkva_img"] = _wimg(
        np.ascontiguousarray(np.asarray(inputs["L1_attn_w_qkva"], f).T), 128)
    g["l1g"] = scl("L1_ffn_ln_gamma", 8)
    g["l1b"] = scl("L1_ffn_ln_beta", 8)
    g["mu_img"] = _img_chunks(np.asarray(inputs["L1_ffn_mu"], f).reshape(D, 256))
    g["l1scale"] = np.asarray(inputs["L1_ffn_scale"], f).reshape(1, 1)
    w1 = np.asarray(inputs["L1_ffn_w1"], f)
    g["ew1_img"] = np.ascontiguousarray(
        w1.reshape(16, 8, 128, 256).transpose(0, 2, 1, 3).reshape(16, 128, 2048))
    w2 = np.asarray(inputs["L1_ffn_w2"], f)
    g["ew2_img"] = np.ascontiguousarray(
        w2.reshape(16, 2, 128, 8, 128).transpose(3, 2, 0, 1, 4).reshape(8, 128, 4096))
    g["b1t"] = np.ascontiguousarray(
        np.asarray(inputs["L1_ffn_b1"], f).reshape(16, 2, 128)
        .transpose(2, 0, 1).reshape(128, 32))
    g["b2t"] = np.ascontiguousarray(
        np.asarray(inputs["L1_ffn_b2"], f).reshape(16, 8, 128)
        .transpose(2, 0, 1).reshape(128, 128))

    theta = 1.0 / 10000.0 ** (np.arange(0, DK, 2, dtype=np.float64) / DK)
    ang = np.outer(np.arange(SEQ, dtype=np.float64), theta)
    cosT = np.cos(ang).T.astype(f)
    sinT = np.sin(ang).T.astype(f)
    c64 = np.concatenate([cosT, cosT], 0)
    sf64 = np.concatenate([-sinT, sinT], 0)
    g["cos2"] = np.ascontiguousarray(np.concatenate([c64, c64], 0)).astype(BF)
    g["sflip2"] = np.ascontiguousarray(np.concatenate([sf64, sf64], 0)).astype(BF)
    m = np.zeros((128, 128), dtype=BF)
    for p in range(128):
        m[p, p:] = 1
    g["mask"] = m
    g["identb"] = np.eye(128, dtype=BF)
    onesq = np.zeros((128, 33), dtype=BF)
    onesq[0:64, 0] = 1
    onesq[64:128, 32] = 1
    g["onesq"] = onesq
    oa = np.zeros((128, 128), dtype=BF)
    oa[0:64, :] = 1
    g["ones64a"] = oa
    ob = np.zeros((128, 128), dtype=BF)
    ob[64:128, :] = 1
    g["ones64b"] = ob
    g["onescol"] = np.ones((128, 1), dtype=BF)
    g["onesrow"] = np.ones((1, 128), dtype=f)
    epsc = np.zeros((128, 4), dtype=f)
    epsc[:, 0] = LN_EPS
    epsc[:, 1] = 1e-12
    epsc[:, 2] = LN_EPS * LN_EPS
    g["epsc"] = epsc
    return g


def build(dbg=(), stop_after=None, check=False):
    nc = bass.Bass("TRN2", target_bir_lowering=False)
    dt = nc.dram_tensor

    def din(name, shape, dtype):
        return dt(name, shape, dtype, kind="ExternalInput").ap()

    x_img = din("x_img", [128, NCH * SEQ], F32)
    wq_img = din("wq_img", [8, 128, 1024], F32)
    wk_img = din("wk_img", [8, 128, 1024], F32)
    wv_img = din("wv_img", [128, 8192], F32)
    wo_img = din("wo_img", [8, 128, 1024], F32)
    w1_img = din("w1_img", [32, 128, 1024], BF16)
    w2_img = din("w2_img", [32, 128, 1024], BF16)
    h1_img = din("h1_img", [32, 128, 4096], BF16)
    h2_img = din("h2_img", [32, 128, 4096], BF16)
    w3_img = din("w3_img", [8, 128, 4096], BF16)
    s1_i = din("s1", [128, 32], F32)
    s2_i = din("s2", [128, 32], F32)
    sh1_i = din("sh1", [128, 32], F32)
    sh2_i = din("sh2", [128, 32], F32)
    s3_i = din("s3", [128, 8], F32)
    l1gamma_i = din("l1gamma", [128, 8], F32)
    qkva_img = din("qkva_img", [24, 128, 1024], F32)
    l1g_i = din("l1g", [128, 8], F32)
    l1b_i = din("l1b", [128, 8], F32)
    mu_img = din("mu_img", [128, 2048], F32)
    l1scale_i = din("l1scale", [1, 1], F32)
    ew1_img = din("ew1_img", [16, 128, 2048], F32)
    ew2_img = din("ew2_img", [8, 128, 4096], F32)
    b1t_i = din("b1t", [128, 32], F32)
    b2t_i = din("b2t", [128, 128], F32)
    cos2_i = din("cos2", [128, 2048], BF16)
    sflip2_i = din("sflip2", [128, 2048], BF16)
    mask_i = din("mask", [128, 128], BF16)
    identb_i = din("identb", [128, 128], BF16)
    onesq_i = din("onesq", [128, 33], BF16)
    ones64a_i = din("ones64a", [128, 128], BF16)
    ones64b_i = din("ones64b", [128, 128], BF16)
    onescol_i = din("onescol", [128, 1], BF16)
    onesrow_i = din("onesrow", [1, 128], F32)
    epsc_i = din("epsc", [128, 4], F32)

    outT = dt("outT", [128, NCH * SEQ], F32, kind="ExternalOutput").ap()
    settle = dt("settle", [128, NCH * SEQ], F32, kind="Internal").ap()
    snaps = {}
    if "snaps" in dbg:
        for nm in ["ss1", "ss2", "ss3"]:
            snaps[nm] = dt(nm, [128, NCH * SEQ], F32, kind="ExternalOutput").ap()
    okind = "ExternalOutput" if "oT" in dbg else "Internal"
    odram = dt("odram", [8, 128, SEQ], BF16, kind=okind).ap()

    dbg_t = {}
    for name in dbg:
        if name in ("oT", "snaps"):
            continue
        shp, dtp = {
            "xn": ([128, NCH * SEQ], BF16), "x2": ([128, NCH * SEQ], F32),
            "attn": ([128, NCH * SEQ], F32), "xr": ([128, NCH * SEQ], BF16),
            "xln": ([128, NCH * SEQ], BF16), "mixS": ([128, NCH * SEQ], F32),
            "disp": ([128, 16 * 256], BF16), "combt": ([128, 2 * SEQ], BF16),
            "ctm": ([128, 16 * 256], BF16), "ltm": ([128, 16 * 256], BF16),
            "cm": ([128, 64], F32),
            "sit": ([128, 8 * 256], BF16), "sot": ([128, 8 * 256], BF16),
            "lsb": ([128, 2 * SEQ], BF16), "bh": ([128, 64], F32),
            "bt": ([1, 32], F32), "qt": ([128, SEQ], BF16),
            "vt": ([128, NKC * NH * 65], BF16),
            "qsq": ([128, SEQ], BF16), "nrm": ([128, 4096], F32),
            "c2t": ([128, SEQ], BF16), "sft": ([128, SEQ], BF16),
            "qct": ([128, SEQ], BF16), "qft": ([128, SEQ], BF16),
        }[name]
        dbg_t[name] = dt("dbg_" + name, shp, dtp, kind="ExternalOutput").ap()

    ctx = ExitStack()
    with ctx:
        sb = lambda n, s, d: ctx.enter_context(nc.sbuf_tensor(n, s, d))
        S = sb("S", [128, NCH * SEQ], F32)
        XN = sb("XN", [128, NCH * SEQ], BF16)
        WS = sb("WS", [128, 2 * 4096], BF16)
        s1 = sb("s1t", [128, 32], F32)
        s2 = sb("s2t", [128, 32], F32)
        sh1 = sb("sh1t", [128, 32], F32)
        sh2 = sb("sh2t", [128, 32], F32)
        s3 = sb("s3t", [128, 8], F32)
        l1gm = sb("l1gm", [128, 8], F32)
        l1g = sb("l1gt", [128, 8], F32)
        l1b = sb("l1bt", [128, 8], F32)
        b1t = sb("b1tt", [128, 32], F32)
        b2t = sb("b2tt", [128, 128], F32)
        mask = sb("maskt", [128, 128], BF16)
        idb = sb("idbt", [128, 128], BF16)
        onesq = sb("onesqt", [128, 33], BF16)
        ones64a = sb("ones64at", [128, 128], BF16)
        ones64b = sb("ones64bt", [128, 128], BF16)
        onescol = sb("onescolt", [128, 1], BF16)
        onesrow = sb("onesrowt", [1, 128], F32)
        l1scale = sb("l1scalet", [1, 4], F32)
        BH = sb("BHt", [128, 64], F32)
        JNK = sb("JNKt", [128, 256], F32)
        EPSC = sb("EPSCt", [128, 4], F32)
        BT = sb("BTt", [1, 32], F32)
        WVT = sb("WVT", [128, 4096], BF16)

        pA = ctx.enter_context(nc.psum_tensor("pA", [128, 512], F32))
        pB = ctx.enter_context(nc.psum_tensor("pB", [128, 512], F32))
        pC = ctx.enter_context(nc.psum_tensor("pC", [128, 512], F32))
        pD = ctx.enter_context(nc.psum_tensor("pD", [128, 512], F32))
        pW = ctx.enter_context(nc.psum_tensor("pW", [128, 2048], F32))
        pb4 = [pA, pB, pC, pD]

        sem = {}
        cnt = {}
        for name in ["gq", "hq", "pe", "ve", "se",
                     "gA", "gB", "gC", "gD", "hA", "hB", "hC",
                     "oA", "oB", "xA", "xB"]:
            sem[name] = ctx.enter_context(nc.semaphore(name))
            cnt[name] = 0

        def inc(name, instr, k=1):
            instr.then_inc(sem[name], k)
            cnt[name] += k
            return cnt[name]

        _wlog = []

        def W(eng, name, val=None):
            v = cnt[name] if val is None else val
            _wlog.append((name, v))
            eng.wait_ge(sem[name], v)

        def vflush(v):
            v.tensor_copy(JNK[:, 0:128], JNK[:, 128:256])

        def _check_waits():
            bad = [(n, v, cnt[n]) for (n, v) in _wlog if v > cnt[n]]
            if bad:
                raise RuntimeError("unsatisfiable waits: %s" % bad[:10])

        def _finish():
            with nc.Block() as b:
                @b.sync
                def _(sy):
                    W(sy, "ve")
                    W(sy, "se")
                    W(sy, "pe")
                    inc("hq", sy.dma_start(outT, S[:, :]), 16)
                    W(sy, "hq")

        ws_free = [0, 0, 0]
        _gslot = ["gA", "gB", "gC"]
        _hslot = ["hA", "hB", "hC"]

        def ws_load(eng_dma, qname, src, ncols, slot):
            if ws_free[slot] > 0:
                eng_dma.wait_ge(sem["pe"], ws_free[slot])
            sname = _gslot[slot] if qname == "gq" else _hslot[slot]
            r = eng_dma.dma_start(WS[:, slot * 4096: slot * 4096 + ncols], src)
            inc(sname, r, 16)
            return (sname, cnt[sname])

        _blkid = [0]

        def blk():
            _blkid[0] += 1
            return nc.Block("b%d" % _blkid[0], no_gpsimd_drain=True)

        with blk() as b:
            @b.sync
            def _(sy):
                inc("xA", sy.dma_start(S[:, 0:4 * SEQ], x_img[:, 0:4 * SEQ]),
                    16)
                small = [(s1[:, :], s1_i), (s2[:, :], s2_i),
                         (sh1[:, :], sh1_i), (sh2[:, :], sh2_i), (s3[:, :], s3_i),
                         (l1gm[:, :], l1gamma_i), (l1g[:, :], l1g_i),
                         (l1b[:, :], l1b_i), (b1t[:, :], b1t_i),
                         (b2t[:, :], b2t_i), (mask[:, :], mask_i),
                         (idb[:, :], identb_i), (onesq[:, :], onesq_i),
                         (ones64a[:, :], ones64a_i), (ones64b[:, :], ones64b_i),
                         (onescol[:, :], onescol_i), (onesrow[:, :], onesrow_i),
                         (l1scale[0:1, 0:1], l1scale_i), (EPSC[:, :], epsc_i)]
                for d_, s_ in small:
                    inc("hq", sy.dma_start(d_, s_), 16)
                W(sy, "hq")
                W(sy, "xA")
                inc("xB", sy.dma_start(S[:, 4 * SEQ:], x_img[:, 4 * SEQ:]),
                    16)

            @b.gpsimd
            def _(gp):
                for dc in range(NCH):
                    inc("gD", gp.dma_start(
                        WVT[:, dc * 512:(dc + 1) * 512],
                        wv_img[:, dc * 1024: dc * 1024 + 512]), 16)

            @b.vector
            def _(v):
                v.memset(JNK[:, :], 0.0)

        _lnid = [0]

        def ln_stats(src, gates=None, scratch=None):
            _lnid[0] += 1
            sctx = ExitStack()
            if scratch is not None:
                XBF = scratch[:, 0:SEQ]
                XSQ = scratch[:, SEQ:2 * SEQ]
            else:
                XBF = sctx.enter_context(nc.sbuf_tensor("st_xbf%d" % _lnid[0], [128, SEQ], BF16))
                XSQ = sctx.enter_context(nc.sbuf_tensor("st_xsq%d" % _lnid[0], [128, SEQ], BF16))
            with blk() as b:
                se0 = cnt["se"]
                ve0 = cnt["ve"]
                pe0 = cnt["pe"]

                @b.scalar
                def _(sc):
                    for c in range(NCH):
                        if gates is not None:
                            sc.wait_ge(sem[gates[c][0]], gates[c][1])
                        if c >= 1:
                            W(sc, "pe", pe0 + c)
                        inc("se", sc.copy(XBF[:, :],
                                          src[:, c * SEQ:(c + 1) * SEQ]))

                @b.vector
                def _(v):
                    for c in range(NCH):
                        if gates is not None:
                            v.wait_ge(sem[gates[c][0]], gates[c][1])
                        if c >= 1:
                            W(v, "pe", pe0 + c)
                        inc("ve", v.tensor_mul(
                            XSQ[:, :], src[:, c * SEQ:(c + 1) * SEQ],
                            src[:, c * SEQ:(c + 1) * SEQ]))

                @b.tensor
                def _(te):
                    for c in range(NCH):
                        W(te, "se", se0 + c + 1)
                        W(te, "ve", ve0 + c + 1)
                        for st in range(4):
                            cl = slice(st * 512, st * 512 + 512)
                            te.matmul(pW[0:1, cl], onescol[:, :], XBF[:, cl],
                                      start=(c == 0), stop=(c == NCH - 1))
                            mm = te.matmul(pW[32:33, cl], onescol[:, :], XSQ[:, cl],
                                           start=(c == 0), stop=(c == NCH - 1))
                            if st == 3:
                                inc("pe", mm)
            sctx.close()

        def bcast_rows(rows_tiles):
            with blk() as b:
                pe0 = cnt["pe"]
                ve0 = cnt["ve"]

                @b.tensor
                def _(te):
                    W(te, "ve")
                    for i, (row, _t) in enumerate(rows_tiles):
                        if i >= 1:
                            W(te, "ve", ve0 + i * 4)
                        for st in range(4):
                            inc("pe", te.matmul(
                                pb4[st][0:128, 0:512], onesrow[0:1, :],
                                row[0:1, st * 512:(st + 1) * 512],
                                start=True, stop=True))

                @b.vector
                def _(v):
                    for i, (row, tile) in enumerate(rows_tiles):
                        for st in range(4):
                            W(v, "pe", pe0 + i * 4 + st + 1)
                            inc("ve", v.tensor_copy(
                                tile[:, st * 512:(st + 1) * 512],
                                pb4[st][0:128, 0:512]))

        s1ctx = ExitStack()
        BCA = s1ctx.enter_context(nc.sbuf_tensor("BCA", [128, SEQ], F32))
        BCB = s1ctx.enter_context(nc.sbuf_tensor("BCB", [128, SEQ], F32))
        RW = s1ctx.enter_context(nc.sbuf_tensor("RW", [1, 6144], F32))
        A_ = RW[0:1, 0:2048]
        B_ = RW[0:1, 2048:4096]
        C_ = RW[0:1, 4096:6144]

        ln_stats(S, gates=[("xA", cnt["xA"])] * 4 + [("xB", cnt["xB"])] * 4)
        with blk() as b:
            @b.vector
            def _(v):
                v.tensor_scalar_mul(A_, pW[0:1, :], 1.0 / D)
                v.tensor_scalar_mul(B_, pW[32:33, :], 1.0 / D)
                vflush(v)
                v.tensor_mul(C_, A_, A_)
                vflush(v)
                inc("ve", v.tensor_sub(B_, B_, C_))
        with blk() as b:
            @b.scalar
            def _(sc):
                inc("se", sc.activation(C_, B_, AF.Sqrt, bias=EPSC[0:1, 0:1]))
        with blk() as b:
            @b.vector
            def _(v):
                v.reciprocal(B_, C_)
                vflush(v)
                inc("ve", v.scalar_tensor_tensor(A_, A_, -1.0, B_,
                                                 ALU.mult, ALU.mult))
        bcast_rows([(B_, BCA), (A_, BCB)])
        with blk() as b:
            @b.vector
            def _(v):
                for c in range(NCH):
                    cs = slice(c * SEQ, (c + 1) * SEQ)
                    v.tensor_mul(pW[:, :], S[:, cs], BCA[:, :])
                    inc("ve", v.tensor_add(XN[:, cs], pW[:, :], BCB[:, :]))

            if "xn" in dbg_t:
                @b.sync
                def _(sy):
                    W(sy, "ve")
                    inc("hq", sy.dma_start(dbg_t["xn"], XN[:, :]), 16)
                    W(sy, "hq")
        s1ctx.close()
        if stop_after == "ln0":
            _finish()
            return nc

        actx = ExitStack()
        Vt = actx.enter_context(nc.sbuf_tensor("Vt", [128, NKC * NH * 65], BF16))
        C2 = actx.enter_context(nc.sbuf_tensor("C2", [128, SEQ], BF16))
        SFt = actx.enter_context(nc.sbuf_tensor("SFt", [128, SEQ], BF16))
        QT = actx.enter_context(nc.sbuf_tensor("QT", [128, 2 * SEQ], BF16))
        KT = actx.enter_context(nc.sbuf_tensor("KT", [128, SEQ], BF16))
        QC = actx.enter_context(nc.sbuf_tensor("QC", [128, SEQ], BF16))
        QF = actx.enter_context(nc.sbuf_tensor("QF", [128, SEQ], BF16))
        PT = actx.enter_context(nc.sbuf_tensor("PT", [128, 4 * 512], BF16))
        QSQ = actx.enter_context(nc.sbuf_tensor("QSQ", [128, 2 * SEQ], BF16))
        OSB = actx.enter_context(nc.sbuf_tensor("OSB", [128, 2 * 512], BF16))
        RBt = actx.enter_context(nc.sbuf_tensor("RBt", [128, 512], F32))
        RRt = actx.enter_context(nc.sbuf_tensor("RRt", [1, 1024], F32))
        vtr = Vt[:, :].rearrange("p (k h c) -> p k h c", k=NKC, h=NH)

        slot_q = {hp: 0 for hp in range(NCH)}
        slot_k = {hp: 1 for hp in range(NCH)}
        gq_load = {}
        osb_free = [0, 0]

        with blk() as b:
            gd_h0 = cnt["gD"]
            pe0 = cnt["pe"]
            se0 = cnt["se"]

            @b.sync
            def _(sy):
                inc("hq", sy.dma_start(C2[:, :], cos2_i), 16)
                inc("hq", sy.dma_start(SFt[:, :], sflip2_i), 16)

            @b.vector
            def _(v):
                inc("ve", v.memset(vtr[:, :, :, 64:65], 1.0))

            @b.gpsimd
            def _(gp):
                gp.wait_ge(sem["pe"], pe0 + 16)
                for dc in range(NCH):
                    inc("gD", gp.dma_start(
                        WVT[:, dc * 512:(dc + 1) * 512],
                        wv_img[:, dc * 1024 + 512: dc * 1024 + 1024]), 16)
            gd_h1 = cnt["gD"]

            @b.tensor
            def _(te):
                for st2 in range(2):
                    W(te, "gD", gd_h0 if st2 == 0 else gd_h1)
                    for nc_ in range(NKC):
                        i = st2 * 16 + nc_
                        if i >= 4:
                            W(te, "se", se0 + i - 3)
                        bank = pb4[i % 4]
                        for dc in range(NCH):
                            mm = te.matmul(
                                bank[0:128, 0:512],
                                XN[:, dc * SEQ + nc_ * 128: dc * SEQ + nc_ * 128 + 128],
                                WVT[:, dc * 512:(dc + 1) * 512],
                                start=(dc == 0), stop=(dc == NCH - 1))
                        inc("pe", mm)

            @b.scalar
            def _(sc):
                for st2 in range(2):
                    for nc_ in range(NKC):
                        i = st2 * 16 + nc_
                        W(sc, "pe", pe0 + i + 1)
                        src = pb4[i % 4][0:128, 0:512].rearrange(
                            "p (h c) -> p h c", h=8)
                        inc("se", sc.copy(
                            vtr[:, nc_, st2 * 8:(st2 + 1) * 8, 0:64], src))

        proj_done = {}


        def emit_bound_block(nhp, par):
            with blk() as b:
                pe0 = cnt["pe"]
                ve0 = cnt["ve"]

                @b.tensor
                def _(te):
                    for ti in range(2):
                        if ti == 1:
                            W(te, "ve", ve0 + 1)
                        qb = (par if ti == 0 else 1 - par) * SEQ
                        for st in range(4):
                            cl = slice(qb + st * 512, qb + (st + 1) * 512)
                            te.matmul(pW[0:128, st * 512:(st + 1) * 512],
                                      ones64a[:, :], QSQ[:, cl],
                                      start=True, stop=True)
                            mm = te.matmul(pb4[st][0:128, 0:512],
                                           ones64b[:, :], QSQ[:, cl],
                                           start=True, stop=True)
                        inc("pe", mm)

                @b.vector
                def _(v):
                    v.tensor_copy(QF[0:32, :], QC[32:64, :])
                    v.tensor_copy(QF[32:64, :], QC[0:32, :])
                    v.tensor_copy(QF[64:96, :], QC[96:128, :])
                    v.tensor_copy(QF[96:128, :], QC[64:96, :])
                    v.tensor_mul(KT[:, :], QC[:, :], C2[:, :])
                    v.tensor_mul(QF[:, :], QF[:, :], SFt[:, :])
                    v.tensor_add(KT[:, :], KT[:, :], QF[:, :])
                    for ti in range(2):
                        W(v, "pe", pe0 + ti + 1)
                        v.reduce_max(BH[:, 8 + ti * 4:8 + ti * 4 + 1],
                                     pW[:, :], axis=AX.X)
                        for st in range(4):
                            v.reduce_max(BH[:, 16 + st:16 + st + 1],
                                         pb4[st][0:128, 0:512], axis=AX.X)
                        vflush(v)
                        v.tensor_max(BH[:, 24:25], BH[:, 16:17], BH[:, 17:18])
                        v.tensor_max(BH[:, 25:26], BH[:, 18:19], BH[:, 19:20])
                        vflush(v)
                        v.tensor_max(BH[:, 8 + ti * 4 + 1:8 + ti * 4 + 2],
                                     BH[:, 24:25], BH[:, 25:26])
                        inc("ve", v.tensor_copy(JNK[:, 128:256], JNK[:, 0:128]))
                    vflush(v)
                    v.tensor_mul(BH[:, 26:27], BH[:, 8:9], BH[:, 12:13])
                    v.tensor_mul(BH[:, 27:28], BH[:, 9:10], BH[:, 13:14])
                    vflush(v)
                    inc("ve", v.tensor_tensor(BH[:, 28 + par:29 + par],
                                              BH[:, 26:27], BH[:, 27:28],
                                              ALU.max))

        for hp in range(-1, NCH):
            nhp = hp + 1
            par = nhp % 2
            cpar = hp % 2
            do_proj = nhp < NCH
            do_scores = hp >= 0
            h0, h1 = 2 * hp, 2 * hp + 1

            with blk() as b:
                if do_scores:
                    sched = [(qs, ki) for qs in range(NQS)
                             for ki in range(4 * qs + 4)]
                    nt_ = len(sched)
                else:
                    sched = []
                    nt_ = 0
                events = []
                for idx in range(nt_):
                    events.append(("S", idx))
                    if idx >= 1:
                        j = idx - 1
                        events.append(("P", j))
                        if sched[j][1] == 4 * sched[j][0] + 3:
                            events.append(("R", sched[j][0]))
                if nt_:
                    events.append(("P", nt_ - 1))
                    events.append(("R", NQS - 1))
                if do_proj:
                    jev = [("J", k) for k in range(8)]
                    if do_scores:
                        merged = []
                        nxt = 4
                        for i, e in enumerate(events):
                            merged.append(e)
                            if i + 1 == nxt and jev:
                                merged.append(jev.pop(0))
                                nxt += 9
                        merged.extend(jev)
                        events = merged
                    else:
                        events = jev
                pe0 = cnt["pe"]
                se0 = cnt["se"]
                ve0 = cnt["ve"]
                pe_scores = {}
                pe_pv = {}
                pe_rb = {}
                pe_strip = {}
                p = pe0
                for e in events:
                    p += 1
                    if e[0] == "S":
                        pe_scores[e[1]] = p
                    elif e[0] == "P":
                        pe_pv[e[1]] = p
                    elif e[0] == "R":
                        pe_rb[e[1]] = p
                    else:
                        pe_strip[e[1]] = p
                se_exp = {}
                se_ocp = {}
                se_qc = {}
                s_ = se0 + (1 if do_scores else 0)
                qcpos = {}
                qc_nexp = {}
                if do_proj and do_scores:
                    scnt, maxp = 0, -1
                    for e in events:
                        if e[0] == "S":
                            scnt += 1
                        elif e[0] == "P":
                            maxp = max(maxp, e[1])
                        elif e[0] == "J":
                            k = e[1]
                            nx = max(scnt - 1, maxp + 1, 1)
                            if k >= 4:
                                nx = max(nx, qc_nexp[3] + 1)
                            nx = min(nx, nt_)
                            qc_nexp[k] = nx
                            qcpos.setdefault(nx, []).append(k)
                se_lcp = {}
                se_rbt = {}
                sjobs = {}
                nexp = 0
                for idx, (qs, ki) in enumerate(sched):
                    nexp += 1
                    qcs = list(qcpos.get(nexp, ()))
                    if ki == 4 * qs + 3:
                        jl = [("qc", k) for k in qcs
                              if pe_strip[k] < pe_rb[qs]]
                        jl += [("lcp", qs), ("ocp", qs), ("rbt", qs)]
                        jl += [("qc", k) for k in qcs
                               if pe_strip[k] > pe_rb[qs]]
                    else:
                        jl = [("qc", k) for k in qcs]
                    sjobs[idx] = jl
                sd = {"qc": se_qc, "lcp": se_lcp, "ocp": se_ocp,
                      "rbt": se_rbt}
                for idx, (qs, ki) in enumerate(sched):
                    s_ += 1
                    se_exp[idx] = s_
                    for j in sjobs[idx]:
                        s_ += 1
                        sd[j[0]][j[1]] = s_
                if do_proj:
                    for k in range(8):
                        if k not in se_qc:
                            s_ += 1
                            se_qc[k] = s_
                bankw = {}
                jwait = {}
                rwait = {}
                for e in events:
                    if e[0] == "J":
                        lw = bankw.get(e[1] % 2)
                        if lw is not None:
                            jwait[e[1]] = lw
                        bankw[e[1] % 2] = e
                    elif e[0] == "R":
                        lw = bankw.get(e[1] % 2)
                        if lw is not None:
                            rwait[e[1]] = lw
                        bankw[e[1] % 2] = e

                ve_mask = {}
                ve_nrm = {}
                ve_sq = {}
                ve_ropeq = None
                vjobs = {}
                if do_proj and do_scores:
                    for k in range(8):
                        pos = min(qc_nexp[k], nt_ - 1)
                        vjobs.setdefault(pos, []).append(("sq", k))
                        if k == 3:
                            vjobs.setdefault(pos, []).append(("ropeq",))
                if do_scores:
                    for idx, (qs, ki) in enumerate(sched):
                        if ki == 4 * qs + 3:
                            for i in range(4):
                                vjobs.setdefault(idx + 1 + i, []).append(
                                    ("cl", qs, i))
                v_ = ve0
                vposs = sorted(set(range(nt_)) | set(vjobs.keys()))
                for idx in vposs:
                    if idx < nt_:
                        qs, ki = sched[idx]
                        if ki >= 4 * qs:
                            v_ += 1
                        ve_mask[idx] = v_
                    for job in vjobs.get(idx, []):
                        if job[0] == "sq":
                            v_ += 1
                            ve_sq[job[1]] = v_
                        elif job[0] == "ropeq":
                            v_ += 1
                            ve_ropeq = v_
                        elif job[2] == 3:
                            v_ += 1
                            ve_nrm[job[1]] = v_
                if not do_scores and do_proj:
                    for k in range(8):
                        v_ += 1
                        ve_sq[k] = v_
                        if k == 3:
                            v_ += 1
                            ve_ropeq = v_

                if do_proj:
                    @b.gpsimd
                    def _(gp):
                        if nhp >= 1:
                            ws_free[slot_q[nhp]] = proj_done[(nhp - 1, 0)]
                            ws_free[slot_k[nhp]] = proj_done[(nhp - 1, 1)]
                        gq_load[nhp] = [
                            ws_load(gp, "gq", wq_img[nhp], 1024, slot_q[nhp]),
                            ws_load(gp, "gq", wk_img[nhp], 1024, slot_k[nhp])]
                    proj_done[(nhp, 0)] = pe_strip[3]
                    proj_done[(nhp, 1)] = pe_strip[7]

                @b.tensor
                def _(te):
                    for e in events:
                        if e[0] == "S":
                            idx = e[1]
                            qs, ki = sched[idx]
                            pr = idx % 2
                            cl = slice(cpar * SEQ + qs * 512,
                                       cpar * SEQ + (qs + 1) * 512)
                            kb = slice(ki * 128, (ki + 1) * 128)
                            if idx >= 2:
                                W(te, "se", se_exp[idx - 2])
                            te.matmul(pW[:, pr * 1024:pr * 1024 + 512],
                                      KT[0:64, kb], QT[0:64, cl],
                                      start=True, stop=True)
                            inc("pe", te.matmul(
                                pW[:, pr * 1024 + 512:pr * 1024 + 1024],
                                KT[64:128, kb], QT[64:128, cl],
                                start=True, stop=True))
                        elif e[0] == "P":
                            j = e[1]
                            qs, ki = sched[j]
                            sb_ = j % 2
                            W(te, "se", se_exp[j])
                            W(te, "ve", ve_mask[j])
                            if ki == 0 and qs > 0:
                                W(te, "se", se_ocp[qs - 1])
                            pt0 = PT[:, sb_ * 1024: sb_ * 1024 + 512]
                            pt1 = PT[:, sb_ * 1024 + 512: sb_ * 1024 + 1024]
                            te.matmul(pC[0:65, 0:512],
                                      Vt[:, ki * NH * 65 + h0 * 65: ki * NH * 65 + h0 * 65 + 65],
                                      pt0, start=(ki == 0), stop=(ki == 4 * qs + 3))
                            inc("pe", te.matmul(
                                pD[0:65, 0:512],
                                Vt[:, ki * NH * 65 + h1 * 65: ki * NH * 65 + h1 * 65 + 65],
                                pt1, start=(ki == 0), stop=(ki == 4 * qs + 3)))
                        elif e[0] == "R":
                            qs = e[1]
                            W(te, "se", se_lcp[qs])
                            lw = rwait.get(qs)
                            if lw is not None:
                                W(te, "se", se_qc[lw[1]] if lw[0] == "J"
                                  else se_rbt[lw[1]])
                            pbR = pb4[qs % 2]
                            te.matmul(pbR[0:64, 0:512], onesrow[0:1, 0:64],
                                      RRt[0:1, 0:512], start=True, stop=True)
                            inc("pe", te.matmul(pbR[64:128, 0:512],
                                                onesrow[0:1, 0:64],
                                                RRt[0:1, 512:1024],
                                                start=True, stop=True))
                        else:
                            k = e[1]
                            ti, st = k // 4, k % 4
                            sn, sv = gq_load[nhp][ti]
                            te.wait_ge(sem[sn], sv)
                            lw = jwait.get(k)
                            if lw is not None:
                                W(te, "se", se_qc[lw[1]] if lw[0] == "J"
                                  else se_rbt[lw[1]])
                            slot = slot_q[nhp] if ti == 0 else slot_k[nhp]
                            bank = pb4[k % 2]
                            for dc in range(NCH):
                                mm = te.matmul(
                                    bank[0:128, 0:512],
                                    WS[:, slot * 4096 + dc * 128: slot * 4096 + dc * 128 + 128],
                                    XN[:, dc * SEQ + st * 512: dc * SEQ + (st + 1) * 512],
                                    start=(dc == 0), stop=(dc == NCH - 1))
                            inc("pe", mm)

                @b.scalar
                def _(sc):
                    if do_scores:
                        sc.copy(JNK[:, 0:64], JNK[:, 64:128])
                        sc.activation(BH[:, 30:31], BH[:, 28 + cpar:29 + cpar],
                                      AF.Sqrt, scale=1.0 / 64.0)
                        sc.copy(JNK[:, 64:128], JNK[:, 0:64])
                        inc("se", sc.activation(BH[:, 48 + cpar:49 + cpar],
                                                BH[:, 30:31], AF.Identity,
                                                scale=-1.0))
                    nexp = 0
                    for idx, (qs, ki) in enumerate(sched):
                        sb_ = idx % 2
                        W(sc, "pe", pe_scores[idx])
                        inc("se", sc.activation(
                            PT[:, sb_ * 1024:(sb_ + 1) * 1024],
                            pW[:, sb_ * 1024:(sb_ + 1) * 1024],
                            AF.Exp, bias=BH[:, 48 + cpar:49 + cpar],
                            scale=0.125))
                        for j in sjobs[idx]:
                            if j[0] == "qc":
                                k = j[1]
                                st = k % 4
                                W(sc, "pe", pe_strip[k])
                                if k >= 4:
                                    W(sc, "ve", ve_ropeq)
                                inc("se", sc.copy(
                                    QC[:, st * 512:(st + 1) * 512],
                                    pb4[k % 2][0:128, 0:512]))
                            elif j[0] == "lcp":
                                W(sc, "pe", pe_pv[idx])
                                if qs >= 1:
                                    W(sc, "pe", pe_rb[qs - 1])
                                sc.copy(RRt[0:1, 0:512], pC[64:65, 0:512])
                                inc("se", sc.copy(RRt[0:1, 512:1024],
                                                  pD[64:65, 0:512]))
                            elif j[0] == "ocp":
                                osl = OSB[:, (qs % 2) * 512:(qs % 2 + 1) * 512]
                                if osb_free[qs % 2] > 0:
                                    sc.wait_ge(
                                        sem["oA" if qs % 2 == 0 else "oB"],
                                        osb_free[qs % 2])
                                sc.copy(osl[0:64, :], pC[0:64, 0:512])
                                inc("se", sc.copy(osl[64:128, :],
                                                  pD[0:64, 0:512]))
                            else:
                                W(sc, "pe", pe_rb[qs])
                                if qs >= 1:
                                    W(sc, "ve", ve_nrm[qs - 1])
                                pbR = pb4[qs % 2]
                                sc.copy(RBt[0:64, :], pbR[0:64, 0:512])
                                inc("se", sc.copy(RBt[64:128, :],
                                                  pbR[64:128, 0:512]))
                    if do_proj and not do_scores:
                        for k in range(8):
                            ti, st = k // 4, k % 4
                            W(sc, "pe", pe_strip[k])
                            if k >= 4:
                                W(sc, "ve", ve_ropeq)
                            inc("se", sc.copy(
                                QC[:, st * 512:(st + 1) * 512],
                                pb4[k % 2][0:128, 0:512]))

                def _emit_vjob(v, job):
                    ds = slice(par * SEQ, par * SEQ + SEQ)
                    if job[0] == "cl":
                        qs_, i = job[1], job[2]
                        c0 = slice(i * 128, (i + 1) * 128)
                        ob_ = slice((qs_ % 2) * 512 + i * 128,
                                    (qs_ % 2) * 512 + (i + 1) * 128)
                        if i == 0:
                            W(v, "se", se_rbt[qs_])
                        v.reciprocal(RBt[:, c0], RBt[:, c0])
                        r = v.tensor_mul(OSB[:, ob_], OSB[:, ob_], RBt[:, c0])
                        if i == 3:
                            inc("ve", r)
                    elif job[0] == "sq":
                        k = job[1]
                        st = k % 4
                        W(v, "se", se_qc[k])
                        cl = slice(st * 512, (st + 1) * 512)
                        qb = (par if k < 4 else 1 - par) * SEQ
                        dcl = slice(qb + st * 512, qb + (st + 1) * 512)
                        inc("ve", v.tensor_mul(QSQ[:, dcl], QC[:, cl],
                                               QC[:, cl]))
                    else:
                        v.tensor_copy(QF[0:32, :], QC[32:64, :])
                        v.tensor_copy(QF[32:64, :], QC[0:32, :])
                        v.tensor_copy(QF[64:96, :], QC[96:128, :])
                        v.tensor_copy(QF[96:128, :], QC[64:96, :])
                        v.tensor_mul(QT[:, ds], QC[:, :], C2[:, :])
                        v.tensor_mul(QF[:, :], QF[:, :], SFt[:, :])
                        inc("ve", v.tensor_add(QT[:, ds], QT[:, ds], QF[:, :]))

                @b.vector
                def _(v):
                    for idx in vposs:
                        if idx < nt_:
                            qs, ki = sched[idx]
                            sb_ = idx % 2
                            m = ki - 4 * qs
                            if m >= 0:
                                W(v, "se", se_exp[idx])
                                for half in range(2):
                                    tb = sb_ * 1024 + half * 512
                                    if m >= 1:
                                        v.memset(PT[:, tb:tb + m * 128], 0.0)
                                    r = v.tensor_mul(
                                        PT[:, tb + m * 128:tb + (m + 1) * 128],
                                        PT[:, tb + m * 128:tb + (m + 1) * 128],
                                        mask[:, :])
                                inc("ve", r)
                        for job in vjobs.get(idx, []):
                            _emit_vjob(v, job)
                    if not do_scores and do_proj:
                        for k in range(8):
                            _emit_vjob(v, ("sq", k))
                            if k == 3:
                                _emit_vjob(v, ("ropeq",))

                if do_scores:
                    @b.sync
                    def _(sy):
                        for qs in range(NQS):
                            W(sy, "ve", ve_nrm[qs])
                            osl = OSB[:, (qs % 2) * 512:(qs % 2 + 1) * 512]
                            osb_free[qs % 2] = inc(
                                "oA" if qs % 2 == 0 else "oB",
                                sy.dma_start(
                                    odram[hp, :, qs * 512:(qs + 1) * 512],
                                    osl), 16)
                        W(sy, "oA")
                        W(sy, "oB")

            if do_proj:
                emit_bound_block(nhp, par)
            if stop_after == ("ahp%d" % hp):
                _finish()
                actx.close()
                _check_waits()
                return nc

        actx.close()

        woctx = ExitStack()
        WOT = woctx.enter_context(nc.sbuf_tensor("WOT", [128, 8192], BF16))
        OL = woctx.enter_context(nc.sbuf_tensor("OL", [128, 2 * 4096], BF16))
        with blk() as b:
            g0 = cnt["gq"]

            @b.gpsimd
            def _(gp):
                for oc in range(8):
                    inc("gD", gp.dma_start(WOT[:, oc * 1024:(oc + 1) * 1024],
                                           wo_img[oc]), 16)
            gd_wot = cnt["gD"]
            pe0 = cnt["pe"]
            ve0 = cnt["ve"]
            ol_cnt = {}

            @b.sync
            def _(sy):
                for ns in range(4):
                    if ns >= 2:
                        W(sy, "pe", pe0 + (ns - 1) * 8)
                    src = odram[:, :, ns * 512:(ns + 1) * 512] \
                        .rearrange("h p n -> p h n")
                    sn = _hslot[ns % 2]
                    r = sy.dma_start(
                        OL[:, (ns % 2) * 4096:(ns % 2 + 1) * 4096]
                        .rearrange("p (h n) -> p h n", h=8), src)
                    ol_cnt[ns] = (sn, inc(sn, r, 16))

            @b.tensor
            def _(te):
                W(te, "gD", gd_wot)
                for ns in range(4):
                    te.wait_ge(sem[ol_cnt[ns][0]], ol_cnt[ns][1])
                    ob = (ns % 2) * 4096
                    for oc in range(8):
                        bank = pb4[oc % 4]
                        if ns * 8 + oc >= 4:
                            W(te, "ve", ve0 + ns * 8 + oc - 3)
                        for dc in range(NCH):
                            mm = te.matmul(
                                bank[0:128, 0:512],
                                WOT[:, oc * 1024 + dc * 128: oc * 1024 + dc * 128 + 128],
                                OL[:, ob + dc * 512: ob + (dc + 1) * 512],
                                start=(dc == 0), stop=(dc == NCH - 1))
                        inc("pe", mm)

            @b.vector
            def _(v):
                for ns in range(4):
                    for oc in range(8):
                        W(v, "pe", pe0 + ns * 8 + oc + 1)
                        bank = pb4[oc % 4]
                        dstc = slice(oc * SEQ + ns * 512, oc * SEQ + (ns + 1) * 512)
                        inc("ve", v.tensor_add(S[:, dstc], S[:, dstc],
                                               bank[0:128, 0:512]))
        woctx.close()
        if snaps:
            with blk() as b:
                @b.sync
                def _(sy):
                    W(sy, "ve")
                    inc("hq", sy.dma_start(snaps["ss1"], S[:, :]), 16)
        if "attn" in dbg_t:
            with blk() as b:
                @b.sync
                def _(sy):
                    W(sy, "ve")
                    inc("hq", sy.dma_start(dbg_t["attn"], S[:, :]), 16)
                    W(sy, "hq")

        if stop_after == "attn":
            _finish()
            return nc

        fctx = ExitStack()
        HA = fctx.enter_context(nc.sbuf_tensor("HA", [128, HCH * 512], BF16))
        HB = fctx.enter_context(nc.sbuf_tensor("HB", [128, HCH * 512], BF16))
        WS3 = fctx.enter_context(nc.sbuf_tensor("WS3", [128, 4096], BF16))

        def wsap(slot, lo, hi):
            if slot < 2:
                return WS[:, slot * 4096 + lo: slot * 4096 + hi]
            return WS3[:, lo:hi]
        for nt in range(4):
            ntc = slice(nt * 512, (nt + 1) * 512)
            with blk() as b:
                pe0 = cnt["pe"]
                se0 = cnt["se"]
                ve0 = cnt["ve"]
                hq0 = cnt["hq"]
                peA1 = {o: pe0 + 2 * o + 1 for o in range(32)}
                peA2 = {o: pe0 + 2 * o + 2 for o in range(32)}
                seA = {o: se0 + o + 1 for o in range(32)}
                veA = {o: ve0 + o + 1 for o in range(32)}
                peH1 = {o: pe0 + 64 + o + 1 for o in range(32)}
                seH1 = {o: se0 + 32 + o + 1 for o in range(32)}
                peH2 = {o: pe0 + 96 + o + 1 for o in range(32)}
                seH2 = {o: se0 + 64 + o + 1 for o in range(32)}
                peW3 = {o: pe0 + 128 + o + 1 for o in range(8)}
                veW3 = {o: ve0 + 32 + o + 1 for o in range(8)}
                hqA = {}
                hqH1 = {}
                hqH2 = {}
                hqW3 = {}
                _scnt = dict(cnt)

                def _ld2(o, tab):
                    slot = o % 3
                    sn = _hslot[slot]
                    _scnt[sn] += 32
                    tab[o] = (sn, _scnt[sn])
                for o in range(32):
                    _ld2(o, hqA)
                for o in range(32):
                    _ld2(o, hqH1)
                for o in range(32):
                    _ld2(o, hqH2)

                def _ld1(o, tab):
                    slot = o % 3
                    sn = _hslot[slot]
                    _scnt[sn] += 16
                    tab[o] = (sn, _scnt[sn])
                hqA = {}
                hqH1 = {}
                hqH2 = {}
                hqW3 = {}
                _scnt = dict(cnt)
                for o in range(32):
                    sn = _hslot[o % 3]
                    _scnt[sn] += 32
                    hqA[o] = (sn, _scnt[sn])
                for o in range(32):
                    sn = _hslot[o % 3]
                    _scnt[sn] += 16
                    hqH1[o] = (sn, _scnt[sn])
                for o in range(32):
                    sn = _hslot[o % 3]
                    _scnt[sn] += 16
                    hqH2[o] = (sn, _scnt[sn])
                for o in range(8):
                    sn = _hslot[o % 3]
                    _scnt[sn] += 16
                    hqW3[o] = (sn, _scnt[sn])

                @b.sync
                def _(sy):
                    for o in range(32):
                        slot = o % 3
                        sn = _hslot[slot]
                        if ws_free[slot] > 0:
                            sy.wait_ge(sem["pe"], ws_free[slot])
                        inc(sn, sy.dma_start(
                            wsap(slot, 0, 1024), w1_img[o]), 16)
                        inc(sn, sy.dma_start(
                            wsap(slot, 1024, 2048),
                            w2_img[o]), 16)
                        ws_free[slot] = peA2[o]
                    for img, tab in [(h1_img, peH1), (h2_img, peH2)]:
                        for o in range(32):
                            slot = o % 3
                            sn = _hslot[slot]
                            sy.wait_ge(sem["pe"], ws_free[slot])
                            inc(sn, sy.dma_start(
                                wsap(slot, 0, 4096), img[o]), 16)
                            ws_free[slot] = tab[o]
                    for o in range(8):
                        slot = o % 3
                        sn = _hslot[slot]
                        sy.wait_ge(sem["pe"], ws_free[slot])
                        inc(sn, sy.dma_start(
                            wsap(slot, 0, 4096), w3_img[o]), 16)
                        ws_free[slot] = peW3[o]

                @b.tensor
                def _(te):
                    for o in range(32):
                        slot = o % 3
                        te.wait_ge(sem[hqA[o][0]], hqA[o][1])
                        if o >= 2:
                            W(te, "ve", veA[o - 2])
                        bA = pb4[2 * (o % 2)]
                        bB = pb4[2 * (o % 2) + 1]
                        for dc in range(NCH):
                            mm = te.matmul(
                                bA[0:128, 0:512],
                                wsap(slot, dc * 128, dc * 128 + 128),
                                XN[:, dc * SEQ + nt * 512: dc * SEQ + (nt + 1) * 512],
                                start=(dc == 0), stop=(dc == NCH - 1))
                        inc("pe", mm)
                        for dc in range(NCH):
                            mm = te.matmul(
                                bB[0:128, 0:512],
                                wsap(slot, 1024 + dc * 128, 1024 + dc * 128 + 128),
                                XN[:, dc * SEQ + nt * 512: dc * SEQ + (nt + 1) * 512],
                                start=(dc == 0), stop=(dc == NCH - 1))
                        inc("pe", mm)
                    for o in range(32):
                        slot = o % 3
                        te.wait_ge(sem[hqH1[o][0]], hqH1[o][1])
                        if o < 4:
                            W(te, "ve", veA[31])
                        bank = pb4[o % 4]
                        if o >= 4:
                            W(te, "se", seH1[o - 4])
                        for kc in range(HCH):
                            mm = te.matmul(
                                bank[0:128, 0:512],
                                wsap(slot, kc * 128, kc * 128 + 128),
                                HA[:, kc * 512:(kc + 1) * 512],
                                start=(kc == 0), stop=(kc == HCH - 1))
                        inc("pe", mm)
                    for o in range(32):
                        slot = o % 3
                        te.wait_ge(sem[hqH2[o][0]], hqH2[o][1])
                        if o < 4:
                            W(te, "se", seH1[31])
                        bank = pb4[o % 4]
                        if o >= 4:
                            W(te, "se", seH2[o - 4])
                        for kc in range(HCH):
                            mm = te.matmul(
                                bank[0:128, 0:512],
                                wsap(slot, kc * 128, kc * 128 + 128),
                                HB[:, kc * 512:(kc + 1) * 512],
                                start=(kc == 0), stop=(kc == HCH - 1))
                        inc("pe", mm)
                    for o in range(8):
                        slot = o % 3
                        te.wait_ge(sem[hqW3[o][0]], hqW3[o][1])
                        if o < 4:
                            W(te, "se", seH2[31])
                        bank = pb4[o % 4]
                        if o >= 4:
                            W(te, "ve", veW3[o - 4])
                        for kc in range(HCH):
                            mm = te.matmul(
                                bank[0:128, 0:512],
                                wsap(slot, kc * 128, kc * 128 + 128),
                                HA[:, kc * 512:(kc + 1) * 512],
                                start=(kc == 0), stop=(kc == HCH - 1))
                        inc("pe", mm)

                @b.scalar
                def _(sc):
                    for o in range(32):
                        W(sc, "pe", peA1[o])
                        inc("se", sc.activation(HA[:, o * 512:(o + 1) * 512],
                                                pb4[2 * (o % 2)][0:128, 0:512],
                                                AF.Silu, scale=s1[:, o:o + 1]))
                    for o in range(32):
                        W(sc, "pe", peH1[o])
                        inc("se", sc.activation(HB[:, o * 512:(o + 1) * 512],
                                                pb4[o % 4][0:128, 0:512],
                                                AF.Gelu, scale=sh1[:, o:o + 1]))
                    for o in range(32):
                        W(sc, "pe", peH2[o])
                        inc("se", sc.activation(HA[:, o * 512:(o + 1) * 512],
                                                pb4[o % 4][0:128, 0:512],
                                                AF.Gelu, scale=sh2[:, o:o + 1]))

                @b.vector
                def _(v):
                    for o in range(32):
                        W(v, "pe", peA2[o])
                        W(v, "se", seA[o])
                        hsl = HA[:, o * 512:(o + 1) * 512]
                        inc("ve", v.scalar_tensor_tensor(
                            hsl, pb4[2 * (o % 2) + 1][0:128, 0:512],
                            s2[:, o:o + 1], hsl, ALU.mult, ALU.mult))
                    for o in range(8):
                        W(v, "pe", peW3[o])
                        dstc = slice(o * SEQ + nt * 512, o * SEQ + (nt + 1) * 512)
                        inc("ve", v.scalar_tensor_tensor(
                            S[:, dstc], pb4[o % 4][0:128, 0:512],
                            s3[:, o:o + 1], S[:, dstc], ALU.mult, ALU.add))
        fctx.close()
        if snaps:
            with blk() as b:
                @b.sync
                def _(sy):
                    W(sy, "ve")
                    inc("hq", sy.dma_start(snaps["ss2"], S[:, :]), 16)
        if "x2" in dbg_t:
            with blk() as b:
                @b.sync
                def _(sy):
                    W(sy, "ve")
                    inc("hq", sy.dma_start(dbg_t["x2"], S[:, :]), 16)
                    W(sy, "hq")

        if stop_after == "ffn":
            _finish()
            return nc

        xrctx = ExitStack()
        XR = xrctx.enter_context(nc.sbuf_tensor("XR", [128, NCH * SEQ], BF16))
        s7ctx = ExitStack()
        BCA = s7ctx.enter_context(nc.sbuf_tensor("BCA7", [128, SEQ], F32))
        BCB = s7ctx.enter_context(nc.sbuf_tensor("BCB7", [128, SEQ], F32))
        s7r = ExitStack()
        RW = s7r.enter_context(nc.sbuf_tensor("RW7", [1, 8192], F32))
        A_ = RW[0:1, 0:2048]
        B_ = RW[0:1, 2048:4096]
        C_ = RW[0:1, 4096:6144]
        V_ = RW[0:1, 6144:8192]
        D_ = BCA[0:1, 0:2048]
        E_ = BCB[0:1, 0:2048]

        ln_stats(S, scratch=WVT)
        with blk() as b:
            @b.vector
            def _(v):
                v.tensor_scalar_mul(A_, pW[0:1, :], 1.0 / D)
                v.tensor_scalar_mul(B_, pW[32:33, :], 1.0 / D)
                vflush(v)
                v.tensor_mul(C_, A_, A_)
                vflush(v)
                inc("ve", v.tensor_sub(V_, B_, C_))
        with blk() as b:
            @b.scalar
            def _(sc):
                sc.activation(C_, V_, AF.Sqrt, scale=1.0 + LN_EPS,
                              bias=EPSC[0:1, 2:3])
                inc("se", sc.activation(D_, V_, AF.Sqrt, scale=float(D)))
        with blk() as b:
            @b.vector
            def _(v):
                v.reciprocal(C_, C_)
                v.tensor_scalar_max(D_, D_, 1e-18)
                vflush(v)
                v.scalar_tensor_tensor(B_, A_, -1.0, C_,
                                       ALU.mult, ALU.mult)
                v.reciprocal(D_, D_)
                vflush(v)
                v.tensor_scalar_mul(D_, D_, 32.0)
                vflush(v)
                inc("ve", v.scalar_tensor_tensor(E_, A_, -1.0, D_,
                                                 ALU.mult, ALU.mult))
        bcast_rows([(D_, BCA), (E_, BCB)])
        with blk() as b:
            @b.vector
            def _(v):
                for c in range(NCH):
                    cs = slice(c * SEQ, (c + 1) * SEQ)
                    v.tensor_mul(pW[:, :], S[:, cs], BCA[:, :])
                    inc("ve", v.tensor_add(XR[:, cs], pW[:, :], BCB[:, :]))
            if "xr" in dbg_t:
                @b.sync
                def _(sy):
                    W(sy, "ve")
                    inc("hq", sy.dma_start(dbg_t["xr"], XR[:, :]), 16)
                    W(sy, "hq")
        bcast_rows([(C_, BCA), (B_, BCB)])
        s7r.close()
        xln_in_mixer = not (stop_after == "l1n" or "xln" in dbg_t)
        if not xln_in_mixer:
            with blk() as b:
                @b.vector
                def _(v):
                    for c in range(NCH):
                        cs = slice(c * SEQ, (c + 1) * SEQ)
                        v.tensor_mul(pW[:, :], S[:, cs], BCA[:, :])
                        v.tensor_add(XN[:, cs], pW[:, :], BCB[:, :])
                        inc("ve", v.tensor_scalar(
                            XN[:, cs], XN[:, cs], l1g[:, c:c + 1],
                            l1b[:, c:c + 1], ALU.mult, ALU.add))
                if "xln" in dbg_t:
                    @b.sync
                    def _(sy):
                        W(sy, "ve")
                        inc("hq", sy.dma_start(dbg_t["xln"], XN[:, :]), 16)
                        W(sy, "hq")

        if stop_after == "l1n":
            _finish()
            s7ctx.close()
            xrctx.close()
            return nc

        mctx = ExitStack()
        KVT = mctx.enter_context(nc.sbuf_tensor("KVT", [128, SEQ], F32))
        AT = mctx.enter_context(nc.sbuf_tensor("AT", [128, SEQ], F32))
        KVS = mctx.enter_context(nc.sbuf_tensor("KVS", [128, SEQ], F32))
        MIXT = mctx.enter_context(nc.sbuf_tensor("MIXT", [128, SEQ], F32))
        with blk() as b:
            mx_prev = None
            for c in range(NCH):
                g0 = cnt["gq"]
                pe0 = cnt["pe"]
                se0 = cnt["se"]
                ve0 = cnt["ve"]
                prevc = mx_prev

                mx_ld = {}

                @b.gpsimd
                def _(gp):
                    for j, oc in enumerate([8 + c, 16 + c, c]):
                        slot = j % 2
                        if ws_free[slot] > 0:
                            gp.wait_ge(sem["pe"], ws_free[slot])
                        if j == 2:
                            gp.wait_ge(sem["pe"], pe0 + 4)
                        mx_ld[j] = ws_load(gp, "gq", qkva_img[oc], 1024, slot)

                @b.vector
                def _(v):
                    for j in range(3):
                        slot = j % 2
                        v.wait_ge(sem[mx_ld[j][0]], mx_ld[j][1])
                        for dc in range(NCH):
                            r = v.tensor_scalar_mul(
                                WS[:, slot * 4096 + dc * 128: slot * 4096 + (dc + 1) * 128],
                                WS[:, slot * 4096 + dc * 128: slot * 4096 + (dc + 1) * 128],
                                l1gm[:, dc:dc + 1])
                        inc("ve", r)

                @b.tensor
                def _(te):
                    for j, dst in enumerate(["kv", "a", "q"]):
                        slot = j % 2
                        W(te, "ve", ve0 + j + 1)
                        if j == 0 and prevc is not None:
                            W(te, "se", prevc[0])
                        if j == 2 and prevc is not None:
                            W(te, "ve", prevc[2])
                        for st in range(4):
                            bank = pb4[st]
                            if j >= 1:
                                W(te, "se", se0 + j * 4 - 4 + st + 1)
                            for dc in range(NCH):
                                out = bank[0:128, 0:512] if j < 2 else \
                                    pW[:, st * 512:(st + 1) * 512]
                                mm = te.matmul(
                                    out,
                                    WS[:, slot * 4096 + dc * 128: slot * 4096 + dc * 128 + 128],
                                    XR[:, dc * SEQ + st * 512: dc * SEQ + (st + 1) * 512],
                                    start=(dc == 0), stop=(dc == NCH - 1))
                            inc("pe", mm)
                ws_free[0] = cnt["pe"]
                ws_free[1] = cnt["pe"]

                @b.scalar
                def _(sc):
                    for st in range(4):
                        W(sc, "pe", pe0 + st + 1)
                        if st == 0 and prevc is not None:
                            W(sc, "ve", prevc[1])
                        inc("se", sc.copy(KVT[:, st * 512:(st + 1) * 512],
                                          pb4[st][0:128, 0:512]))
                    for st in range(4):
                        W(sc, "pe", pe0 + 4 + st + 1)
                        inc("se", sc.activation(AT[:, st * 512:(st + 1) * 512],
                                                pb4[st][0:128, 0:512], AF.Sigmoid))

                @b.vector
                def _(v):
                    cs = slice(c * SEQ, (c + 1) * SEQ)
                    if xln_in_mixer:
                        v.tensor_mul(MIXT[:, :], S[:, cs], BCA[:, :])
                        v.tensor_add(XN[:, cs], MIXT[:, :], BCB[:, :])
                        v.tensor_scalar(XN[:, cs], XN[:, cs],
                                        l1g[:, c:c + 1], l1b[:, c:c + 1],
                                        ALU.mult, ALU.add)
                    W(v, "se", se0 + 8)
                    inc("ve", v.tensor_tensor_scan(KVS[:, :], AT[:, :], KVT[:, :],
                                                   0.0, ALU.mult, ALU.add))
                    W(v, "pe", pe0 + 12)
                    v.tensor_mul(MIXT[:, :], pW[:, :], KVS[:, :])
                    inc("ve", v.tensor_add(S[:, cs], S[:, cs], MIXT[:, :]))
                mx_prev = (se0 + 8, ve0 + 4, ve0 + 5)
        if snaps:
            with blk() as b:
                @b.sync
                def _(sy):
                    W(sy, "ve")
                    inc("hq", sy.dma_start(snaps["ss3"], S[:, :]), 16)
        mctx.close()
        s7ctx.close()
        xrctx.close()
        if "mixS" in dbg_t:
            with blk() as b:
                @b.sync
                def _(sy):
                    W(sy, "ve")
                    inc("hq", sy.dma_start(dbg_t["mixS"], S[:, :]), 16)
                    W(sy, "hq")

        if stop_after == "mix":
            _finish()
            return nc

        m1ctx = ExitStack()
        DTM = m1ctx.enter_context(nc.sbuf_tensor("DTM", [128, 16 * 256], BF16))
        CBT = m1ctx.enter_context(nc.sbuf_tensor("CBT", [128, 2 * SEQ], BF16))
        inner9 = ExitStack()
        MUB = inner9.enter_context(nc.sbuf_tensor("MUB", [128, 2048], BF16))
        R3B = inner9.enter_context(nc.sbuf_tensor("R3B", [128, SEQ], F32))
        LSB = inner9.enter_context(nc.sbuf_tensor("LSB", [128, 2 * SEQ], BF16))
        SQS = inner9.enter_context(nc.sbuf_tensor("SQS", [128, 2048], BF16))
        SQS2 = inner9.enter_context(nc.sbuf_tensor("SQS2", [128, 2048], BF16))
        RW9 = inner9.enter_context(nc.sbuf_tensor("RW9", [1, 2304], F32))
        NMX = inner9.enter_context(nc.sbuf_tensor("NMX", [128, 32], F32))
        DSP = inner9.enter_context(nc.sbuf_tensor("DSP", [128, SEQ], BF16))
        CTM = inner9.enter_context(nc.sbuf_tensor("CTM", [128, 16 * 256], BF16))
        LTM = inner9.enter_context(nc.sbuf_tensor("LTM", [128, 16 * 256], BF16))
        CM0 = inner9.enter_context(nc.sbuf_tensor("CM0", [128, 16], F32))
        CM1 = inner9.enter_context(nc.sbuf_tensor("CM1", [128, 16], F32))
        CS0 = inner9.enter_context(nc.sbuf_tensor("CS0", [128, 16], F32))
        CS1 = inner9.enter_context(nc.sbuf_tensor("CS1", [128, 16], F32))
        r3row = RW9[0:1, 0:2048]
        murow = RW9[0:1, 2048:2304]

        with blk() as b:
            g0 = cnt["gq"]

            @b.gpsimd
            def _(gp):
                inc("gD", gp.dma_start(MUB[:, :], mu_img), 16)
            gd_mub = cnt["gD"]
            se0 = cnt["se"]
            pe0 = cnt["pe"]

            @b.scalar
            def _(sc):
                W(sc, "gD", gd_mub)
                inc("se", sc.activation(SQS[:, :], MUB[:, :], AF.Square))

            @b.tensor
            def _(te):
                W(te, "se", se0 + 1)
                for dc in range(NCH):
                    mm = te.matmul(pA[0:1, 0:256], onescol[:, :],
                                   SQS[:, dc * 256:(dc + 1) * 256],
                                   start=(dc == 0), stop=(dc == NCH - 1))
                inc("pe", mm)

            @b.scalar
            def _(sc):
                W(sc, "pe", pe0 + 1)
                inc("se", sc.activation(murow, pA[0:1, 0:256], AF.Sqrt))
        with blk() as b:
            @b.vector
            def _(v):
                v.tensor_scalar_max(murow, murow, 1e-12)
                vflush(v)
                inc("ve", v.reciprocal(murow, murow))
        with blk() as b:
            pe1 = cnt["pe"]

            @b.tensor
            def _(te):
                inc("pe", te.matmul(pB[0:128, 0:256], onesrow[0:1, :], murow,
                                    start=True, stop=True))

            @b.vector
            def _(v):
                W(v, "pe", pe1 + 1)
                for dc in range(NCH):
                    r = v.tensor_mul(MUB[:, dc * 256:(dc + 1) * 256],
                                     MUB[:, dc * 256:(dc + 1) * 256],
                                     pB[0:128, 0:256])
                inc("ve", r)

        with blk() as b:
            se0 = cnt["se"]
            ve0 = cnt["ve"]
            pe0 = cnt["pe"]
            sq_done = {}
            scn, ven = 0, 0
            for c in range(NCH):
                if c % 2 == 0:
                    scn += 1
                    sq_done[c] = ("se", se0 + scn)
                else:
                    ven += 1
                    sq_done[c] = ("ve", ve0 + ven)

            @b.scalar
            def _(sc):
                for c in range(0, NCH, 2):
                    if c >= 2:
                        W(sc, "pe", pe0 + c - 1)
                    inc("se", sc.activation(SQS[:, :],
                                            XN[:, c * SEQ:(c + 1) * SEQ],
                                            AF.Square))

            @b.vector
            def _(v):
                for c in range(1, NCH, 2):
                    if c >= 2:
                        W(v, "pe", pe0 + c - 1)
                    inc("ve", v.tensor_mul(SQS2[:, :],
                                           XN[:, c * SEQ:(c + 1) * SEQ],
                                           XN[:, c * SEQ:(c + 1) * SEQ]))

            @b.tensor
            def _(te):
                for c in range(NCH):
                    en, val = sq_done[c]
                    W(te, en, val)
                    buf = SQS if c % 2 == 0 else SQS2
                    for st in range(4):
                        mm = te.matmul(pW[0:1, st * 512:(st + 1) * 512],
                                       onescol[:, :],
                                       buf[:, st * 512:(st + 1) * 512],
                                       start=(c == 0), stop=(c == NCH - 1))
                    inc("pe", mm)

            @b.scalar
            def _(sc):
                W(sc, "pe", pe0 + 8)
                inc("se", sc.activation(r3row, pW[0:1, :], AF.Sqrt))
        with blk() as b:
            @b.vector
            def _(v):
                v.tensor_scalar_max(r3row, r3row, 1e-12)
                vflush(v)
                v.reciprocal(r3row, r3row)
                vflush(v)
                inc("ve", v.tensor_scalar_mul(r3row, r3row,
                                              l1scale[0:1, 0:1]))
        bcast_rows([(r3row, R3B)])

        for esc in range(2):
            with blk() as b:
                pe0 = cnt["pe"]

                @b.tensor
                def _(te):
                    for st in range(4):
                        for dc in range(NCH):
                            mm = te.matmul(
                                pW[:, st * 512:(st + 1) * 512],
                                MUB[:, dc * 256 + esc * 128: dc * 256 + esc * 128 + 128],
                                XN[:, dc * SEQ + st * 512: dc * SEQ + (st + 1) * 512],
                                start=(dc == 0), stop=(dc == NCH - 1))
                    inc("pe", mm)

                @b.vector
                def _(v):
                    W(v, "pe", pe0 + 1)
                    lsl = LSB[:, esc * SEQ:(esc + 1) * SEQ]
                    v.tensor_mul(lsl, pW[:, :], R3B[:, :])
                    inc("ve", v.reduce_max(NMX[:, 0:1], lsl, axis=AX.X,
                                           negate=True))
            with blk() as b:
                @b.scalar
                def _(sc):
                    lsl = LSB[:, esc * SEQ:(esc + 1) * SEQ]
                    inc("se", sc.activation(DSP[:, :], lsl, AF.Exp,
                                            bias=NMX[:, 0:1],
                                            accum_out=NMX[:, 8:9]))
            with blk() as b:
                @b.vector
                def _(v):
                    v.reciprocal(NMX[:, 16:17], NMX[:, 8:9])
                    vflush(v)
                    inc("ve", v.tensor_scalar_mul(DSP[:, :], DSP[:, :],
                                                  NMX[:, 16:17]))
            with blk() as b:
                pe1 = cnt["pe"]
                se1 = cnt["se"]

                @b.tensor
                def _(te):
                    for nc_ in range(16):
                        pt = (pA if nc_ % 2 == 0 else pB)[:, 0:64].bitcast(BF16)
                        if nc_ >= 2:
                            W(te, "se", se1 + nc_ - 1)
                        inc("pe", te.transpose(pt, DSP[:, nc_ * 128:(nc_ + 1) * 128],
                                               idb[:, :]))

                @b.scalar
                def _(sc):
                    for nc_ in range(16):
                        W(sc, "pe", pe1 + nc_ + 1)
                        pt = (pA if nc_ % 2 == 0 else pB)[:, 0:64].bitcast(BF16)
                        inc("se", sc.copy(
                            DTM[:, nc_ * 256 + esc * 128: nc_ * 256 + (esc + 1) * 128],
                            pt))
        if "disp" in dbg_t:
            with blk() as b:
                @b.sync
                def _(sy):
                    W(sy, "se")
                    inc("hq", sy.dma_start(dbg_t["disp"], DTM[:, :]), 16)
                    W(sy, "hq")
        if "lsb" in dbg_t:
            with blk() as b:
                @b.sync
                def _(sy):
                    W(sy, "se")
                    inc("hq", sy.dma_start(dbg_t["lsb"], LSB[:, :]), 16)
                    W(sy, "hq")

        with blk() as b:
            pe0 = cnt["pe"]
            se0 = cnt["se"]

            @b.tensor
            def _(te):
                for i in range(32):
                    nc_, esc = i // 2, i % 2
                    pt = (pA if i % 2 == 0 else pB)[:, 0:64].bitcast(BF16)
                    if i >= 2:
                        W(te, "se", se0 + i - 1)
                    inc("pe", te.transpose(
                        pt, LSB[:, esc * SEQ + nc_ * 128: esc * SEQ + (nc_ + 1) * 128],
                        idb[:, :]))

            @b.scalar
            def _(sc):
                for i in range(32):
                    nc_, esc = i // 2, i % 2
                    W(sc, "pe", pe0 + i + 1)
                    pt = (pA if i % 2 == 0 else pB)[:, 0:64].bitcast(BF16)
                    inc("se", sc.copy(
                        LTM[:, nc_ * 256 + esc * 128: nc_ * 256 + (esc + 1) * 128],
                        pt))
        with blk() as b:
            @b.vector
            def _(v):
                for nc_ in range(16):
                    v.reduce_max(CM0[:, nc_:nc_ + 1],
                                 LTM[:, nc_ * 256: nc_ * 256 + 128], axis=AX.X)
                    v.reduce_max(CM1[:, nc_:nc_ + 1],
                                 LTM[:, nc_ * 256 + 128: nc_ * 256 + 256], axis=AX.X)
                vflush(v)
                v.tensor_max(CM0[:, 0:16], CM0[:, 0:16], CM1[:, 0:16])
                vflush(v)
                inc("ve", v.tensor_scalar_mul(CM0[:, 0:16], CM0[:, 0:16], -1.0))
        with blk() as b:
            @b.scalar
            def _(sc):
                for nc_ in range(16):
                    sc.activation(CTM[:, nc_ * 256: nc_ * 256 + 128],
                                  LTM[:, nc_ * 256: nc_ * 256 + 128], AF.Exp,
                                  bias=CM0[:, nc_:nc_ + 1],
                                  accum_out=CS0[:, nc_:nc_ + 1])
                    inc("se", sc.activation(
                        CTM[:, nc_ * 256 + 128: nc_ * 256 + 256],
                        LTM[:, nc_ * 256 + 128: nc_ * 256 + 256], AF.Exp,
                        bias=CM0[:, nc_:nc_ + 1],
                        accum_out=CS1[:, nc_:nc_ + 1]))
        with blk() as b:
            @b.vector
            def _(v):
                v.tensor_add(CS0[:, 0:16], CS0[:, 0:16], CS1[:, 0:16])
                vflush(v)
                v.reciprocal(CS0[:, 0:16], CS0[:, 0:16])
                vflush(v)
                for nc_ in range(16):
                    r = v.tensor_scalar_mul(CTM[:, nc_ * 256:(nc_ + 1) * 256],
                                            CTM[:, nc_ * 256:(nc_ + 1) * 256],
                                            CS0[:, nc_:nc_ + 1])
                inc("ve", r)

        if dbg_t:
            with blk() as b:
                @b.sync
                def _(sy):
                    W(sy, "ve")
                    for nm, src in [("ctm", CTM[:, :]), ("ltm", LTM[:, :])]:
                        if nm in dbg_t:
                            inc("hq", sy.dma_start(dbg_t[nm], src), 16)
                    if "cm" in dbg_t:
                        inc("hq", sy.dma_start(dbg_t["cm"][:, 0:16], CM0[:, :]), 16)
                        inc("hq", sy.dma_start(dbg_t["cm"][:, 16:32], CM1[:, :]), 16)
                        inc("hq", sy.dma_start(dbg_t["cm"][:, 32:48], CS0[:, :]), 16)
                        inc("hq", sy.dma_start(dbg_t["cm"][:, 48:64], CS1[:, :]), 16)
                    W(sy, "hq")

        with blk() as b:
            pe0 = cnt["pe"]
            se0 = cnt["se"]

            @b.tensor
            def _(te):
                W(te, "ve")
                for i in range(32):
                    nc_, esc = i // 2, i % 2
                    pt = (pA if i % 2 == 0 else pB)[:, 0:64].bitcast(BF16)
                    if i >= 2:
                        W(te, "se", se0 + i - 1)
                    inc("pe", te.transpose(
                        pt, CTM[:, nc_ * 256 + esc * 128: nc_ * 256 + (esc + 1) * 128],
                        idb[:, :]))

            @b.scalar
            def _(sc):
                for i in range(32):
                    nc_, esc = i // 2, i % 2
                    W(sc, "pe", pe0 + i + 1)
                    pt = (pA if i % 2 == 0 else pB)[:, 0:64].bitcast(BF16)
                    inc("se", sc.copy(
                        CBT[:, esc * SEQ + nc_ * 128: esc * SEQ + (nc_ + 1) * 128],
                        pt))
        if "combt" in dbg_t:
            with blk() as b:
                @b.sync
                def _(sy):
                    W(sy, "se")
                    inc("hq", sy.dma_start(dbg_t["combt"], CBT[:, :]), 16)
                    W(sy, "hq")

        inner9.close()

        s9b = ExitStack()
        XTM = s9b.enter_context(nc.sbuf_tensor("XTM", [128, 16 * 1024], BF16))
        SIT = s9b.enter_context(nc.sbuf_tensor("SIT", [128, 8 * 256], BF16))
        with blk() as b:
            pe0 = cnt["pe"]
            se0 = cnt["se"]

            @b.tensor
            def _(te):
                for i in range(128):
                    dc, nc_ = i // 16, i % 16
                    pt = (pA if i % 2 == 0 else pB)[:, 0:64].bitcast(BF16)
                    if i >= 2:
                        W(te, "se", se0 + i - 1)
                    inc("pe", te.transpose(
                        pt, XN[:, dc * SEQ + nc_ * 128: dc * SEQ + (nc_ + 1) * 128],
                        idb[:, :]))

            @b.scalar
            def _(sc):
                for i in range(128):
                    dc, nc_ = i // 16, i % 16
                    W(sc, "pe", pe0 + i + 1)
                    pt = (pA if i % 2 == 0 else pB)[:, 0:64].bitcast(BF16)
                    inc("se", sc.copy(
                        XTM[:, nc_ * 1024 + dc * 128: nc_ * 1024 + (dc + 1) * 128],
                        pt))

        with blk() as b:
            pe0 = cnt["pe"]
            se0 = cnt["se"]

            @b.tensor
            def _(te):
                for dc in range(NCH):
                    bank = pb4[dc % 4]
                    if dc >= 4:
                        W(te, "se", se0 + dc - 3)
                    for nc_ in range(16):
                        mm = te.matmul(
                            bank[0:128, 0:256],
                            XTM[:, nc_ * 1024 + dc * 128: nc_ * 1024 + (dc + 1) * 128],
                            DTM[:, nc_ * 256:(nc_ + 1) * 256],
                            start=(nc_ == 0), stop=(nc_ == 15))
                    inc("pe", mm)

            @b.scalar
            def _(sc):
                for dc in range(NCH):
                    W(sc, "pe", pe0 + dc + 1)
                    inc("se", sc.copy(SIT[:, dc * 256:(dc + 1) * 256],
                                      pb4[dc % 4][0:128, 0:256]))
        if "sit" in dbg_t:
            with blk() as b:
                @b.sync
                def _(sy):
                    W(sy, "se")
                    inc("hq", sy.dma_start(dbg_t["sit"], SIT[:, :]), 16)
                    W(sy, "hq")

        if stop_after == "moe1":
            _finish()
            s9b.close()
            m1ctx.close()
            return nc

        s9c = ExitStack()
        HT = s9c.enter_context(nc.sbuf_tensor("HT", [128, 2 * 256], BF16))
        SOT = s9c.enter_context(nc.sbuf_tensor("SOT", [128, 8 * 256], BF16))
        STM = s9c.enter_context(nc.sbuf_tensor("STM", [128, 2 * 1024], BF16))
        with blk() as b:
            pe0 = cnt["pe"]
            se0 = cnt["se"]
            g0 = cnt["gq"]

            e_ld = {}

            @b.gpsimd
            def _(gp):
                for e in range(16):
                    slot = e % 2
                    e_ld[e] = ws_load(gp, "gq", ew1_img[e], 2048, slot)
                    ws_free[slot] = pe0 + (e + 1) * 2

            @b.tensor
            def _(te):
                for e in range(16):
                    slot = e % 2
                    te.wait_ge(sem[e_ld[e][0]], e_ld[e][1])
                    if e >= 1:
                        W(te, "se", se0 + (e - 1) * 2 + 2)
                    for ih in range(2):
                        for dc in range(NCH):
                            mm = te.matmul(
                                pC[0:128, ih * 16:(ih + 1) * 16],
                                WS[:, slot * 4096 + dc * 256 + ih * 128: slot * 4096 + dc * 256 + (ih + 1) * 128],
                                SIT[:, dc * 256 + e * 16: dc * 256 + (e + 1) * 16],
                                start=(dc == 0), stop=(dc == NCH - 1))
                        inc("pe", mm)

            @b.scalar
            def _(sc):
                for e in range(16):
                    W(sc, "pe", pe0 + e * 2 + 2)
                    for ih in range(2):
                        inc("se", sc.activation(
                            HT[:, ih * 256 + e * 16: ih * 256 + (e + 1) * 16],
                            pC[0:128, ih * 16:(ih + 1) * 16], AF.Gelu,
                            bias=b1t[:, e * 2 + ih: e * 2 + ih + 1]))

        if stop_after == "moe2a":
            _finish()
            s9c.close()
            s9b.close()
            m1ctx.close()
            return nc

        with blk() as b:
            pe0 = cnt["pe"]
            se0 = cnt["se"]
            g0 = cnt["gq"]

            w2_ld = {}

            @b.gpsimd
            def _(gp):
                for dc in range(8):
                    slot = dc % 2
                    w2_ld[dc] = ws_load(gp, "gq", ew2_img[dc], 4096, slot)
                    ws_free[slot] = pe0 + (dc + 1) * 16

            @b.tensor
            def _(te):
                for dc in range(8):
                    slot = dc % 2
                    te.wait_ge(sem[w2_ld[dc][0]], w2_ld[dc][1])
                    bank = pC if dc % 2 == 0 else pD
                    if dc >= 2:
                        W(te, "se", se0 + (dc - 2) * 16 + 16)
                    for e in range(16):
                        for hh in range(2):
                            mm = te.matmul(
                                bank[0:128, e * 16:(e + 1) * 16],
                                WS[:, slot * 4096 + (e * 2 + hh) * 128: slot * 4096 + (e * 2 + hh + 1) * 128],
                                HT[:, hh * 256 + e * 16: hh * 256 + (e + 1) * 16],
                                start=(hh == 0), stop=(hh == 1))
                        inc("pe", mm)

            @b.vector
            def _(v):
                for dc in range(8):
                    bank = pC if dc % 2 == 0 else pD
                    W(v, "pe", pe0 + (dc + 1) * 16)
                    for e in range(16):
                        inc("se", v.tensor_scalar_add(
                            SOT[:, dc * 256 + e * 16: dc * 256 + (e + 1) * 16],
                            bank[0:128, e * 16:(e + 1) * 16],
                            b2t[:, e * 8 + dc: e * 8 + dc + 1]))
        if "sot" in dbg_t:
            with blk() as b:
                @b.sync
                def _(sy):
                    W(sy, "se")
                    inc("hq", sy.dma_start(dbg_t["sot"], SOT[:, :]), 16)
                    W(sy, "hq")

        if stop_after == "moe2b":
            _finish()
            s9c.close()
            s9b.close()
            m1ctx.close()
            return nc

        with blk() as b:
            pe0 = cnt["pe"]
            se0 = cnt["se"]

            @b.tensor
            def _(te):
                W(te, "se")
                for i in range(16):
                    dc, esc = i // 2, i % 2
                    pt = (pA if i % 2 == 0 else pB)[:, 0:64].bitcast(BF16)
                    if i >= 2:
                        W(te, "se", se0 + i - 1)
                    inc("pe", te.transpose(
                        pt, SOT[:, dc * 256 + esc * 128: dc * 256 + (esc + 1) * 128],
                        idb[:, :]))

            @b.scalar
            def _(sc):
                for i in range(16):
                    dc, esc = i // 2, i % 2
                    W(sc, "pe", pe0 + i + 1)
                    pt = (pA if i % 2 == 0 else pB)[:, 0:64].bitcast(BF16)
                    inc("se", sc.copy(
                        STM[:, esc * 1024 + dc * 128: esc * 1024 + (dc + 1) * 128],
                        pt))

        if stop_after == "moe2c":
            _finish()
            s9c.close()
            s9b.close()
            m1ctx.close()
            return nc

        with blk() as b:
            pe0 = cnt["pe"]
            ve0 = cnt["ve"]

            @b.tensor
            def _(te):
                W(te, "se")
                i = 0
                for dc in range(8):
                    for st in range(4):
                        bank = pb4[i % 4]
                        if i >= 4:
                            W(te, "ve", ve0 + i - 3)
                        for esc in range(2):
                            mm = te.matmul(
                                bank[0:128, 0:512],
                                STM[:, esc * 1024 + dc * 128: esc * 1024 + (dc + 1) * 128],
                                CBT[:, esc * SEQ + st * 512: esc * SEQ + (st + 1) * 512],
                                start=(esc == 0), stop=(esc == 1))
                        inc("pe", mm)
                        i += 1

            @b.vector
            def _(v):
                i = 0
                for dc in range(8):
                    for st in range(4):
                        W(v, "pe", pe0 + i + 1)
                        dstc = slice(dc * SEQ + st * 512, dc * SEQ + (st + 1) * 512)
                        inc("ve", v.tensor_add(S[:, dstc], S[:, dstc],
                                               pb4[i % 4][0:128, 0:512]))
                        i += 1

            @b.sync
            def _(sy):
                for dc in range(8):
                    W(sy, "ve", ve0 + dc * 4 + 4)
                    inc("hq", sy.dma_start(
                        outT[:, dc * SEQ:(dc + 1) * SEQ],
                        S[:, dc * SEQ:(dc + 1) * SEQ]), 16)
                W(sy, "hq")
        s9c.close()
        s9b.close()
        m1ctx.close()
        _check_waits()
    return nc


def kernel(**inputs):
    g = _prep(inputs)
    nc = build()
    common = {k: v for k, v in g.items() if k != "x_img"}
    in_maps = []
    for b in range(8):
        m = dict(common)
        m["x_img"] = g["x_img"][b]
        in_maps.append(m)
    res = run_bass_kernel_spmd(nc, in_maps, core_ids=list(range(8)))
    outs = []
    for b in range(8):
        img = np.asarray(res.results[b]["outT"], dtype=np.float32)
        outs.append(img.reshape(128, NCH, SEQ).transpose(1, 0, 2)
                    .reshape(D, SEQ).T)
    return np.stack(outs, 0).astype(np.float32)


# version 23
# speedup vs baseline: 1.0037x; 1.0004x over previous
import numpy as np
import ml_dtypes
from contextlib import ExitStack

import concourse.bass as bass
import concourse.mybir as mybir
from concourse.bass_utils import run_bass_kernel_spmd

F32 = mybir.dt.float32
BF16 = mybir.dt.bfloat16
AF = mybir.ActivationFunctionType
ALU = mybir.AluOpType
AX = mybir.AxisListType
BF = ml_dtypes.bfloat16

D = 1024
SEQ = 2048
NCH = 8
NH = 16
DK = 64
DH = 4096
HCH = 32
NQS = 4
NKC = 16
LN_EPS = 1e-5


def _img_chunks(a, p=128):
    c = a.shape[0] // p
    return np.ascontiguousarray(
        a.reshape(c, p, a.shape[1]).transpose(1, 0, 2).reshape(p, c * a.shape[1]))


def _wimg(wT, oc_size):
    K, O = wT.shape
    kc, oc = K // 128, O // oc_size
    b = wT.reshape(kc, 128, oc, oc_size).transpose(2, 1, 0, 3)
    return np.ascontiguousarray(b.reshape(oc, 128, kc * oc_size))


def _prep(inputs):
    f = np.float32
    g = {}
    x = np.asarray(inputs["L0_attn_x"], dtype=f)
    g["x_img"] = [np.ascontiguousarray(
        x[b].T.reshape(NCH, 128, SEQ).transpose(1, 0, 2).reshape(128, NCH * SEQ))
        for b in range(8)]

    g["wq_img"] = _wimg(np.ascontiguousarray(np.asarray(inputs["L0_attn_W_Q"], f).T), 128)
    g["wk_img"] = _wimg(np.ascontiguousarray(np.asarray(inputs["L0_attn_W_K"], f).T), 128)
    g["wv_img"] = _img_chunks(np.ascontiguousarray(np.asarray(inputs["L0_attn_W_V"], f).T))
    g["wo_img"] = _wimg(np.ascontiguousarray(np.asarray(inputs["L0_attn_W_O"], f).T), 128)

    for nm, tag in [("L0_ffn_w1_w", "w1"), ("L0_ffn_w2_w", "w2"),
                    ("L0_ffn_h1_w", "h1"), ("L0_ffn_h2_w", "h2"),
                    ("L0_ffn_w3_w", "w3")]:
        wT = np.ascontiguousarray(np.asarray(inputs[nm], f).T).astype(BF)
        g[tag + "_img"] = _wimg(wT, 128)

    def scl(name, c):
        return np.ascontiguousarray(np.asarray(inputs[name], f).reshape(c, 128).T)
    g["s1"] = scl("L0_ffn_w1_s", 32)
    g["s2"] = scl("L0_ffn_w2_s", 32)
    g["sh1"] = scl("L0_ffn_h1_s", 32)
    g["sh2"] = scl("L0_ffn_h2_s", 32)
    g["s3"] = scl("L0_ffn_w3_s", 8)
    g["l1gamma"] = scl("L1_attn_gamma", 8)
    g["qkva_img"] = _wimg(
        np.ascontiguousarray(np.asarray(inputs["L1_attn_w_qkva"], f).T), 128)
    g["l1g"] = scl("L1_ffn_ln_gamma", 8)
    g["l1b"] = scl("L1_ffn_ln_beta", 8)
    g["mu_img"] = _img_chunks(np.asarray(inputs["L1_ffn_mu"], f).reshape(D, 256))
    g["l1scale"] = np.asarray(inputs["L1_ffn_scale"], f).reshape(1, 1)
    w1 = np.asarray(inputs["L1_ffn_w1"], f)
    g["ew1_img"] = np.ascontiguousarray(
        w1.reshape(16, 8, 128, 256).transpose(0, 2, 1, 3).reshape(16, 128, 2048))
    w2 = np.asarray(inputs["L1_ffn_w2"], f)
    g["ew2_img"] = np.ascontiguousarray(
        w2.reshape(16, 2, 128, 8, 128).transpose(3, 2, 0, 1, 4).reshape(8, 128, 4096))
    g["b1t"] = np.ascontiguousarray(
        np.asarray(inputs["L1_ffn_b1"], f).reshape(16, 2, 128)
        .transpose(2, 0, 1).reshape(128, 32))
    g["b2t"] = np.ascontiguousarray(
        np.asarray(inputs["L1_ffn_b2"], f).reshape(16, 8, 128)
        .transpose(2, 0, 1).reshape(128, 128))

    theta = 1.0 / 10000.0 ** (np.arange(0, DK, 2, dtype=np.float64) / DK)
    ang = np.outer(np.arange(SEQ, dtype=np.float64), theta)
    cosT = np.cos(ang).T.astype(f)
    sinT = np.sin(ang).T.astype(f)
    c64 = np.concatenate([cosT, cosT], 0)
    sf64 = np.concatenate([-sinT, sinT], 0)
    g["cos2"] = np.ascontiguousarray(np.concatenate([c64, c64], 0)).astype(BF)
    g["sflip2"] = np.ascontiguousarray(np.concatenate([sf64, sf64], 0)).astype(BF)
    m = np.zeros((128, 128), dtype=BF)
    for p in range(128):
        m[p, p:] = 1
    g["mask"] = m
    g["identb"] = np.eye(128, dtype=BF)
    onesq = np.zeros((128, 33), dtype=BF)
    onesq[0:64, 0] = 1
    onesq[64:128, 32] = 1
    g["onesq"] = onesq
    oa = np.zeros((128, 128), dtype=BF)
    oa[0:64, :] = 1
    g["ones64a"] = oa
    ob = np.zeros((128, 128), dtype=BF)
    ob[64:128, :] = 1
    g["ones64b"] = ob
    g["onescol"] = np.ones((128, 1), dtype=BF)
    g["onesrow"] = np.ones((1, 128), dtype=f)
    epsc = np.zeros((128, 4), dtype=f)
    epsc[:, 0] = LN_EPS
    epsc[:, 1] = 1e-12
    epsc[:, 2] = LN_EPS * LN_EPS
    g["epsc"] = epsc
    return g


def build(dbg=(), stop_after=None, check=False):
    nc = bass.Bass("TRN2", target_bir_lowering=False)
    dt = nc.dram_tensor

    def din(name, shape, dtype):
        return dt(name, shape, dtype, kind="ExternalInput").ap()

    x_img = din("x_img", [128, NCH * SEQ], F32)
    wq_img = din("wq_img", [8, 128, 1024], F32)
    wk_img = din("wk_img", [8, 128, 1024], F32)
    wv_img = din("wv_img", [128, 8192], F32)
    wo_img = din("wo_img", [8, 128, 1024], F32)
    w1_img = din("w1_img", [32, 128, 1024], BF16)
    w2_img = din("w2_img", [32, 128, 1024], BF16)
    h1_img = din("h1_img", [32, 128, 4096], BF16)
    h2_img = din("h2_img", [32, 128, 4096], BF16)
    w3_img = din("w3_img", [8, 128, 4096], BF16)
    s1_i = din("s1", [128, 32], F32)
    s2_i = din("s2", [128, 32], F32)
    sh1_i = din("sh1", [128, 32], F32)
    sh2_i = din("sh2", [128, 32], F32)
    s3_i = din("s3", [128, 8], F32)
    l1gamma_i = din("l1gamma", [128, 8], F32)
    qkva_img = din("qkva_img", [24, 128, 1024], F32)
    l1g_i = din("l1g", [128, 8], F32)
    l1b_i = din("l1b", [128, 8], F32)
    mu_img = din("mu_img", [128, 2048], F32)
    l1scale_i = din("l1scale", [1, 1], F32)
    ew1_img = din("ew1_img", [16, 128, 2048], F32)
    ew2_img = din("ew2_img", [8, 128, 4096], F32)
    b1t_i = din("b1t", [128, 32], F32)
    b2t_i = din("b2t", [128, 128], F32)
    cos2_i = din("cos2", [128, 2048], BF16)
    sflip2_i = din("sflip2", [128, 2048], BF16)
    mask_i = din("mask", [128, 128], BF16)
    identb_i = din("identb", [128, 128], BF16)
    onesq_i = din("onesq", [128, 33], BF16)
    ones64a_i = din("ones64a", [128, 128], BF16)
    ones64b_i = din("ones64b", [128, 128], BF16)
    onescol_i = din("onescol", [128, 1], BF16)
    onesrow_i = din("onesrow", [1, 128], F32)
    epsc_i = din("epsc", [128, 4], F32)

    outT = dt("outT", [128, NCH * SEQ], F32, kind="ExternalOutput").ap()
    settle = dt("settle", [128, NCH * SEQ], F32, kind="Internal").ap()
    snaps = {}
    if "snaps" in dbg:
        for nm in ["ss1", "ss2", "ss3"]:
            snaps[nm] = dt(nm, [128, NCH * SEQ], F32, kind="ExternalOutput").ap()
    okind = "ExternalOutput" if "oT" in dbg else "Internal"
    odram = dt("odram", [8, 128, SEQ], BF16, kind=okind).ap()

    dbg_t = {}
    for name in dbg:
        if name in ("oT", "snaps"):
            continue
        shp, dtp = {
            "xn": ([128, NCH * SEQ], BF16), "x2": ([128, NCH * SEQ], F32),
            "attn": ([128, NCH * SEQ], F32), "xr": ([128, NCH * SEQ], BF16),
            "xln": ([128, NCH * SEQ], BF16), "mixS": ([128, NCH * SEQ], F32),
            "disp": ([128, 16 * 256], BF16), "combt": ([128, 2 * SEQ], BF16),
            "ctm": ([128, 16 * 256], BF16), "ltm": ([128, 16 * 256], BF16),
            "cm": ([128, 64], F32),
            "sit": ([128, 8 * 256], BF16), "sot": ([128, 8 * 256], BF16),
            "lsb": ([128, 2 * SEQ], BF16), "bh": ([128, 64], F32),
            "bt": ([1, 32], F32), "qt": ([128, SEQ], BF16),
            "vt": ([128, NKC * NH * 65], BF16),
            "qsq": ([128, SEQ], BF16), "nrm": ([128, 4096], F32),
            "c2t": ([128, SEQ], BF16), "sft": ([128, SEQ], BF16),
            "qct": ([128, SEQ], BF16), "qft": ([128, SEQ], BF16),
        }[name]
        dbg_t[name] = dt("dbg_" + name, shp, dtp, kind="ExternalOutput").ap()

    ctx = ExitStack()
    with ctx:
        sb = lambda n, s, d: ctx.enter_context(nc.sbuf_tensor(n, s, d))
        S = sb("S", [128, NCH * SEQ], F32)
        XN = sb("XN", [128, NCH * SEQ], BF16)
        WS = sb("WS", [128, 2 * 4096], BF16)
        s1 = sb("s1t", [128, 32], F32)
        s2 = sb("s2t", [128, 32], F32)
        sh1 = sb("sh1t", [128, 32], F32)
        sh2 = sb("sh2t", [128, 32], F32)
        s3 = sb("s3t", [128, 8], F32)
        l1gm = sb("l1gm", [128, 8], F32)
        l1g = sb("l1gt", [128, 8], F32)
        l1b = sb("l1bt", [128, 8], F32)
        b1t = sb("b1tt", [128, 32], F32)
        b2t = sb("b2tt", [128, 128], F32)
        mask = sb("maskt", [128, 128], BF16)
        idb = sb("idbt", [128, 128], BF16)
        onesq = sb("onesqt", [128, 33], BF16)
        ones64a = sb("ones64at", [128, 128], BF16)
        ones64b = sb("ones64bt", [128, 128], BF16)
        onescol = sb("onescolt", [128, 1], BF16)
        onesrow = sb("onesrowt", [1, 128], F32)
        l1scale = sb("l1scalet", [1, 4], F32)
        BH = sb("BHt", [128, 64], F32)
        JNK = sb("JNKt", [128, 256], F32)
        EPSC = sb("EPSCt", [128, 4], F32)
        BT = sb("BTt", [1, 32], F32)
        WVT = sb("WVT", [128, 4096], BF16)

        pA = ctx.enter_context(nc.psum_tensor("pA", [128, 512], F32))
        pB = ctx.enter_context(nc.psum_tensor("pB", [128, 512], F32))
        pC = ctx.enter_context(nc.psum_tensor("pC", [128, 512], F32))
        pD = ctx.enter_context(nc.psum_tensor("pD", [128, 512], F32))
        pW = ctx.enter_context(nc.psum_tensor("pW", [128, 2048], F32))
        pb4 = [pA, pB, pC, pD]

        sem = {}
        cnt = {}
        for name in ["gq", "hq", "pe", "ve", "se",
                     "gA", "gB", "gC", "gD", "hA", "hB", "hC",
                     "oA", "oB", "xA", "xB"]:
            sem[name] = ctx.enter_context(nc.semaphore(name))
            cnt[name] = 0

        def inc(name, instr, k=1):
            instr.then_inc(sem[name], k)
            cnt[name] += k
            return cnt[name]

        _wlog = []

        def W(eng, name, val=None):
            v = cnt[name] if val is None else val
            _wlog.append((name, v))
            eng.wait_ge(sem[name], v)

        def vflush(v):
            v.tensor_copy(JNK[:, 0:128], JNK[:, 128:256])

        def _check_waits():
            bad = [(n, v, cnt[n]) for (n, v) in _wlog if v > cnt[n]]
            if bad:
                raise RuntimeError("unsatisfiable waits: %s" % bad[:10])

        def _finish():
            with nc.Block() as b:
                @b.sync
                def _(sy):
                    W(sy, "ve")
                    W(sy, "se")
                    W(sy, "pe")
                    inc("hq", sy.dma_start(outT, S[:, :]), 16)
                    W(sy, "hq")

        ws_free = [0, 0, 0]
        _gslot = ["gA", "gB", "gC"]
        _hslot = ["hA", "hB", "hC"]

        def ws_load(eng_dma, qname, src, ncols, slot):
            if ws_free[slot] > 0:
                eng_dma.wait_ge(sem["pe"], ws_free[slot])
            sname = _gslot[slot] if qname == "gq" else _hslot[slot]
            r = eng_dma.dma_start(WS[:, slot * 4096: slot * 4096 + ncols], src)
            inc(sname, r, 16)
            return (sname, cnt[sname])

        _blkid = [0]

        def blk():
            _blkid[0] += 1
            return nc.Block("b%d" % _blkid[0], no_gpsimd_drain=True)

        with blk() as b:
            @b.sync
            def _(sy):
                inc("xA", sy.dma_start(S[:, 0:4 * SEQ], x_img[:, 0:4 * SEQ]),
                    16)
                small = [(s1[:, :], s1_i), (s2[:, :], s2_i),
                         (sh1[:, :], sh1_i), (sh2[:, :], sh2_i), (s3[:, :], s3_i),
                         (l1gm[:, :], l1gamma_i), (l1g[:, :], l1g_i),
                         (l1b[:, :], l1b_i), (b1t[:, :], b1t_i),
                         (b2t[:, :], b2t_i), (mask[:, :], mask_i),
                         (idb[:, :], identb_i), (onesq[:, :], onesq_i),
                         (ones64a[:, :], ones64a_i), (ones64b[:, :], ones64b_i),
                         (onescol[:, :], onescol_i), (onesrow[:, :], onesrow_i),
                         (l1scale[0:1, 0:1], l1scale_i), (EPSC[:, :], epsc_i)]
                for d_, s_ in small:
                    inc("hq", sy.dma_start(d_, s_), 16)
                W(sy, "hq")
                W(sy, "xA")
                inc("xB", sy.dma_start(S[:, 4 * SEQ:], x_img[:, 4 * SEQ:]),
                    16)

            @b.gpsimd
            def _(gp):
                for dc in range(NCH):
                    inc("gD", gp.dma_start(
                        WVT[:, dc * 512:(dc + 1) * 512],
                        wv_img[:, dc * 1024: dc * 1024 + 512]), 16)

            @b.vector
            def _(v):
                v.memset(JNK[:, :], 0.0)

        _lnid = [0]

        def ln_stats(src, gates=None, scratch=None):
            _lnid[0] += 1
            sctx = ExitStack()
            if scratch is not None:
                XBF = scratch[:, 0:SEQ]
                XSQ = scratch[:, SEQ:2 * SEQ]
            else:
                XBF = sctx.enter_context(nc.sbuf_tensor("st_xbf%d" % _lnid[0], [128, SEQ], BF16))
                XSQ = sctx.enter_context(nc.sbuf_tensor("st_xsq%d" % _lnid[0], [128, SEQ], BF16))
            with blk() as b:
                se0 = cnt["se"]
                ve0 = cnt["ve"]
                pe0 = cnt["pe"]

                @b.scalar
                def _(sc):
                    for c in range(NCH):
                        if gates is not None:
                            sc.wait_ge(sem[gates[c][0]], gates[c][1])
                        if c >= 1:
                            W(sc, "pe", pe0 + c)
                        inc("se", sc.copy(XBF[:, :],
                                          src[:, c * SEQ:(c + 1) * SEQ]))

                @b.vector
                def _(v):
                    for c in range(NCH):
                        if gates is not None:
                            v.wait_ge(sem[gates[c][0]], gates[c][1])
                        if c >= 1:
                            W(v, "pe", pe0 + c)
                        inc("ve", v.tensor_mul(
                            XSQ[:, :], src[:, c * SEQ:(c + 1) * SEQ],
                            src[:, c * SEQ:(c + 1) * SEQ]))

                @b.tensor
                def _(te):
                    for c in range(NCH):
                        W(te, "se", se0 + c + 1)
                        W(te, "ve", ve0 + c + 1)
                        for st in range(4):
                            cl = slice(st * 512, st * 512 + 512)
                            te.matmul(pW[0:1, cl], onescol[:, :], XBF[:, cl],
                                      start=(c == 0), stop=(c == NCH - 1))
                            mm = te.matmul(pW[32:33, cl], onescol[:, :], XSQ[:, cl],
                                           start=(c == 0), stop=(c == NCH - 1))
                            if st == 3:
                                inc("pe", mm)
            sctx.close()

        def bcast_rows(rows_tiles):
            with blk() as b:
                pe0 = cnt["pe"]
                se0 = cnt["se"]

                @b.tensor
                def _(te):
                    W(te, "ve")
                    for i, (row, _t) in enumerate(rows_tiles):
                        if i >= 1:
                            W(te, "se", se0 + i * 4)
                        for st in range(4):
                            inc("pe", te.matmul(
                                pb4[st][0:128, 0:512], onesrow[0:1, :],
                                row[0:1, st * 512:(st + 1) * 512],
                                start=True, stop=True))

                @b.scalar
                def _(sc):
                    for i, (row, tile) in enumerate(rows_tiles):
                        for st in range(4):
                            W(sc, "pe", pe0 + i * 4 + st + 1)
                            inc("se", sc.copy(
                                tile[:, st * 512:(st + 1) * 512],
                                pb4[st][0:128, 0:512]))

        s1ctx = ExitStack()
        BCA = s1ctx.enter_context(nc.sbuf_tensor("BCA", [128, SEQ], F32))
        BCB = s1ctx.enter_context(nc.sbuf_tensor("BCB", [128, SEQ], F32))
        RW = s1ctx.enter_context(nc.sbuf_tensor("RW", [1, 6144], F32))
        A_ = RW[0:1, 0:2048]
        B_ = RW[0:1, 2048:4096]
        C_ = RW[0:1, 4096:6144]

        ln_stats(S, gates=[("xA", cnt["xA"])] * 4 + [("xB", cnt["xB"])] * 4)
        with blk() as b:
            @b.vector
            def _(v):
                v.tensor_scalar_mul(A_, pW[0:1, :], 1.0 / D)
                v.tensor_scalar_mul(B_, pW[32:33, :], 1.0 / D)
                vflush(v)
                v.tensor_mul(C_, A_, A_)
                vflush(v)
                inc("ve", v.tensor_sub(B_, B_, C_))
        with blk() as b:
            @b.scalar
            def _(sc):
                inc("se", sc.activation(C_, B_, AF.Sqrt, bias=EPSC[0:1, 0:1]))
        with blk() as b:
            @b.vector
            def _(v):
                v.reciprocal(B_, C_)
                vflush(v)
                inc("ve", v.scalar_tensor_tensor(A_, A_, -1.0, B_,
                                                 ALU.mult, ALU.mult))
        bcast_rows([(B_, BCA), (A_, BCB)])
        with blk() as b:
            @b.vector
            def _(v):
                for c in range(NCH):
                    cs = slice(c * SEQ, (c + 1) * SEQ)
                    v.tensor_mul(pW[:, :], S[:, cs], BCA[:, :])
                    inc("ve", v.tensor_add(XN[:, cs], pW[:, :], BCB[:, :]))

            if "xn" in dbg_t:
                @b.sync
                def _(sy):
                    W(sy, "ve")
                    inc("hq", sy.dma_start(dbg_t["xn"], XN[:, :]), 16)
                    W(sy, "hq")
        s1ctx.close()
        if stop_after == "ln0":
            _finish()
            return nc

        actx = ExitStack()
        Vt = actx.enter_context(nc.sbuf_tensor("Vt", [128, NKC * NH * 65], BF16))
        C2 = actx.enter_context(nc.sbuf_tensor("C2", [128, SEQ], BF16))
        SFt = actx.enter_context(nc.sbuf_tensor("SFt", [128, SEQ], BF16))
        QT = actx.enter_context(nc.sbuf_tensor("QT", [128, 2 * SEQ], BF16))
        KT = actx.enter_context(nc.sbuf_tensor("KT", [128, SEQ], BF16))
        QC = actx.enter_context(nc.sbuf_tensor("QC", [128, SEQ], BF16))
        QF = actx.enter_context(nc.sbuf_tensor("QF", [128, SEQ], BF16))
        PT = actx.enter_context(nc.sbuf_tensor("PT", [128, 4 * 512], BF16))
        QSQ = actx.enter_context(nc.sbuf_tensor("QSQ", [128, 2 * SEQ], BF16))
        OSB = actx.enter_context(nc.sbuf_tensor("OSB", [128, 2 * 512], BF16))
        RBt = actx.enter_context(nc.sbuf_tensor("RBt", [128, 512], F32))
        RRt = actx.enter_context(nc.sbuf_tensor("RRt", [1, 1024], F32))
        vtr = Vt[:, :].rearrange("p (k h c) -> p k h c", k=NKC, h=NH)

        slot_q = {hp: 0 for hp in range(NCH)}
        slot_k = {hp: 1 for hp in range(NCH)}
        gq_load = {}
        osb_free = [0, 0]

        with blk() as b:
            gd_h0 = cnt["gD"]
            pe0 = cnt["pe"]
            se0 = cnt["se"]

            @b.sync
            def _(sy):
                inc("hq", sy.dma_start(C2[:, :], cos2_i), 16)
                inc("hq", sy.dma_start(SFt[:, :], sflip2_i), 16)

            @b.vector
            def _(v):
                inc("ve", v.memset(vtr[:, :, :, 64:65], 1.0))

            @b.gpsimd
            def _(gp):
                gp.wait_ge(sem["pe"], pe0 + 16)
                for dc in range(NCH):
                    inc("gD", gp.dma_start(
                        WVT[:, dc * 512:(dc + 1) * 512],
                        wv_img[:, dc * 1024 + 512: dc * 1024 + 1024]), 16)
            gd_h1 = cnt["gD"]

            @b.tensor
            def _(te):
                for st2 in range(2):
                    W(te, "gD", gd_h0 if st2 == 0 else gd_h1)
                    for nc_ in range(NKC):
                        i = st2 * 16 + nc_
                        if i >= 4:
                            W(te, "se", se0 + i - 3)
                        bank = pb4[i % 4]
                        for dc in range(NCH):
                            mm = te.matmul(
                                bank[0:128, 0:512],
                                XN[:, dc * SEQ + nc_ * 128: dc * SEQ + nc_ * 128 + 128],
                                WVT[:, dc * 512:(dc + 1) * 512],
                                start=(dc == 0), stop=(dc == NCH - 1))
                        inc("pe", mm)

            @b.scalar
            def _(sc):
                for st2 in range(2):
                    for nc_ in range(NKC):
                        i = st2 * 16 + nc_
                        W(sc, "pe", pe0 + i + 1)
                        src = pb4[i % 4][0:128, 0:512].rearrange(
                            "p (h c) -> p h c", h=8)
                        inc("se", sc.copy(
                            vtr[:, nc_, st2 * 8:(st2 + 1) * 8, 0:64], src))

        proj_done = {}


        def emit_bound_block(nhp, par):
            with blk() as b:
                pe0 = cnt["pe"]
                ve0 = cnt["ve"]

                @b.tensor
                def _(te):
                    for ti in range(2):
                        if ti == 1:
                            W(te, "ve", ve0 + 1)
                        qb = (par if ti == 0 else 1 - par) * SEQ
                        for st in range(4):
                            cl = slice(qb + st * 512, qb + (st + 1) * 512)
                            te.matmul(pW[0:128, st * 512:(st + 1) * 512],
                                      ones64a[:, :], QSQ[:, cl],
                                      start=True, stop=True)
                            mm = te.matmul(pb4[st][0:128, 0:512],
                                           ones64b[:, :], QSQ[:, cl],
                                           start=True, stop=True)
                        inc("pe", mm)

                @b.vector
                def _(v):
                    v.tensor_copy(QF[0:32, :], QC[32:64, :])
                    v.tensor_copy(QF[32:64, :], QC[0:32, :])
                    v.tensor_copy(QF[64:96, :], QC[96:128, :])
                    v.tensor_copy(QF[96:128, :], QC[64:96, :])
                    v.tensor_mul(KT[:, :], QC[:, :], C2[:, :])
                    v.tensor_mul(QF[:, :], QF[:, :], SFt[:, :])
                    v.tensor_add(KT[:, :], KT[:, :], QF[:, :])
                    for ti in range(2):
                        W(v, "pe", pe0 + ti + 1)
                        v.reduce_max(BH[:, 8 + ti * 4:8 + ti * 4 + 1],
                                     pW[:, :], axis=AX.X)
                        for st in range(4):
                            v.reduce_max(BH[:, 16 + st:16 + st + 1],
                                         pb4[st][0:128, 0:512], axis=AX.X)
                        vflush(v)
                        v.tensor_max(BH[:, 24:25], BH[:, 16:17], BH[:, 17:18])
                        v.tensor_max(BH[:, 25:26], BH[:, 18:19], BH[:, 19:20])
                        vflush(v)
                        v.tensor_max(BH[:, 8 + ti * 4 + 1:8 + ti * 4 + 2],
                                     BH[:, 24:25], BH[:, 25:26])
                        inc("ve", v.tensor_copy(JNK[:, 128:256], JNK[:, 0:128]))
                    vflush(v)
                    v.tensor_mul(BH[:, 26:27], BH[:, 8:9], BH[:, 12:13])
                    v.tensor_mul(BH[:, 27:28], BH[:, 9:10], BH[:, 13:14])
                    vflush(v)
                    inc("ve", v.tensor_tensor(BH[:, 28 + par:29 + par],
                                              BH[:, 26:27], BH[:, 27:28],
                                              ALU.max))

        for hp in range(-1, NCH):
            nhp = hp + 1
            par = nhp % 2
            cpar = hp % 2
            do_proj = nhp < NCH
            do_scores = hp >= 0
            h0, h1 = 2 * hp, 2 * hp + 1

            with blk() as b:
                if do_scores:
                    sched = [(qs, ki) for qs in range(NQS)
                             for ki in range(4 * qs + 4)]
                    nt_ = len(sched)
                else:
                    sched = []
                    nt_ = 0
                events = []
                for idx in range(nt_):
                    events.append(("S", idx))
                    if idx >= 1:
                        j = idx - 1
                        events.append(("P", j))
                        if sched[j][1] == 4 * sched[j][0] + 3:
                            events.append(("R", sched[j][0]))
                if nt_:
                    events.append(("P", nt_ - 1))
                    events.append(("R", NQS - 1))
                if do_proj:
                    jev = [("J", k) for k in range(8)]
                    if do_scores:
                        merged = []
                        nxt = 4
                        for i, e in enumerate(events):
                            merged.append(e)
                            if i + 1 == nxt and jev:
                                merged.append(jev.pop(0))
                                nxt += 9
                        merged.extend(jev)
                        events = merged
                    else:
                        events = jev
                pe0 = cnt["pe"]
                se0 = cnt["se"]
                ve0 = cnt["ve"]
                pe_scores = {}
                pe_pv = {}
                pe_rb = {}
                pe_strip = {}
                p = pe0
                for e in events:
                    p += 1
                    if e[0] == "S":
                        pe_scores[e[1]] = p
                    elif e[0] == "P":
                        pe_pv[e[1]] = p
                    elif e[0] == "R":
                        pe_rb[e[1]] = p
                    else:
                        pe_strip[e[1]] = p
                se_exp = {}
                se_ocp = {}
                se_qc = {}
                s_ = se0 + (1 if do_scores else 0)
                qcpos = {}
                qc_nexp = {}
                if do_proj and do_scores:
                    scnt, maxp = 0, -1
                    for e in events:
                        if e[0] == "S":
                            scnt += 1
                        elif e[0] == "P":
                            maxp = max(maxp, e[1])
                        elif e[0] == "J":
                            k = e[1]
                            nx = max(scnt - 1, maxp + 1, 1)
                            if k >= 4:
                                nx = max(nx, qc_nexp[3] + 1)
                            nx = min(nx, nt_)
                            qc_nexp[k] = nx
                            qcpos.setdefault(nx, []).append(k)
                se_lcp = {}
                se_rbt = {}
                sjobs = {}
                nexp = 0
                for idx, (qs, ki) in enumerate(sched):
                    nexp += 1
                    qcs = list(qcpos.get(nexp, ()))
                    if ki == 4 * qs + 3:
                        jl = [("qc", k) for k in qcs
                              if pe_strip[k] < pe_rb[qs]]
                        jl += [("lcp", qs), ("ocp", qs), ("rbt", qs)]
                        jl += [("qc", k) for k in qcs
                               if pe_strip[k] > pe_rb[qs]]
                    else:
                        jl = [("qc", k) for k in qcs]
                    sjobs[idx] = jl
                sd = {"qc": se_qc, "lcp": se_lcp, "ocp": se_ocp,
                      "rbt": se_rbt}
                for idx, (qs, ki) in enumerate(sched):
                    s_ += 1
                    se_exp[idx] = s_
                    for j in sjobs[idx]:
                        s_ += 1
                        sd[j[0]][j[1]] = s_
                if do_proj:
                    for k in range(8):
                        if k not in se_qc:
                            s_ += 1
                            se_qc[k] = s_
                bankw = {}
                jwait = {}
                rwait = {}
                for e in events:
                    if e[0] == "J":
                        lw = bankw.get(e[1] % 2)
                        if lw is not None:
                            jwait[e[1]] = lw
                        bankw[e[1] % 2] = e
                    elif e[0] == "R":
                        lw = bankw.get(e[1] % 2)
                        if lw is not None:
                            rwait[e[1]] = lw
                        bankw[e[1] % 2] = e

                ve_mask = {}
                ve_nrm = {}
                ve_sq = {}
                ve_ropeq = None
                vjobs = {}
                if do_proj and do_scores:
                    for k in range(8):
                        pos = min(qc_nexp[k], nt_ - 1)
                        vjobs.setdefault(pos, []).append(("sq", k))
                        if k == 3:
                            vjobs.setdefault(pos, []).append(("ropeq",))
                if do_scores:
                    for idx, (qs, ki) in enumerate(sched):
                        if ki == 4 * qs + 3:
                            for i in range(4):
                                vjobs.setdefault(idx + 1 + i, []).append(
                                    ("cl", qs, i))
                v_ = ve0
                vposs = sorted(set(range(nt_)) | set(vjobs.keys()))
                for idx in vposs:
                    if idx < nt_:
                        qs, ki = sched[idx]
                        if ki >= 4 * qs:
                            v_ += 1
                        ve_mask[idx] = v_
                    for job in vjobs.get(idx, []):
                        if job[0] == "sq":
                            v_ += 1
                            ve_sq[job[1]] = v_
                        elif job[0] == "ropeq":
                            v_ += 1
                            ve_ropeq = v_
                        elif job[2] == 3:
                            v_ += 1
                            ve_nrm[job[1]] = v_
                if not do_scores and do_proj:
                    for k in range(8):
                        v_ += 1
                        ve_sq[k] = v_
                        if k == 3:
                            v_ += 1
                            ve_ropeq = v_

                if do_proj:
                    @b.gpsimd
                    def _(gp):
                        if nhp >= 1:
                            ws_free[slot_q[nhp]] = proj_done[(nhp - 1, 0)]
                            ws_free[slot_k[nhp]] = proj_done[(nhp - 1, 1)]
                        gq_load[nhp] = [
                            ws_load(gp, "gq", wq_img[nhp], 1024, slot_q[nhp]),
                            ws_load(gp, "gq", wk_img[nhp], 1024, slot_k[nhp])]
                    proj_done[(nhp, 0)] = pe_strip[3]
                    proj_done[(nhp, 1)] = pe_strip[7]

                @b.tensor
                def _(te):
                    for e in events:
                        if e[0] == "S":
                            idx = e[1]
                            qs, ki = sched[idx]
                            pr = idx % 2
                            cl = slice(cpar * SEQ + qs * 512,
                                       cpar * SEQ + (qs + 1) * 512)
                            kb = slice(ki * 128, (ki + 1) * 128)
                            if idx >= 2:
                                W(te, "se", se_exp[idx - 2])
                            te.matmul(pW[:, pr * 1024:pr * 1024 + 512],
                                      KT[0:64, kb], QT[0:64, cl],
                                      start=True, stop=True)
                            inc("pe", te.matmul(
                                pW[:, pr * 1024 + 512:pr * 1024 + 1024],
                                KT[64:128, kb], QT[64:128, cl],
                                start=True, stop=True))
                        elif e[0] == "P":
                            j = e[1]
                            qs, ki = sched[j]
                            sb_ = j % 2
                            W(te, "se", se_exp[j])
                            W(te, "ve", ve_mask[j])
                            if ki == 0 and qs > 0:
                                W(te, "se", se_ocp[qs - 1])
                            pt0 = PT[:, sb_ * 1024: sb_ * 1024 + 512]
                            pt1 = PT[:, sb_ * 1024 + 512: sb_ * 1024 + 1024]
                            te.matmul(pC[0:65, 0:512],
                                      Vt[:, ki * NH * 65 + h0 * 65: ki * NH * 65 + h0 * 65 + 65],
                                      pt0, start=(ki == 0), stop=(ki == 4 * qs + 3))
                            inc("pe", te.matmul(
                                pD[0:65, 0:512],
                                Vt[:, ki * NH * 65 + h1 * 65: ki * NH * 65 + h1 * 65 + 65],
                                pt1, start=(ki == 0), stop=(ki == 4 * qs + 3)))
                        elif e[0] == "R":
                            qs = e[1]
                            W(te, "se", se_lcp[qs])
                            lw = rwait.get(qs)
                            if lw is not None:
                                W(te, "se", se_qc[lw[1]] if lw[0] == "J"
                                  else se_rbt[lw[1]])
                            pbR = pb4[qs % 2]
                            te.matmul(pbR[0:64, 0:512], onesrow[0:1, 0:64],
                                      RRt[0:1, 0:512], start=True, stop=True)
                            inc("pe", te.matmul(pbR[64:128, 0:512],
                                                onesrow[0:1, 0:64],
                                                RRt[0:1, 512:1024],
                                                start=True, stop=True))
                        else:
                            k = e[1]
                            ti, st = k // 4, k % 4
                            sn, sv = gq_load[nhp][ti]
                            te.wait_ge(sem[sn], sv)
                            lw = jwait.get(k)
                            if lw is not None:
                                W(te, "se", se_qc[lw[1]] if lw[0] == "J"
                                  else se_rbt[lw[1]])
                            slot = slot_q[nhp] if ti == 0 else slot_k[nhp]
                            bank = pb4[k % 2]
                            for dc in range(NCH):
                                mm = te.matmul(
                                    bank[0:128, 0:512],
                                    WS[:, slot * 4096 + dc * 128: slot * 4096 + dc * 128 + 128],
                                    XN[:, dc * SEQ + st * 512: dc * SEQ + (st + 1) * 512],
                                    start=(dc == 0), stop=(dc == NCH - 1))
                            inc("pe", mm)

                @b.scalar
                def _(sc):
                    if do_scores:
                        sc.copy(JNK[:, 0:64], JNK[:, 64:128])
                        sc.activation(BH[:, 30:31], BH[:, 28 + cpar:29 + cpar],
                                      AF.Sqrt, scale=1.0 / 64.0)
                        sc.copy(JNK[:, 64:128], JNK[:, 0:64])
                        inc("se", sc.activation(BH[:, 48 + cpar:49 + cpar],
                                                BH[:, 30:31], AF.Identity,
                                                scale=-1.0))
                    nexp = 0
                    for idx, (qs, ki) in enumerate(sched):
                        sb_ = idx % 2
                        W(sc, "pe", pe_scores[idx])
                        inc("se", sc.activation(
                            PT[:, sb_ * 1024:(sb_ + 1) * 1024],
                            pW[:, sb_ * 1024:(sb_ + 1) * 1024],
                            AF.Exp, bias=BH[:, 48 + cpar:49 + cpar],
                            scale=0.125))
                        for j in sjobs[idx]:
                            if j[0] == "qc":
                                k = j[1]
                                st = k % 4
                                W(sc, "pe", pe_strip[k])
                                if k >= 4:
                                    W(sc, "ve", ve_ropeq)
                                inc("se", sc.copy(
                                    QC[:, st * 512:(st + 1) * 512],
                                    pb4[k % 2][0:128, 0:512]))
                            elif j[0] == "lcp":
                                W(sc, "pe", pe_pv[idx])
                                if qs >= 1:
                                    W(sc, "pe", pe_rb[qs - 1])
                                sc.copy(RRt[0:1, 0:512], pC[64:65, 0:512])
                                inc("se", sc.copy(RRt[0:1, 512:1024],
                                                  pD[64:65, 0:512]))
                            elif j[0] == "ocp":
                                osl = OSB[:, (qs % 2) * 512:(qs % 2 + 1) * 512]
                                if osb_free[qs % 2] > 0:
                                    sc.wait_ge(
                                        sem["oA" if qs % 2 == 0 else "oB"],
                                        osb_free[qs % 2])
                                sc.copy(osl[0:64, :], pC[0:64, 0:512])
                                inc("se", sc.copy(osl[64:128, :],
                                                  pD[0:64, 0:512]))
                            else:
                                W(sc, "pe", pe_rb[qs])
                                if qs >= 1:
                                    W(sc, "ve", ve_nrm[qs - 1])
                                pbR = pb4[qs % 2]
                                sc.copy(RBt[0:64, :], pbR[0:64, 0:512])
                                inc("se", sc.copy(RBt[64:128, :],
                                                  pbR[64:128, 0:512]))
                    if do_proj and not do_scores:
                        for k in range(8):
                            ti, st = k // 4, k % 4
                            W(sc, "pe", pe_strip[k])
                            if k >= 4:
                                W(sc, "ve", ve_ropeq)
                            inc("se", sc.copy(
                                QC[:, st * 512:(st + 1) * 512],
                                pb4[k % 2][0:128, 0:512]))

                def _emit_vjob(v, job):
                    ds = slice(par * SEQ, par * SEQ + SEQ)
                    if job[0] == "cl":
                        qs_, i = job[1], job[2]
                        c0 = slice(i * 128, (i + 1) * 128)
                        ob_ = slice((qs_ % 2) * 512 + i * 128,
                                    (qs_ % 2) * 512 + (i + 1) * 128)
                        if i == 0:
                            W(v, "se", se_rbt[qs_])
                        v.reciprocal(RBt[:, c0], RBt[:, c0])
                        r = v.tensor_mul(OSB[:, ob_], OSB[:, ob_], RBt[:, c0])
                        if i == 3:
                            inc("ve", r)
                    elif job[0] == "sq":
                        k = job[1]
                        st = k % 4
                        W(v, "se", se_qc[k])
                        cl = slice(st * 512, (st + 1) * 512)
                        qb = (par if k < 4 else 1 - par) * SEQ
                        dcl = slice(qb + st * 512, qb + (st + 1) * 512)
                        inc("ve", v.tensor_mul(QSQ[:, dcl], QC[:, cl],
                                               QC[:, cl]))
                    else:
                        v.tensor_copy(QF[0:32, :], QC[32:64, :])
                        v.tensor_copy(QF[32:64, :], QC[0:32, :])
                        v.tensor_copy(QF[64:96, :], QC[96:128, :])
                        v.tensor_copy(QF[96:128, :], QC[64:96, :])
                        v.tensor_mul(QT[:, ds], QC[:, :], C2[:, :])
                        v.tensor_mul(QF[:, :], QF[:, :], SFt[:, :])
                        inc("ve", v.tensor_add(QT[:, ds], QT[:, ds], QF[:, :]))

                @b.vector
                def _(v):
                    for idx in vposs:
                        if idx < nt_:
                            qs, ki = sched[idx]
                            sb_ = idx % 2
                            m = ki - 4 * qs
                            if m >= 0:
                                W(v, "se", se_exp[idx])
                                for half in range(2):
                                    tb = sb_ * 1024 + half * 512
                                    if m >= 1:
                                        v.memset(PT[:, tb:tb + m * 128], 0.0)
                                    r = v.tensor_mul(
                                        PT[:, tb + m * 128:tb + (m + 1) * 128],
                                        PT[:, tb + m * 128:tb + (m + 1) * 128],
                                        mask[:, :])
                                inc("ve", r)
                        for job in vjobs.get(idx, []):
                            _emit_vjob(v, job)
                    if not do_scores and do_proj:
                        for k in range(8):
                            _emit_vjob(v, ("sq", k))
                            if k == 3:
                                _emit_vjob(v, ("ropeq",))

                if do_scores:
                    @b.sync
                    def _(sy):
                        for qs in range(NQS):
                            W(sy, "ve", ve_nrm[qs])
                            osl = OSB[:, (qs % 2) * 512:(qs % 2 + 1) * 512]
                            osb_free[qs % 2] = inc(
                                "oA" if qs % 2 == 0 else "oB",
                                sy.dma_start(
                                    odram[hp, :, qs * 512:(qs + 1) * 512],
                                    osl), 16)
                        W(sy, "oA")
                        W(sy, "oB")

            if do_proj:
                emit_bound_block(nhp, par)
            if stop_after == ("ahp%d" % hp):
                _finish()
                actx.close()
                _check_waits()
                return nc

        actx.close()

        woctx = ExitStack()
        WOT = woctx.enter_context(nc.sbuf_tensor("WOT", [128, 8192], BF16))
        OL = woctx.enter_context(nc.sbuf_tensor("OL", [128, 2 * 4096], BF16))
        with blk() as b:
            g0 = cnt["gq"]

            @b.gpsimd
            def _(gp):
                for oc in range(8):
                    inc("gD", gp.dma_start(WOT[:, oc * 1024:(oc + 1) * 1024],
                                           wo_img[oc]), 16)
            gd_wot = cnt["gD"]
            pe0 = cnt["pe"]
            ve0 = cnt["ve"]
            ol_cnt = {}

            @b.sync
            def _(sy):
                for ns in range(4):
                    if ns >= 2:
                        W(sy, "pe", pe0 + (ns - 1) * 8)
                    src = odram[:, :, ns * 512:(ns + 1) * 512] \
                        .rearrange("h p n -> p h n")
                    sn = _hslot[ns % 2]
                    r = sy.dma_start(
                        OL[:, (ns % 2) * 4096:(ns % 2 + 1) * 4096]
                        .rearrange("p (h n) -> p h n", h=8), src)
                    ol_cnt[ns] = (sn, inc(sn, r, 16))

            @b.tensor
            def _(te):
                W(te, "gD", gd_wot)
                for ns in range(4):
                    te.wait_ge(sem[ol_cnt[ns][0]], ol_cnt[ns][1])
                    ob = (ns % 2) * 4096
                    for oc in range(8):
                        bank = pb4[oc % 4]
                        if ns * 8 + oc >= 4:
                            W(te, "ve", ve0 + ns * 8 + oc - 3)
                        for dc in range(NCH):
                            mm = te.matmul(
                                bank[0:128, 0:512],
                                WOT[:, oc * 1024 + dc * 128: oc * 1024 + dc * 128 + 128],
                                OL[:, ob + dc * 512: ob + (dc + 1) * 512],
                                start=(dc == 0), stop=(dc == NCH - 1))
                        inc("pe", mm)

            @b.vector
            def _(v):
                for ns in range(4):
                    for oc in range(8):
                        W(v, "pe", pe0 + ns * 8 + oc + 1)
                        bank = pb4[oc % 4]
                        dstc = slice(oc * SEQ + ns * 512, oc * SEQ + (ns + 1) * 512)
                        inc("ve", v.tensor_add(S[:, dstc], S[:, dstc],
                                               bank[0:128, 0:512]))
        woctx.close()
        if snaps:
            with blk() as b:
                @b.sync
                def _(sy):
                    W(sy, "ve")
                    inc("hq", sy.dma_start(snaps["ss1"], S[:, :]), 16)
        if "attn" in dbg_t:
            with blk() as b:
                @b.sync
                def _(sy):
                    W(sy, "ve")
                    inc("hq", sy.dma_start(dbg_t["attn"], S[:, :]), 16)
                    W(sy, "hq")

        if stop_after == "attn":
            _finish()
            return nc

        fctx = ExitStack()
        HA = fctx.enter_context(nc.sbuf_tensor("HA", [128, HCH * 512], BF16))
        HB = fctx.enter_context(nc.sbuf_tensor("HB", [128, HCH * 512], BF16))
        WS3 = fctx.enter_context(nc.sbuf_tensor("WS3", [128, 4096], BF16))

        def wsap(slot, lo, hi):
            if slot < 2:
                return WS[:, slot * 4096 + lo: slot * 4096 + hi]
            return WS3[:, lo:hi]
        for nt in range(4):
            ntc = slice(nt * 512, (nt + 1) * 512)
            with blk() as b:
                pe0 = cnt["pe"]
                se0 = cnt["se"]
                ve0 = cnt["ve"]
                hq0 = cnt["hq"]
                peA1 = {o: pe0 + 2 * o + 1 for o in range(32)}
                peA2 = {o: pe0 + 2 * o + 2 for o in range(32)}
                seA = {o: se0 + o + 1 for o in range(32)}
                veA = {o: ve0 + o + 1 for o in range(32)}
                peH1 = {o: pe0 + 64 + o + 1 for o in range(32)}
                seH1 = {o: se0 + 32 + o + 1 for o in range(32)}
                peH2 = {o: pe0 + 96 + o + 1 for o in range(32)}
                seH2 = {o: se0 + 64 + o + 1 for o in range(32)}
                peW3 = {o: pe0 + 128 + o + 1 for o in range(8)}
                veW3 = {o: ve0 + 32 + o + 1 for o in range(8)}
                hqA = {}
                hqH1 = {}
                hqH2 = {}
                hqW3 = {}
                _scnt = dict(cnt)

                def _ld2(o, tab):
                    slot = o % 3
                    sn = _hslot[slot]
                    _scnt[sn] += 32
                    tab[o] = (sn, _scnt[sn])
                for o in range(32):
                    _ld2(o, hqA)
                for o in range(32):
                    _ld2(o, hqH1)
                for o in range(32):
                    _ld2(o, hqH2)

                def _ld1(o, tab):
                    slot = o % 3
                    sn = _hslot[slot]
                    _scnt[sn] += 16
                    tab[o] = (sn, _scnt[sn])
                hqA = {}
                hqH1 = {}
                hqH2 = {}
                hqW3 = {}
                _scnt = dict(cnt)
                for o in range(32):
                    sn = _hslot[o % 3]
                    _scnt[sn] += 32
                    hqA[o] = (sn, _scnt[sn])
                for o in range(32):
                    sn = _hslot[o % 3]
                    _scnt[sn] += 16
                    hqH1[o] = (sn, _scnt[sn])
                for o in range(32):
                    sn = _hslot[o % 3]
                    _scnt[sn] += 16
                    hqH2[o] = (sn, _scnt[sn])
                for o in range(8):
                    sn = _hslot[o % 3]
                    _scnt[sn] += 16
                    hqW3[o] = (sn, _scnt[sn])

                @b.sync
                def _(sy):
                    for o in range(32):
                        slot = o % 3
                        sn = _hslot[slot]
                        if ws_free[slot] > 0:
                            sy.wait_ge(sem["pe"], ws_free[slot])
                        inc(sn, sy.dma_start(
                            wsap(slot, 0, 1024), w1_img[o]), 16)
                        inc(sn, sy.dma_start(
                            wsap(slot, 1024, 2048),
                            w2_img[o]), 16)
                        ws_free[slot] = peA2[o]
                    for img, tab in [(h1_img, peH1), (h2_img, peH2)]:
                        for o in range(32):
                            slot = o % 3
                            sn = _hslot[slot]
                            sy.wait_ge(sem["pe"], ws_free[slot])
                            inc(sn, sy.dma_start(
                                wsap(slot, 0, 4096), img[o]), 16)
                            ws_free[slot] = tab[o]
                    for o in range(8):
                        slot = o % 3
                        sn = _hslot[slot]
                        sy.wait_ge(sem["pe"], ws_free[slot])
                        inc(sn, sy.dma_start(
                            wsap(slot, 0, 4096), w3_img[o]), 16)
                        ws_free[slot] = peW3[o]

                @b.tensor
                def _(te):
                    for o in range(32):
                        slot = o % 3
                        te.wait_ge(sem[hqA[o][0]], hqA[o][1])
                        if o >= 2:
                            W(te, "ve", veA[o - 2])
                        bA = pb4[2 * (o % 2)]
                        bB = pb4[2 * (o % 2) + 1]
                        for dc in range(NCH):
                            mm = te.matmul(
                                bA[0:128, 0:512],
                                wsap(slot, dc * 128, dc * 128 + 128),
                                XN[:, dc * SEQ + nt * 512: dc * SEQ + (nt + 1) * 512],
                                start=(dc == 0), stop=(dc == NCH - 1))
                        inc("pe", mm)
                        for dc in range(NCH):
                            mm = te.matmul(
                                bB[0:128, 0:512],
                                wsap(slot, 1024 + dc * 128, 1024 + dc * 128 + 128),
                                XN[:, dc * SEQ + nt * 512: dc * SEQ + (nt + 1) * 512],
                                start=(dc == 0), stop=(dc == NCH - 1))
                        inc("pe", mm)
                    for o in range(32):
                        slot = o % 3
                        te.wait_ge(sem[hqH1[o][0]], hqH1[o][1])
                        if o < 4:
                            W(te, "ve", veA[31])
                        bank = pb4[o % 4]
                        if o >= 4:
                            W(te, "se", seH1[o - 4])
                        for kc in range(HCH):
                            mm = te.matmul(
                                bank[0:128, 0:512],
                                wsap(slot, kc * 128, kc * 128 + 128),
                                HA[:, kc * 512:(kc + 1) * 512],
                                start=(kc == 0), stop=(kc == HCH - 1))
                        inc("pe", mm)
                    for o in range(32):
                        slot = o % 3
                        te.wait_ge(sem[hqH2[o][0]], hqH2[o][1])
                        if o < 4:
                            W(te, "se", seH1[31])
                        bank = pb4[o % 4]
                        if o >= 4:
                            W(te, "se", seH2[o - 4])
                        for kc in range(HCH):
                            mm = te.matmul(
                                bank[0:128, 0:512],
                                wsap(slot, kc * 128, kc * 128 + 128),
                                HB[:, kc * 512:(kc + 1) * 512],
                                start=(kc == 0), stop=(kc == HCH - 1))
                        inc("pe", mm)
                    for o in range(8):
                        slot = o % 3
                        te.wait_ge(sem[hqW3[o][0]], hqW3[o][1])
                        if o < 4:
                            W(te, "se", seH2[31])
                        bank = pb4[o % 4]
                        if o >= 4:
                            W(te, "ve", veW3[o - 4])
                        for kc in range(HCH):
                            mm = te.matmul(
                                bank[0:128, 0:512],
                                wsap(slot, kc * 128, kc * 128 + 128),
                                HA[:, kc * 512:(kc + 1) * 512],
                                start=(kc == 0), stop=(kc == HCH - 1))
                        inc("pe", mm)

                @b.scalar
                def _(sc):
                    for o in range(32):
                        W(sc, "pe", peA1[o])
                        inc("se", sc.activation(HA[:, o * 512:(o + 1) * 512],
                                                pb4[2 * (o % 2)][0:128, 0:512],
                                                AF.Silu, scale=s1[:, o:o + 1]))
                    for o in range(32):
                        W(sc, "pe", peH1[o])
                        inc("se", sc.activation(HB[:, o * 512:(o + 1) * 512],
                                                pb4[o % 4][0:128, 0:512],
                                                AF.Gelu, scale=sh1[:, o:o + 1]))
                    for o in range(32):
                        W(sc, "pe", peH2[o])
                        inc("se", sc.activation(HA[:, o * 512:(o + 1) * 512],
                                                pb4[o % 4][0:128, 0:512],
                                                AF.Gelu, scale=sh2[:, o:o + 1]))

                @b.vector
                def _(v):
                    for o in range(32):
                        W(v, "pe", peA2[o])
                        W(v, "se", seA[o])
                        hsl = HA[:, o * 512:(o + 1) * 512]
                        inc("ve", v.scalar_tensor_tensor(
                            hsl, pb4[2 * (o % 2) + 1][0:128, 0:512],
                            s2[:, o:o + 1], hsl, ALU.mult, ALU.mult))
                    for o in range(8):
                        W(v, "pe", peW3[o])
                        dstc = slice(o * SEQ + nt * 512, o * SEQ + (nt + 1) * 512)
                        inc("ve", v.scalar_tensor_tensor(
                            S[:, dstc], pb4[o % 4][0:128, 0:512],
                            s3[:, o:o + 1], S[:, dstc], ALU.mult, ALU.add))
        fctx.close()
        if snaps:
            with blk() as b:
                @b.sync
                def _(sy):
                    W(sy, "ve")
                    inc("hq", sy.dma_start(snaps["ss2"], S[:, :]), 16)
        if "x2" in dbg_t:
            with blk() as b:
                @b.sync
                def _(sy):
                    W(sy, "ve")
                    inc("hq", sy.dma_start(dbg_t["x2"], S[:, :]), 16)
                    W(sy, "hq")

        if stop_after == "ffn":
            _finish()
            return nc

        xrctx = ExitStack()
        XR = xrctx.enter_context(nc.sbuf_tensor("XR", [128, NCH * SEQ], BF16))
        s7ctx = ExitStack()
        BCA = s7ctx.enter_context(nc.sbuf_tensor("BCA7", [128, SEQ], F32))
        BCB = s7ctx.enter_context(nc.sbuf_tensor("BCB7", [128, SEQ], F32))
        s7r = ExitStack()
        RW = s7r.enter_context(nc.sbuf_tensor("RW7", [1, 8192], F32))
        A_ = RW[0:1, 0:2048]
        B_ = RW[0:1, 2048:4096]
        C_ = RW[0:1, 4096:6144]
        V_ = RW[0:1, 6144:8192]
        D_ = BCA[0:1, 0:2048]
        E_ = BCB[0:1, 0:2048]

        ln_stats(S, scratch=WVT)
        with blk() as b:
            @b.vector
            def _(v):
                v.tensor_scalar_mul(A_, pW[0:1, :], 1.0 / D)
                v.tensor_scalar_mul(B_, pW[32:33, :], 1.0 / D)
                vflush(v)
                v.tensor_mul(C_, A_, A_)
                vflush(v)
                inc("ve", v.tensor_sub(V_, B_, C_))
        with blk() as b:
            @b.scalar
            def _(sc):
                sc.activation(C_, V_, AF.Sqrt, scale=1.0 + LN_EPS,
                              bias=EPSC[0:1, 2:3])
                inc("se", sc.activation(D_, V_, AF.Sqrt, scale=float(D)))
        with blk() as b:
            @b.vector
            def _(v):
                v.reciprocal(C_, C_)
                v.tensor_scalar_max(D_, D_, 1e-18)
                vflush(v)
                v.scalar_tensor_tensor(B_, A_, -1.0, C_,
                                       ALU.mult, ALU.mult)
                v.reciprocal(D_, D_)
                vflush(v)
                v.tensor_scalar_mul(D_, D_, 32.0)
                vflush(v)
                inc("ve", v.scalar_tensor_tensor(E_, A_, -1.0, D_,
                                                 ALU.mult, ALU.mult))
        bcast_rows([(D_, BCA), (E_, BCB)])
        with blk() as b:
            @b.vector
            def _(v):
                for c in range(NCH):
                    cs = slice(c * SEQ, (c + 1) * SEQ)
                    v.tensor_mul(pW[:, :], S[:, cs], BCA[:, :])
                    inc("ve", v.tensor_add(XR[:, cs], pW[:, :], BCB[:, :]))
            if "xr" in dbg_t:
                @b.sync
                def _(sy):
                    W(sy, "ve")
                    inc("hq", sy.dma_start(dbg_t["xr"], XR[:, :]), 16)
                    W(sy, "hq")
        bcast_rows([(C_, BCA), (B_, BCB)])
        s7r.close()
        xln_in_mixer = not (stop_after == "l1n" or "xln" in dbg_t)
        if not xln_in_mixer:
            with blk() as b:
                @b.vector
                def _(v):
                    for c in range(NCH):
                        cs = slice(c * SEQ, (c + 1) * SEQ)
                        v.tensor_mul(pW[:, :], S[:, cs], BCA[:, :])
                        v.tensor_add(XN[:, cs], pW[:, :], BCB[:, :])
                        inc("ve", v.tensor_scalar(
                            XN[:, cs], XN[:, cs], l1g[:, c:c + 1],
                            l1b[:, c:c + 1], ALU.mult, ALU.add))
                if "xln" in dbg_t:
                    @b.sync
                    def _(sy):
                        W(sy, "ve")
                        inc("hq", sy.dma_start(dbg_t["xln"], XN[:, :]), 16)
                        W(sy, "hq")

        if stop_after == "l1n":
            _finish()
            s7ctx.close()
            xrctx.close()
            return nc

        mctx = ExitStack()
        KVT = mctx.enter_context(nc.sbuf_tensor("KVT", [128, SEQ], F32))
        AT = mctx.enter_context(nc.sbuf_tensor("AT", [128, SEQ], F32))
        KVS = mctx.enter_context(nc.sbuf_tensor("KVS", [128, SEQ], F32))
        MIXT = mctx.enter_context(nc.sbuf_tensor("MIXT", [128, SEQ], F32))
        with blk() as b:
            mx_prev = None
            for c in range(NCH):
                g0 = cnt["gq"]
                pe0 = cnt["pe"]
                se0 = cnt["se"]
                ve0 = cnt["ve"]
                prevc = mx_prev

                mx_ld = {}

                @b.gpsimd
                def _(gp):
                    for j, oc in enumerate([8 + c, 16 + c, c]):
                        slot = j % 2
                        if ws_free[slot] > 0:
                            gp.wait_ge(sem["pe"], ws_free[slot])
                        if j == 2:
                            gp.wait_ge(sem["pe"], pe0 + 4)
                        mx_ld[j] = ws_load(gp, "gq", qkva_img[oc], 1024, slot)

                @b.vector
                def _(v):
                    for j in range(3):
                        slot = j % 2
                        v.wait_ge(sem[mx_ld[j][0]], mx_ld[j][1])
                        for dc in range(NCH):
                            r = v.tensor_scalar_mul(
                                WS[:, slot * 4096 + dc * 128: slot * 4096 + (dc + 1) * 128],
                                WS[:, slot * 4096 + dc * 128: slot * 4096 + (dc + 1) * 128],
                                l1gm[:, dc:dc + 1])
                        inc("ve", r)

                @b.tensor
                def _(te):
                    for j, dst in enumerate(["kv", "a", "q"]):
                        slot = j % 2
                        W(te, "ve", ve0 + j + 1)
                        if j == 0 and prevc is not None:
                            W(te, "se", prevc[0])
                        if j == 2 and prevc is not None:
                            W(te, "ve", prevc[2])
                        for st in range(4):
                            bank = pb4[st]
                            if j >= 1:
                                W(te, "se", se0 + j * 4 - 4 + st + 1)
                            for dc in range(NCH):
                                out = bank[0:128, 0:512] if j < 2 else \
                                    pW[:, st * 512:(st + 1) * 512]
                                mm = te.matmul(
                                    out,
                                    WS[:, slot * 4096 + dc * 128: slot * 4096 + dc * 128 + 128],
                                    XR[:, dc * SEQ + st * 512: dc * SEQ + (st + 1) * 512],
                                    start=(dc == 0), stop=(dc == NCH - 1))
                            inc("pe", mm)
                ws_free[0] = cnt["pe"]
                ws_free[1] = cnt["pe"]

                @b.scalar
                def _(sc):
                    for st in range(4):
                        W(sc, "pe", pe0 + st + 1)
                        if st == 0 and prevc is not None:
                            W(sc, "ve", prevc[1])
                        inc("se", sc.copy(KVT[:, st * 512:(st + 1) * 512],
                                          pb4[st][0:128, 0:512]))
                    for st in range(4):
                        W(sc, "pe", pe0 + 4 + st + 1)
                        inc("se", sc.activation(AT[:, st * 512:(st + 1) * 512],
                                                pb4[st][0:128, 0:512], AF.Sigmoid))

                @b.vector
                def _(v):
                    cs = slice(c * SEQ, (c + 1) * SEQ)
                    if xln_in_mixer:
                        v.tensor_mul(MIXT[:, :], S[:, cs], BCA[:, :])
                        v.tensor_add(XN[:, cs], MIXT[:, :], BCB[:, :])
                        v.tensor_scalar(XN[:, cs], XN[:, cs],
                                        l1g[:, c:c + 1], l1b[:, c:c + 1],
                                        ALU.mult, ALU.add)
                    W(v, "se", se0 + 8)
                    inc("ve", v.tensor_tensor_scan(KVS[:, :], AT[:, :], KVT[:, :],
                                                   0.0, ALU.mult, ALU.add))
                    W(v, "pe", pe0 + 12)
                    v.tensor_mul(MIXT[:, :], pW[:, :], KVS[:, :])
                    inc("ve", v.tensor_add(S[:, cs], S[:, cs], MIXT[:, :]))
                mx_prev = (se0 + 8, ve0 + 4, ve0 + 5)
        if snaps:
            with blk() as b:
                @b.sync
                def _(sy):
                    W(sy, "ve")
                    inc("hq", sy.dma_start(snaps["ss3"], S[:, :]), 16)
        mctx.close()
        s7ctx.close()
        xrctx.close()
        if "mixS" in dbg_t:
            with blk() as b:
                @b.sync
                def _(sy):
                    W(sy, "ve")
                    inc("hq", sy.dma_start(dbg_t["mixS"], S[:, :]), 16)
                    W(sy, "hq")

        if stop_after == "mix":
            _finish()
            return nc

        m1ctx = ExitStack()
        DTM = m1ctx.enter_context(nc.sbuf_tensor("DTM", [128, 16 * 256], BF16))
        CBT = m1ctx.enter_context(nc.sbuf_tensor("CBT", [128, 2 * SEQ], BF16))
        inner9 = ExitStack()
        MUB = inner9.enter_context(nc.sbuf_tensor("MUB", [128, 2048], BF16))
        R3B = inner9.enter_context(nc.sbuf_tensor("R3B", [128, SEQ], F32))
        LSB = inner9.enter_context(nc.sbuf_tensor("LSB", [128, 2 * SEQ], BF16))
        SQS = inner9.enter_context(nc.sbuf_tensor("SQS", [128, 2048], BF16))
        SQS2 = inner9.enter_context(nc.sbuf_tensor("SQS2", [128, 2048], BF16))
        RW9 = inner9.enter_context(nc.sbuf_tensor("RW9", [1, 2304], F32))
        NMX = inner9.enter_context(nc.sbuf_tensor("NMX", [128, 32], F32))
        DSP = inner9.enter_context(nc.sbuf_tensor("DSP", [128, SEQ], BF16))
        CTM = inner9.enter_context(nc.sbuf_tensor("CTM", [128, 16 * 256], BF16))
        LTM = inner9.enter_context(nc.sbuf_tensor("LTM", [128, 16 * 256], BF16))
        CM0 = inner9.enter_context(nc.sbuf_tensor("CM0", [128, 16], F32))
        CM1 = inner9.enter_context(nc.sbuf_tensor("CM1", [128, 16], F32))
        CS0 = inner9.enter_context(nc.sbuf_tensor("CS0", [128, 16], F32))
        CS1 = inner9.enter_context(nc.sbuf_tensor("CS1", [128, 16], F32))
        r3row = RW9[0:1, 0:2048]
        murow = RW9[0:1, 2048:2304]

        with blk() as b:
            g0 = cnt["gq"]

            @b.gpsimd
            def _(gp):
                inc("gD", gp.dma_start(MUB[:, :], mu_img), 16)
            gd_mub = cnt["gD"]
            se0 = cnt["se"]
            pe0 = cnt["pe"]

            @b.scalar
            def _(sc):
                W(sc, "gD", gd_mub)
                inc("se", sc.activation(SQS[:, :], MUB[:, :], AF.Square))

            @b.tensor
            def _(te):
                W(te, "se", se0 + 1)
                for dc in range(NCH):
                    mm = te.matmul(pA[0:1, 0:256], onescol[:, :],
                                   SQS[:, dc * 256:(dc + 1) * 256],
                                   start=(dc == 0), stop=(dc == NCH - 1))
                inc("pe", mm)

            @b.scalar
            def _(sc):
                W(sc, "pe", pe0 + 1)
                inc("se", sc.activation(murow, pA[0:1, 0:256], AF.Sqrt))
        with blk() as b:
            @b.vector
            def _(v):
                v.tensor_scalar_max(murow, murow, 1e-12)
                vflush(v)
                inc("ve", v.reciprocal(murow, murow))
        with blk() as b:
            pe1 = cnt["pe"]

            @b.tensor
            def _(te):
                inc("pe", te.matmul(pB[0:128, 0:256], onesrow[0:1, :], murow,
                                    start=True, stop=True))

            @b.vector
            def _(v):
                W(v, "pe", pe1 + 1)
                for dc in range(NCH):
                    r = v.tensor_mul(MUB[:, dc * 256:(dc + 1) * 256],
                                     MUB[:, dc * 256:(dc + 1) * 256],
                                     pB[0:128, 0:256])
                inc("ve", r)

        with blk() as b:
            se0 = cnt["se"]
            ve0 = cnt["ve"]
            pe0 = cnt["pe"]
            sq_done = {}
            scn, ven = 0, 0
            for c in range(NCH):
                if c % 2 == 0:
                    scn += 1
                    sq_done[c] = ("se", se0 + scn)
                else:
                    ven += 1
                    sq_done[c] = ("ve", ve0 + ven)

            @b.scalar
            def _(sc):
                for c in range(0, NCH, 2):
                    if c >= 2:
                        W(sc, "pe", pe0 + c - 1)
                    inc("se", sc.activation(SQS[:, :],
                                            XN[:, c * SEQ:(c + 1) * SEQ],
                                            AF.Square))

            @b.vector
            def _(v):
                for c in range(1, NCH, 2):
                    if c >= 2:
                        W(v, "pe", pe0 + c - 1)
                    inc("ve", v.tensor_mul(SQS2[:, :],
                                           XN[:, c * SEQ:(c + 1) * SEQ],
                                           XN[:, c * SEQ:(c + 1) * SEQ]))

            @b.tensor
            def _(te):
                for c in range(NCH):
                    en, val = sq_done[c]
                    W(te, en, val)
                    buf = SQS if c % 2 == 0 else SQS2
                    for st in range(4):
                        mm = te.matmul(pW[0:1, st * 512:(st + 1) * 512],
                                       onescol[:, :],
                                       buf[:, st * 512:(st + 1) * 512],
                                       start=(c == 0), stop=(c == NCH - 1))
                    inc("pe", mm)

            @b.scalar
            def _(sc):
                W(sc, "pe", pe0 + 8)
                inc("se", sc.activation(r3row, pW[0:1, :], AF.Sqrt))
        with blk() as b:
            @b.vector
            def _(v):
                v.tensor_scalar_max(r3row, r3row, 1e-12)
                vflush(v)
                v.reciprocal(r3row, r3row)
                vflush(v)
                inc("ve", v.tensor_scalar_mul(r3row, r3row,
                                              l1scale[0:1, 0:1]))
        bcast_rows([(r3row, R3B)])

        for esc in range(2):
            with blk() as b:
                pe0 = cnt["pe"]

                @b.tensor
                def _(te):
                    for st in range(4):
                        for dc in range(NCH):
                            mm = te.matmul(
                                pW[:, st * 512:(st + 1) * 512],
                                MUB[:, dc * 256 + esc * 128: dc * 256 + esc * 128 + 128],
                                XN[:, dc * SEQ + st * 512: dc * SEQ + (st + 1) * 512],
                                start=(dc == 0), stop=(dc == NCH - 1))
                    inc("pe", mm)

                @b.vector
                def _(v):
                    W(v, "pe", pe0 + 1)
                    lsl = LSB[:, esc * SEQ:(esc + 1) * SEQ]
                    v.tensor_mul(lsl, pW[:, :], R3B[:, :])
                    inc("ve", v.reduce_max(NMX[:, 0:1], lsl, axis=AX.X,
                                           negate=True))
            with blk() as b:
                @b.scalar
                def _(sc):
                    lsl = LSB[:, esc * SEQ:(esc + 1) * SEQ]
                    inc("se", sc.activation(DSP[:, :], lsl, AF.Exp,
                                            bias=NMX[:, 0:1],
                                            accum_out=NMX[:, 8:9]))
            with blk() as b:
                @b.vector
                def _(v):
                    v.reciprocal(NMX[:, 16:17], NMX[:, 8:9])
                    vflush(v)
                    inc("ve", v.tensor_scalar_mul(DSP[:, :], DSP[:, :],
                                                  NMX[:, 16:17]))
            with blk() as b:
                pe1 = cnt["pe"]
                se1 = cnt["se"]

                @b.tensor
                def _(te):
                    for nc_ in range(16):
                        pt = (pA if nc_ % 2 == 0 else pB)[:, 0:64].bitcast(BF16)
                        if nc_ >= 2:
                            W(te, "se", se1 + nc_ - 1)
                        inc("pe", te.transpose(pt, DSP[:, nc_ * 128:(nc_ + 1) * 128],
                                               idb[:, :]))

                @b.scalar
                def _(sc):
                    for nc_ in range(16):
                        W(sc, "pe", pe1 + nc_ + 1)
                        pt = (pA if nc_ % 2 == 0 else pB)[:, 0:64].bitcast(BF16)
                        inc("se", sc.copy(
                            DTM[:, nc_ * 256 + esc * 128: nc_ * 256 + (esc + 1) * 128],
                            pt))
        if "disp" in dbg_t:
            with blk() as b:
                @b.sync
                def _(sy):
                    W(sy, "se")
                    inc("hq", sy.dma_start(dbg_t["disp"], DTM[:, :]), 16)
                    W(sy, "hq")
        if "lsb" in dbg_t:
            with blk() as b:
                @b.sync
                def _(sy):
                    W(sy, "se")
                    inc("hq", sy.dma_start(dbg_t["lsb"], LSB[:, :]), 16)
                    W(sy, "hq")

        with blk() as b:
            pe0 = cnt["pe"]
            se0 = cnt["se"]

            @b.tensor
            def _(te):
                for i in range(32):
                    nc_, esc = i // 2, i % 2
                    pt = (pA if i % 2 == 0 else pB)[:, 0:64].bitcast(BF16)
                    if i >= 2:
                        W(te, "se", se0 + i - 1)
                    inc("pe", te.transpose(
                        pt, LSB[:, esc * SEQ + nc_ * 128: esc * SEQ + (nc_ + 1) * 128],
                        idb[:, :]))

            @b.scalar
            def _(sc):
                for i in range(32):
                    nc_, esc = i // 2, i % 2
                    W(sc, "pe", pe0 + i + 1)
                    pt = (pA if i % 2 == 0 else pB)[:, 0:64].bitcast(BF16)
                    inc("se", sc.copy(
                        LTM[:, nc_ * 256 + esc * 128: nc_ * 256 + (esc + 1) * 128],
                        pt))
        with blk() as b:
            @b.vector
            def _(v):
                for nc_ in range(16):
                    v.reduce_max(CM0[:, nc_:nc_ + 1],
                                 LTM[:, nc_ * 256: nc_ * 256 + 128], axis=AX.X)
                    v.reduce_max(CM1[:, nc_:nc_ + 1],
                                 LTM[:, nc_ * 256 + 128: nc_ * 256 + 256], axis=AX.X)
                vflush(v)
                v.tensor_max(CM0[:, 0:16], CM0[:, 0:16], CM1[:, 0:16])
                vflush(v)
                inc("ve", v.tensor_scalar_mul(CM0[:, 0:16], CM0[:, 0:16], -1.0))
        with blk() as b:
            @b.scalar
            def _(sc):
                for nc_ in range(16):
                    sc.activation(CTM[:, nc_ * 256: nc_ * 256 + 128],
                                  LTM[:, nc_ * 256: nc_ * 256 + 128], AF.Exp,
                                  bias=CM0[:, nc_:nc_ + 1],
                                  accum_out=CS0[:, nc_:nc_ + 1])
                    inc("se", sc.activation(
                        CTM[:, nc_ * 256 + 128: nc_ * 256 + 256],
                        LTM[:, nc_ * 256 + 128: nc_ * 256 + 256], AF.Exp,
                        bias=CM0[:, nc_:nc_ + 1],
                        accum_out=CS1[:, nc_:nc_ + 1]))
        with blk() as b:
            @b.vector
            def _(v):
                v.tensor_add(CS0[:, 0:16], CS0[:, 0:16], CS1[:, 0:16])
                vflush(v)
                v.reciprocal(CS0[:, 0:16], CS0[:, 0:16])
                vflush(v)
                for nc_ in range(16):
                    r = v.tensor_scalar_mul(CTM[:, nc_ * 256:(nc_ + 1) * 256],
                                            CTM[:, nc_ * 256:(nc_ + 1) * 256],
                                            CS0[:, nc_:nc_ + 1])
                inc("ve", r)

        if dbg_t:
            with blk() as b:
                @b.sync
                def _(sy):
                    W(sy, "ve")
                    for nm, src in [("ctm", CTM[:, :]), ("ltm", LTM[:, :])]:
                        if nm in dbg_t:
                            inc("hq", sy.dma_start(dbg_t[nm], src), 16)
                    if "cm" in dbg_t:
                        inc("hq", sy.dma_start(dbg_t["cm"][:, 0:16], CM0[:, :]), 16)
                        inc("hq", sy.dma_start(dbg_t["cm"][:, 16:32], CM1[:, :]), 16)
                        inc("hq", sy.dma_start(dbg_t["cm"][:, 32:48], CS0[:, :]), 16)
                        inc("hq", sy.dma_start(dbg_t["cm"][:, 48:64], CS1[:, :]), 16)
                    W(sy, "hq")

        with blk() as b:
            pe0 = cnt["pe"]
            se0 = cnt["se"]

            @b.tensor
            def _(te):
                W(te, "ve")
                for i in range(32):
                    nc_, esc = i // 2, i % 2
                    pt = (pA if i % 2 == 0 else pB)[:, 0:64].bitcast(BF16)
                    if i >= 2:
                        W(te, "se", se0 + i - 1)
                    inc("pe", te.transpose(
                        pt, CTM[:, nc_ * 256 + esc * 128: nc_ * 256 + (esc + 1) * 128],
                        idb[:, :]))

            @b.scalar
            def _(sc):
                for i in range(32):
                    nc_, esc = i // 2, i % 2
                    W(sc, "pe", pe0 + i + 1)
                    pt = (pA if i % 2 == 0 else pB)[:, 0:64].bitcast(BF16)
                    inc("se", sc.copy(
                        CBT[:, esc * SEQ + nc_ * 128: esc * SEQ + (nc_ + 1) * 128],
                        pt))
        if "combt" in dbg_t:
            with blk() as b:
                @b.sync
                def _(sy):
                    W(sy, "se")
                    inc("hq", sy.dma_start(dbg_t["combt"], CBT[:, :]), 16)
                    W(sy, "hq")

        inner9.close()

        s9b = ExitStack()
        XTM = s9b.enter_context(nc.sbuf_tensor("XTM", [128, 16 * 1024], BF16))
        SIT = s9b.enter_context(nc.sbuf_tensor("SIT", [128, 8 * 256], BF16))
        with blk() as b:
            pe0 = cnt["pe"]
            se0 = cnt["se"]

            @b.tensor
            def _(te):
                for i in range(128):
                    dc, nc_ = i // 16, i % 16
                    pt = (pA if i % 2 == 0 else pB)[:, 0:64].bitcast(BF16)
                    if i >= 2:
                        W(te, "se", se0 + i - 1)
                    inc("pe", te.transpose(
                        pt, XN[:, dc * SEQ + nc_ * 128: dc * SEQ + (nc_ + 1) * 128],
                        idb[:, :]))

            @b.scalar
            def _(sc):
                for i in range(128):
                    dc, nc_ = i // 16, i % 16
                    W(sc, "pe", pe0 + i + 1)
                    pt = (pA if i % 2 == 0 else pB)[:, 0:64].bitcast(BF16)
                    inc("se", sc.copy(
                        XTM[:, nc_ * 1024 + dc * 128: nc_ * 1024 + (dc + 1) * 128],
                        pt))

        with blk() as b:
            pe0 = cnt["pe"]
            se0 = cnt["se"]

            @b.tensor
            def _(te):
                for dc in range(NCH):
                    bank = pb4[dc % 4]
                    if dc >= 4:
                        W(te, "se", se0 + dc - 3)
                    for nc_ in range(16):
                        mm = te.matmul(
                            bank[0:128, 0:256],
                            XTM[:, nc_ * 1024 + dc * 128: nc_ * 1024 + (dc + 1) * 128],
                            DTM[:, nc_ * 256:(nc_ + 1) * 256],
                            start=(nc_ == 0), stop=(nc_ == 15))
                    inc("pe", mm)

            @b.scalar
            def _(sc):
                for dc in range(NCH):
                    W(sc, "pe", pe0 + dc + 1)
                    inc("se", sc.copy(SIT[:, dc * 256:(dc + 1) * 256],
                                      pb4[dc % 4][0:128, 0:256]))
        if "sit" in dbg_t:
            with blk() as b:
                @b.sync
                def _(sy):
                    W(sy, "se")
                    inc("hq", sy.dma_start(dbg_t["sit"], SIT[:, :]), 16)
                    W(sy, "hq")

        if stop_after == "moe1":
            _finish()
            s9b.close()
            m1ctx.close()
            return nc

        s9c = ExitStack()
        HT = s9c.enter_context(nc.sbuf_tensor("HT", [128, 2 * 256], BF16))
        SOT = s9c.enter_context(nc.sbuf_tensor("SOT", [128, 8 * 256], BF16))
        STM = s9c.enter_context(nc.sbuf_tensor("STM", [128, 2 * 1024], BF16))
        with blk() as b:
            pe0 = cnt["pe"]
            se0 = cnt["se"]
            g0 = cnt["gq"]

            e_ld = {}

            @b.gpsimd
            def _(gp):
                for e in range(16):
                    slot = e % 2
                    e_ld[e] = ws_load(gp, "gq", ew1_img[e], 2048, slot)
                    ws_free[slot] = pe0 + (e + 1) * 2

            @b.tensor
            def _(te):
                for e in range(16):
                    slot = e % 2
                    te.wait_ge(sem[e_ld[e][0]], e_ld[e][1])
                    if e >= 1:
                        W(te, "se", se0 + (e - 1) * 2 + 2)
                    for ih in range(2):
                        for dc in range(NCH):
                            mm = te.matmul(
                                pC[0:128, ih * 16:(ih + 1) * 16],
                                WS[:, slot * 4096 + dc * 256 + ih * 128: slot * 4096 + dc * 256 + (ih + 1) * 128],
                                SIT[:, dc * 256 + e * 16: dc * 256 + (e + 1) * 16],
                                start=(dc == 0), stop=(dc == NCH - 1))
                        inc("pe", mm)

            @b.scalar
            def _(sc):
                for e in range(16):
                    W(sc, "pe", pe0 + e * 2 + 2)
                    for ih in range(2):
                        inc("se", sc.activation(
                            HT[:, ih * 256 + e * 16: ih * 256 + (e + 1) * 16],
                            pC[0:128, ih * 16:(ih + 1) * 16], AF.Gelu,
                            bias=b1t[:, e * 2 + ih: e * 2 + ih + 1]))

        if stop_after == "moe2a":
            _finish()
            s9c.close()
            s9b.close()
            m1ctx.close()
            return nc

        with blk() as b:
            pe0 = cnt["pe"]
            se0 = cnt["se"]
            g0 = cnt["gq"]

            w2_ld = {}

            @b.gpsimd
            def _(gp):
                for dc in range(8):
                    slot = dc % 2
                    w2_ld[dc] = ws_load(gp, "gq", ew2_img[dc], 4096, slot)
                    ws_free[slot] = pe0 + (dc + 1) * 16

            @b.tensor
            def _(te):
                for dc in range(8):
                    slot = dc % 2
                    te.wait_ge(sem[w2_ld[dc][0]], w2_ld[dc][1])
                    bank = pC if dc % 2 == 0 else pD
                    if dc >= 2:
                        W(te, "se", se0 + (dc - 2) * 16 + 16)
                    for e in range(16):
                        for hh in range(2):
                            mm = te.matmul(
                                bank[0:128, e * 16:(e + 1) * 16],
                                WS[:, slot * 4096 + (e * 2 + hh) * 128: slot * 4096 + (e * 2 + hh + 1) * 128],
                                HT[:, hh * 256 + e * 16: hh * 256 + (e + 1) * 16],
                                start=(hh == 0), stop=(hh == 1))
                        inc("pe", mm)

            @b.vector
            def _(v):
                for dc in range(8):
                    bank = pC if dc % 2 == 0 else pD
                    W(v, "pe", pe0 + (dc + 1) * 16)
                    for e in range(16):
                        inc("se", v.tensor_scalar_add(
                            SOT[:, dc * 256 + e * 16: dc * 256 + (e + 1) * 16],
                            bank[0:128, e * 16:(e + 1) * 16],
                            b2t[:, e * 8 + dc: e * 8 + dc + 1]))
        if "sot" in dbg_t:
            with blk() as b:
                @b.sync
                def _(sy):
                    W(sy, "se")
                    inc("hq", sy.dma_start(dbg_t["sot"], SOT[:, :]), 16)
                    W(sy, "hq")

        if stop_after == "moe2b":
            _finish()
            s9c.close()
            s9b.close()
            m1ctx.close()
            return nc

        with blk() as b:
            pe0 = cnt["pe"]
            se0 = cnt["se"]

            @b.tensor
            def _(te):
                W(te, "se")
                for i in range(16):
                    dc, esc = i // 2, i % 2
                    pt = (pA if i % 2 == 0 else pB)[:, 0:64].bitcast(BF16)
                    if i >= 2:
                        W(te, "se", se0 + i - 1)
                    inc("pe", te.transpose(
                        pt, SOT[:, dc * 256 + esc * 128: dc * 256 + (esc + 1) * 128],
                        idb[:, :]))

            @b.scalar
            def _(sc):
                for i in range(16):
                    dc, esc = i // 2, i % 2
                    W(sc, "pe", pe0 + i + 1)
                    pt = (pA if i % 2 == 0 else pB)[:, 0:64].bitcast(BF16)
                    inc("se", sc.copy(
                        STM[:, esc * 1024 + dc * 128: esc * 1024 + (dc + 1) * 128],
                        pt))

        if stop_after == "moe2c":
            _finish()
            s9c.close()
            s9b.close()
            m1ctx.close()
            return nc

        with blk() as b:
            pe0 = cnt["pe"]
            ve0 = cnt["ve"]

            @b.tensor
            def _(te):
                W(te, "se")
                i = 0
                for dc in range(8):
                    for st in range(4):
                        bank = pb4[i % 4]
                        if i >= 4:
                            W(te, "ve", ve0 + i - 3)
                        for esc in range(2):
                            mm = te.matmul(
                                bank[0:128, 0:512],
                                STM[:, esc * 1024 + dc * 128: esc * 1024 + (dc + 1) * 128],
                                CBT[:, esc * SEQ + st * 512: esc * SEQ + (st + 1) * 512],
                                start=(esc == 0), stop=(esc == 1))
                        inc("pe", mm)
                        i += 1

            @b.vector
            def _(v):
                i = 0
                for dc in range(8):
                    for st in range(4):
                        W(v, "pe", pe0 + i + 1)
                        dstc = slice(dc * SEQ + st * 512, dc * SEQ + (st + 1) * 512)
                        inc("ve", v.tensor_add(S[:, dstc], S[:, dstc],
                                               pb4[i % 4][0:128, 0:512]))
                        i += 1

            @b.sync
            def _(sy):
                for dc in range(8):
                    W(sy, "ve", ve0 + dc * 4 + 4)
                    inc("hq", sy.dma_start(
                        outT[:, dc * SEQ:(dc + 1) * SEQ],
                        S[:, dc * SEQ:(dc + 1) * SEQ]), 16)
                W(sy, "hq")
        s9c.close()
        s9b.close()
        m1ctx.close()
        _check_waits()
    return nc


def kernel(**inputs):
    g = _prep(inputs)
    nc = build()
    common = {k: v for k, v in g.items() if k != "x_img"}
    in_maps = []
    for b in range(8):
        m = dict(common)
        m["x_img"] = g["x_img"][b]
        in_maps.append(m)
    res = run_bass_kernel_spmd(nc, in_maps, core_ids=list(range(8)))
    outs = []
    for b in range(8):
        img = np.asarray(res.results[b]["outT"], dtype=np.float32)
        outs.append(img.reshape(128, NCH, SEQ).transpose(1, 0, 2)
                    .reshape(D, SEQ).T)
    return np.stack(outs, 0).astype(np.float32)
